# version 01
# baseline (speedup 1.0000x reference)
import contextlib
import numpy as np
import concourse.bass as bass
import concourse.mybir as mybir
from concourse.bass_utils import run_bass_kernel_spmd

F32 = mybir.dt.float32
BF16 = mybir.dt.bfloat16
I8 = mybir.dt.int8
AF = mybir.ActivationFunctionType
ALU = mybir.AluOpType
AX = mybir.AxisListType

NCORES = 8
BATCH = 16
SEQ = 2048
D = 2048
SPC = BATCH // NCORES
T = SPC * SEQ
KC = D // 128
NTT = T // 128
NTS = SEQ // 128
H = 32
HD = 64
M_ALIBI = 2.0 ** (-0.25)
INTER = 8192
MI = INTER // 128
EPS = 1e-5
SH = 32
SN = 256
KCONV = 4
CH = 64
NCHUNK = SEQ // CH
CONVD = D + 2 * SN
CCC = CONVD // 128
MASKNEG = -100000.0


def _blk_lhsT(w):
    din, dout = w.shape
    kc, mc = din // 128, dout // 128
    return np.ascontiguousarray(w.reshape(kc, 128, mc, 128).transpose(2, 1, 0, 3))


def _prep_shared(inp):
    import ml_dtypes
    bf = ml_dtypes.bfloat16
    f = np.ascontiguousarray
    W_attn = np.asarray(inp["L0_attn_a_W_attn"], dtype=np.float32)
    w_in = np.asarray(inp["L0_attn_b_w_in"], dtype=np.float32)
    s = {}
    s["wqk"] = f(np.concatenate([_blk_lhsT(W_attn[:, 0:D]),
                                 _blk_lhsT(W_attn[:, D:2 * D])], axis=0))
    s["wv"] = f(W_attn[:, 2 * D:3 * D])
    s["battn"] = np.asarray(inp["L0_attn_a_b_attn"], dtype=np.float32)
    s["wproj"] = f(np.asarray(inp["L0_attn_a_W_proj"], dtype=np.float32))
    s["bprojb"] = f(np.broadcast_to(
        np.asarray(inp["L0_attn_a_b_proj"], dtype=np.float32), (128, D)).copy())
    s["wg"] = _blk_lhsT(np.asarray(inp["L0_ffn_w_gate"]))
    s["wu"] = _blk_lhsT(np.asarray(inp["L0_ffn_w_up"]))
    s["wd"] = f(np.asarray(inp["L0_ffn_w_down"]))
    s["sg"] = np.asarray(inp["L0_ffn_s_gate"], dtype=np.float32)
    s["su"] = np.asarray(inp["L0_ffn_s_up"], dtype=np.float32)
    s["sdownb"] = f(np.broadcast_to(
        np.asarray(inp["L0_ffn_s_down"], dtype=np.float32), (128, D)).copy())
    s["normwb"] = f(np.broadcast_to(
        np.asarray(inp["L0_attn_b_norm_w"], dtype=np.float32), (128, D)).copy())
    s["gnormwb"] = f(np.broadcast_to(
        np.asarray(inp["L0_attn_b_gnorm_w"], dtype=np.float32), (128, D)).copy())
    s["wz"] = _blk_lhsT(w_in[:, 0:D])
    s["wbc"] = _blk_lhsT(w_in[:, D:D + CONVD])
    s["wdt"] = f(w_in[:, D + CONVD:].reshape(KC, 128, SH).transpose(1, 0, 2))
    s["convw"] = np.asarray(inp["L0_attn_b_conv_w"], dtype=np.float32)
    s["convb"] = np.asarray(inp["L0_attn_b_conv_b"], dtype=np.float32)
    s["dtb"] = np.asarray(inp["L0_attn_b_dt_bias"], dtype=np.float32).reshape(SH, 1)
    s["alog"] = np.asarray(inp["L0_attn_b_A_log"], dtype=np.float32).reshape(SH, 1)
    s["dsk"] = np.asarray(inp["L0_attn_b_D_skip"], dtype=np.float32).reshape(SH, 1)
    s["wout"] = _blk_lhsT(np.asarray(inp["L0_attn_b_w_out"], dtype=np.float32))
    s["wo"] = f(np.asarray(inp["L0_attn_b_wo_w"], dtype=np.float32))
    s["eye128b"] = np.eye(128, dtype=bf)
    s["eye128f"] = np.eye(128, dtype=np.float32)
    s["eyetb"] = f(np.tile(np.eye(64, dtype=np.float32), (1, SH)).astype(bf))
    eh = np.zeros((33, SH * CH), dtype=np.float32)
    for h in range(SH):
        eh[h, h * CH:(h + 1) * CH] = 1.0
    s["eh32"] = f(eh.astype(bf))
    ar = np.arange(SEQ, dtype=np.float32)

    def hilo(v):
        hi = v.astype(bf)
        lo = (v - hi.astype(np.float32)).astype(bf)
        return hi.reshape(1, SEQ), lo.reshape(1, SEQ)

    s["miota_hi"], s["miota_lo"] = hilo(M_ALIBI * ar)
    s["negmiota_hi"], s["negmiota_lo"] = hilo(-M_ALIBI * ar)
    s["negmiotacols"] = f((-M_ALIBI * (np.arange(16)[None, :] * 128
                                       + np.arange(128)[:, None])).astype(np.float32))
    s["onesrowb"] = np.ones((1, SEQ), dtype=bf)
    s["onestile"] = np.ones((128, 16), dtype=bf)
    s["diagmask"] = f(np.where(np.arange(128)[None, :] <= np.arange(128)[:, None],
                               0.0, MASKNEG).astype(np.float32))
    s["tri128"] = f(np.where(np.arange(128)[:, None] <= np.arange(128)[None, :],
                             1.0, 0.0).astype(bf))
    s["lt64"] = f(np.where(np.arange(64)[None, :] <= np.arange(64)[:, None],
                           1.0, 0.0).astype(bf))
    s["ones3264"] = np.ones((SH, 64), dtype=bf)
    return s



class C:
    def __init__(self, nc, block, sems):
        self.nc = nc
        self.block = block
        self.s = sems
        self.n = {k: 0 for k in sems}

    def inc(self, name, k=1):
        self.n[name] += k
        return self.n[name]

    def fence(self):
        vals = {k: self.n[k] for k in SEMS}
        for eng in ("tensor", "vector", "scalar", "sync", "gpsimd"):
            @getattr(self.block, eng)
            def _(e, vals=vals):
                for k, v in vals.items():
                    if v > 0:
                        e.wait_ge(self.s[k], v)


SEMS = ["LW", "LA", "LG", "SS", "SG", "MM", "EV", "AC", "PL", "OUT",
        "CHT", "CHV", "CHS", "CHP"]

_UNIQ = [0]


def _uid():
    _UNIQ[0] += 1
    return _UNIQ[0]



def _phase_resln(c, tb, src_x, src_delta, delta_done, src_done,
                 dst_x, xn_dst, extra_rms, final=False, bias_d=None):
    nc = c.nc
    u_ = _uid()
    st = contextlib.ExitStack()
    with st:
        xa = [st.enter_context(nc.sbuf_tensor(f"rl{u_}_xa{j}", [128, D], F32)) for j in (0, 1)]
        da = [st.enter_context(nc.sbuf_tensor(f"rl{u_}_da{j}", [128, D], F32)) for j in (0, 1)]
        xs = [st.enter_context(nc.sbuf_tensor(f"rl{u_}_xs{j}", [128, D], F32)) for j in (0, 1)]
        xc = [st.enter_context(nc.sbuf_tensor(f"rl{u_}_xc{j}", [128, D], F32)) for j in (0, 1)]
        xnb = [st.enter_context(nc.sbuf_tensor(f"rl{u_}_xnb{j}", [128, D], BF16)) for j in (0, 1)]
        xnb2 = [st.enter_context(nc.sbuf_tensor(f"rl{u_}_xnb2{j}", [128, D], BF16)) for j in (0, 1)]
        trash = st.enter_context(nc.sbuf_tensor(f"rl{u_}_trash", [128, D], BF16))
        trash2 = [st.enter_context(nc.sbuf_tensor(f"rl{u_}_tr2{j}", [128, D], BF16)) for j in (0, 1)]
        cols = [st.enter_context(nc.sbuf_tensor(f"rl{u_}_col{j}", [128, 8], F32)) for j in (0, 1)]
        if extra_rms:
            normwb = st.enter_context(nc.sbuf_tensor(f"rl{u_}_nw", [128, D], F32))
            lg_nw = c.inc("LG", 16)

            @c.block.gpsimd
            def _(g):
                g.dma_start(normwb[:], tb["normwb_d"][:]).then_inc(c.s["LG"], 16)
        if bias_d is not None:
            biasb = st.enter_context(nc.sbuf_tensor(f"rl{u_}_bias", [128, D], F32))
            lg_bias = c.inc("LG", 16)

            @c.block.gpsimd
            def _(g):
                g.dma_start(biasb[:], bias_d[:]).then_inc(c.s["LG"], 16)

        has_delta = src_delta is not None
        n_loads = 2 if has_delta else 1
        la0 = c.n["LA"]
        ss0 = c.n["SS"]
        out0 = c.n["OUT"]
        n_stores = (1 if dst_x is not None else 0) + (1 if xn_dst is not None else 0)
        la_tile = [la0 + 16 * n_loads * (i + 1) for i in range(NTT)]
        ev0 = c.n["EV"]
        ac0 = c.n["AC"]
        n_ev_tile = 3 if extra_rms else 2
        n_ac_tile = 2 if extra_rms else 1
        if final:
            n_ev_tile, n_ac_tile = 1, 0

        def ev_sq(i):
            return ev0 + n_ev_tile * i + 1

        def ev_ms2(i):
            return ev0 + n_ev_tile * i + 2

        def ev_last(i):
            return ev0 + n_ev_tile * (i + 1)

        def ac_rstd(i):
            return ac0 + n_ac_tile * i + 1

        def ac_rrms(i):
            return ac0 + n_ac_tile * i + 2

        def ss_tile(i):
            return ss0 + 16 * n_stores * (i + 1)

        xnew_done = []
        xn_done = []
        for i in range(NTT):
            if final:
                xnew_done.append(out0 + 16 * (i + 1))
                xn_done.append(None)
            else:
                base = ss0 + 16 * n_stores * i
                xnew_done.append(base + 16 if dst_x is not None else None)
                xn_done.append(ss_tile(i) if xn_dst is not None else None)

        @c.block.sync
        def _(sy):
            for i in range(NTT):
                j = i % 2
                if i >= 2:
                    sy.wait_ge(c.s["EV"], ev_last(i - 2))
                if src_done is not None:
                    sy.wait_ge(c.s["SS"], src_done[i])
                sy.dma_start(xa[j][:], src_x[i * 128:(i + 1) * 128, :]
                             ).then_inc(c.s["LA"], 16)
                if has_delta:
                    if delta_done is not None:
                        sy.wait_ge(c.s["SS"], delta_done[i])
                    sy.dma_start(da[j][:], src_delta[i * 128:(i + 1) * 128, :]
                                 ).then_inc(c.s["LA"], 16)
                sy.wait_ge(c.s["EV"], ev_last(i))
                if final:
                    sy.dma_start(dst_x[i * 128:(i + 1) * 128, :], xs[j][:]
                                 ).then_inc(c.s["OUT"], 16)
                else:
                    if dst_x is not None:
                        sy.dma_start(dst_x[i * 128:(i + 1) * 128, :], xs[j][:]
                                     ).then_inc(c.s["SS"], 16)
                    if xn_dst is not None:
                        sy.dma_start(xn_dst[i * 128:(i + 1) * 128, :],
                                     (xnb2[j] if extra_rms else xnb[j])[:]
                                     ).then_inc(c.s["SS"], 16)
            if final:
                sy.wait_ge(c.s["OUT"], out0 + 16 * NTT)

        @c.block.vector
        def _(v):
            if extra_rms:
                v.wait_ge(c.s["LG"], lg_nw)
            if bias_d is not None:
                v.wait_ge(c.s["LG"], lg_bias)
            for i in range(NTT):
                j = i % 2
                v.wait_ge(c.s["LA"], la_tile[i])
                if i >= 2:
                    if final:
                        v.wait_ge(c.s["OUT"], out0 + 16 * (i - 1))
                    else:
                        v.wait_ge(c.s["SS"], ss_tile(i - 2))
                if has_delta and bias_d is not None:
                    v.tensor_tensor(xc[j][:], xa[j][:], da[j][:], ALU.add
                                    ).then_inc(c.s["CHV"], 1)
                    v.wait_ge(c.s["CHV"], c.inc("CHV"))
                    op = v.tensor_tensor(xs[j][:], xc[j][:], biasb[:], ALU.add)
                elif has_delta:
                    op = v.tensor_tensor(xs[j][:], xa[j][:], da[j][:], ALU.add)
                else:
                    op = v.tensor_copy(xs[j][:], xa[j][:])
                if final:
                    op.then_inc(c.s["EV"], 1)
                    continue
                op.then_inc(c.s["CHV"], 1)
                v.wait_ge(c.s["CHV"], c.inc("CHV"))
                v.reduce_sum(cols[j][:, 0:1], xs[j][:], axis=AX.X, negate=True
                             ).then_inc(c.s["CHV"], 1)
                v.wait_ge(c.s["CHV"], c.inc("CHV"))
                v.tensor_scalar(cols[j][:, 1:2], cols[j][:, 0:1], 1.0 / D, None, ALU.mult
                                ).then_inc(c.s["CHV"], 1)
                v.wait_ge(c.s["CHV"], c.inc("CHV"))
                v.tensor_scalar(xc[j][:], xs[j][:], cols[j][:, 1:2], None, ALU.add
                                ).then_inc(c.s["EV"], 1)
                v.wait_ge(c.s["AC"], ac_rstd(i))
                if not extra_rms:
                    v.tensor_scalar(xnb[j][:], xc[j][:], cols[j][:, 4:5], None, ALU.mult
                                    ).then_inc(c.s["EV"], 1)
                else:
                    v.tensor_scalar(xnb[j][:], xc[j][:], cols[j][:, 4:5], None, ALU.mult
                                    ).then_inc(c.s["EV"], 1)
                    v.wait_ge(c.s["AC"], ac_rrms(i))
                    v.tensor_scalar(trash2[j][:], xnb[j][:], cols[j][:, 7:8], None, ALU.mult
                                    ).then_inc(c.s["CHV"], 1)
                    v.wait_ge(c.s["CHV"], c.inc("CHV"))
                    v.tensor_tensor(xnb2[j][:], trash2[j][:], normwb[:], ALU.mult
                                    ).then_inc(c.s["EV"], 1)

        if not final:
            @c.block.scalar
            def _(s):
                for i in range(NTT):
                    j = i % 2
                    s.wait_ge(c.s["EV"], ev_sq(i))
                    s.activation(trash[:], xc[j][:], AF.Square, bias=0.0, scale=1.0,
                                 accum_out=cols[j][:, 2:3]).then_inc(c.s["CHS"], 1)
                    s.wait_ge(c.s["CHS"], c.inc("CHS"))
                    s.activation(cols[j][:, 3:4], cols[j][:, 2:3], AF.Ln,
                                 bias=tb["misc"][:, 0:1], scale=tb["misc"][:, 1:2]
                                 ).then_inc(c.s["CHS"], 1)
                    s.wait_ge(c.s["CHS"], c.inc("CHS"))
                    s.activation(cols[j][:, 4:5], cols[j][:, 3:4], AF.Exp,
                                 bias=0.0, scale=tb["misc"][:, 2:3]).then_inc(c.s["AC"], 1)
                    if extra_rms:
                        s.wait_ge(c.s["EV"], ev_ms2(i))
                        s.activation(trash[:], xnb[j][:], AF.Square, bias=0.0,
                                     scale=1.0, accum_out=cols[j][:, 5:6]
                                     ).then_inc(c.s["CHS"], 1)
                        s.wait_ge(c.s["CHS"], c.inc("CHS"))
                        s.activation(cols[j][:, 6:7], cols[j][:, 5:6], AF.Ln,
                                     bias=tb["misc"][:, 0:1], scale=tb["misc"][:, 1:2]
                                     ).then_inc(c.s["CHS"], 1)
                        s.wait_ge(c.s["CHS"], c.inc("CHS"))
                        s.activation(cols[j][:, 7:8], cols[j][:, 6:7], AF.Exp,
                                     bias=0.0, scale=tb["misc"][:, 2:3]).then_inc(c.s["AC"], 1)

        c.n["LA"] = la0 + 16 * n_loads * NTT
        c.n["EV"] = ev0 + n_ev_tile * NTT
        c.n["AC"] = ac0 + n_ac_tile * NTT
        if final:
            c.n["OUT"] = out0 + 16 * NTT
        else:
            c.n["SS"] = ss0 + 16 * n_stores * NTT
    return xnew_done, xn_done



def _phase_init_tables(c, tb, L):
    nc = c.nc
    st = contextlib.ExitStack()
    with st:
        dskrow = st.enter_context(nc.sbuf_tensor("ti_dskrow", [SH, 64], BF16))
        eh32sb = st.enter_context(nc.sbuf_tensor("ti_eh32", [33, SH * CH], BF16))
        tb["eh32"] = eh32sb
        ones3264 = st.enter_context(nc.sbuf_tensor("ti_o3264", [SH, 64], BF16))
        tmpc = st.enter_context(nc.sbuf_tensor("ti_tmpc", [SH, 1], F32))
        ps2 = st.enter_context(nc.psum_tensor("ti_ps2", [64, SH * CH], F32))

        @c.block.gpsimd
        def _(g):
            for dst, src in [
                (tb["bq"][:], L["battn"][0:D].rearrange("(m p) -> p m", p=128)),
                (tb["bk"][:], L["battn"][D:2 * D].rearrange("(m p) -> p m", p=128)),
                (tb["sg"][:], L["sg_in"].rearrange("(m p) -> p m", p=128)),
                (tb["su"][:], L["su_in"].rearrange("(m p) -> p m", p=128)),
                (tb["convw"].ap().rearrange("p (cc k) -> p cc k", cc=CCC),
                 L["convw"].rearrange("(cc p) k -> p cc k", p=128)),
                (tb["convb"][:], L["convb"].rearrange("(cc p) -> p cc", p=128)),
            ]:
                g.dma_start(dst, src, allow_slow_non_contiguous=True
                            ).then_inc(c.s["LG"], 16); c.inc("LG", 16)

        @c.block.sync
        def _(sy):
            for name, src in [("eye128b", L["eye128b"]), ("eye128f", L["eye128f"]),
                              ("eyetb", L["eyetb"]), ("eh32", L["eh32"]),
                              ("negmiotacols", L["negmiotacols"]),
                              ("diagmask", L["diagmask"]),
                              ("tri128", L["tri128"]), ("lt64", L["lt64"]),
                              ("dtb", L["dtb"]), ("alogcol", L["alog"]),
                              ("dskcol", L["dsk"]), ("ones3264i", None)]:
                if name == "ones3264i":
                    sy.dma_start(ones3264[:], L["ones3264_in"][:]
                                 ).then_inc(c.s["LA"], 16); c.inc("LA", 16)
                else:
                    sy.dma_start(tb[name][:], src[:]).then_inc(c.s["LA"], 16)
                    c.inc("LA", 16)
        la_t = c.n["LA"]
        lg_t = c.n["LG"]

        @c.block.scalar
        def _(s):
            s.wait_ge(c.s["LA"], la_t)
            s.activation(tmpc[:], tb["alogcol"][:], AF.Exp, bias=0.0, scale=1.0
                         ).then_inc(c.s["AC"], 1); c.inc("AC")
        ac_a = c.n["AC"]

        @c.block.vector
        def _(v):
            v.memset(tb["misc"][:, 0:1], EPS)
            v.memset(tb["misc"][:, 1:2], 1.0 / D)
            v.memset(tb["misc"][:, 2:3], -0.5)
            v.memset(tb["misc"][:, 3:4], -1.0)
            v.memset(tb["misc"][:, 4:5], 1.0)
            v.wait_ge(c.s["LA"], la_t)
            v.wait_ge(c.s["LG"], lg_t)
            v.wait_ge(c.s["AC"], ac_a)
            v.tensor_scalar(tb["anegcol"][:], tmpc[:], -1.0, None, ALU.mult)
            v.tensor_scalar(dskrow[:], ones3264[:], tb["dskcol"][:], None, ALU.mult
                            ).then_inc(c.s["EV"], 1); c.inc("EV")
        ev_dr = c.n["EV"]

        @c.block.tensor
        def _(t):
            t.wait_ge(c.s["EV"], ev_dr)
            for r in range(4):
                mm = t.matmul(ps2[:, r * 512:(r + 1) * 512], dskrow[:],
                              tb["eh32"][0:SH, r * 512:(r + 1) * 512],
                              start=True, stop=True)
            mm.then_inc(c.s["MM"], 1); c.inc("MM")
        mm_f = c.n["MM"]

        @c.block.vector
        def _(v):
            v.wait_ge(c.s["MM"], mm_f)
            v.tensor_tensor(tb["dskd"][:], ps2[:], tb["eyetb"][:], ALU.mult
                            ).then_inc(c.s["EV"], 1); c.inc("EV")
    return {"la": la_t, "lg": lg_t, "ev": c.n["EV"], "ac": c.n["AC"], "mm": c.n["MM"]}



def _emit_fm_rebuild(c, tb, src_dram, src_done, fmbuf, stg, trp, ntt=NTT, row0=0):
    tw = ntt * 128
    la0 = c.n["LA"]
    mm0 = c.n["MM"]
    ev0 = c.n["EV"]

    def mm_tile(i):
        return mm0 + KC * (i + 1)

    def ev_tile(i):
        return ev0 + KC * (i + 1)

    @c.block.sync
    def _(sy):
        for i in range(ntt):
            if i >= 2:
                sy.wait_ge(c.s["MM"], mm_tile(i - 2))
            if src_done is not None:
                sy.wait_ge(c.s["SS"], src_done[i])
            sy.dma_start(stg[i % 2][:],
                         src_dram[(row0 + i) * 128:(row0 + i + 1) * 128, :]
                         ).then_inc(c.s["LA"], 16)

    @c.block.tensor
    def _(t):
        for i in range(ntt):
            t.wait_ge(c.s["LA"], la0 + 16 * (i + 1))
            for k in range(KC):
                idx = i * KC + k
                if idx >= 2:
                    t.wait_ge(c.s["EV"], ev0 + idx - 1)
                t.transpose(trp[idx % 2][:, 0:128],
                            stg[i % 2][:, k * 128:(k + 1) * 128],
                            tb["eye128b"][:]).then_inc(c.s["MM"], 1)

    @c.block.vector
    def _(v):
        for i in range(ntt):
            for k in range(KC):
                idx = i * KC + k
                v.wait_ge(c.s["MM"], mm0 + idx + 1)
                v.tensor_copy(fmbuf[:, k * tw + i * 128: k * tw + (i + 1) * 128],
                              trp[idx % 2][:, 0:128]).then_inc(c.s["EV"], 1)

    c.n["LA"] = la0 + 16 * ntt
    c.n["MM"] = mm0 + KC * ntt
    c.n["EV"] = ev0 + KC * ntt
    return c.n["MM"], c.n["EV"]



def _phase_qkv(c, tb, xn_d, xn_done, wqk, wv, qf, kf, vt):
    nc = c.nc
    st = contextlib.ExitStack()
    with st:
        xnfm = st.enter_context(nc.sbuf_tensor("qk_xnfm", [128, KC * T], BF16))
        stg = [st.enter_context(nc.sbuf_tensor(f"qk_stg{j}", [128, D], BF16)) for j in (0, 1)]
        wt = [st.enter_context(nc.sbuf_tensor(f"qk_wt{j}", [128, KC * 128], BF16)) for j in (0, 1)]
        qstg = [st.enter_context(nc.sbuf_tensor(f"qk_qstg{j}", [128, 2048], BF16)) for j in (0, 1)]
        bvb = st.enter_context(nc.sbuf_tensor("qk_bvb", [128, D], F32))
        onescol = st.enter_context(nc.sbuf_tensor("qk_ones", [1, 128], F32))

        with (nc.psum_tensor("qk_trp0", [128, 1024], BF16) as trp0,
              nc.psum_tensor("qk_trp1", [128, 1024], BF16) as trp1,
              nc.psum_tensor("qk_psbv", [128, D], F32) as psbv):
            trp = [trp0, trp1]
            la_bv = c.inc("LA", 16)

            @c.block.sync
            def _(sy):
                sy.dma_start(bvb[0:1, :],
                             tb["battn_d"][2 * D:3 * D].rearrange("(o d) -> o d", o=1)
                             ).then_inc(c.s["LA"], 16)

            @c.block.vector
            def _(v):
                v.memset(onescol[:], 1.0).then_inc(c.s["EV"], 1)
            ev_ones = c.inc("EV")

            @c.block.tensor
            def _(t):
                t.wait_ge(c.s["LA"], la_bv)
                t.wait_ge(c.s["EV"], ev_ones)
                for r in range(4):
                    mm = t.matmul(psbv[:, r * 512:(r + 1) * 512], onescol[:],
                                  bvb[0:1, r * 512:(r + 1) * 512],
                                  start=True, stop=True)
                mm.then_inc(c.s["MM"], 1)
            mm_bv = c.inc("MM")

            @c.block.vector
            def _(v):
                v.wait_ge(c.s["MM"], mm_bv)
                v.tensor_copy(bvb[:], psbv[:]).then_inc(c.s["EV"], 1)
            ev_bvb = c.inc("EV")

            mm_fm, ev_fm = _emit_fm_rebuild(c, tb, xn_d, xn_done, xnfm, stg, trp)
        psA = st.enter_context(nc.psum_tensor("qk_psA", [128, 4096], F32))

        lw0 = c.n["LW"]
        mm0 = c.n["MM"]
        ev0 = c.n["EV"]
        ss0 = c.n["SS"]

        def mmh(m, half):
            return mm0 + 2 * m + half + 1

        def evh(m, half):
            return ev0 + 2 * m + half + 1

        def ssh(m, half):
            return ss0 + 16 * (2 * m + half + 1)

        @c.block.gpsimd
        def _(g):
            for m in range(32):
                if m >= 2:
                    g.wait_ge(c.s["MM"], mmh(m - 1, 1))
                g.dma_start(wt[m % 2][:], wqk[m]).then_inc(c.s["LW"], 16)

        @c.block.tensor
        def _(t):
            t.wait_ge(c.s["EV"], ev_fm)
            for m in range(32):
                t.wait_ge(c.s["LW"], lw0 + 16 * (m + 1))
                for half in range(2):
                    if m > 0:
                        t.wait_ge(c.s["EV"], evh(m - 1, half))
                    for k in range(KC):
                        lt = wt[m % 2][:, k * 128:(k + 1) * 128]
                        for tc in range(4):
                            col = half * 2048 + tc * 512
                            mm = t.matmul(psA[:, col:col + 512], lt,
                                          xnfm[:, k * T + col:k * T + col + 512],
                                          start=(k == 0), stop=(k == KC - 1))
                    mm.then_inc(c.s["MM"], 1)

        @c.block.vector
        def _(v):
            for m in range(32):
                for half in range(2):
                    if m >= 1:
                        v.wait_ge(c.s["SS"], ssh(m - 1, half))
                    v.wait_ge(c.s["MM"], mmh(m, half))
                    sl = slice(half * 2048, (half + 1) * 2048)
                    if m < 16:
                        v.tensor_scalar(qstg[half][:], psA[:, sl],
                                        tb["bq"][:, m:m + 1], 0.125,
                                        ALU.add, ALU.mult).then_inc(c.s["EV"], 1)
                    else:
                        v.tensor_scalar(qstg[half][:], psA[:, sl],
                                        tb["bk"][:, m - 16:m - 15], None,
                                        ALU.add).then_inc(c.s["EV"], 1)

        @c.block.sync
        def _(sy):
            for m in range(32):
                dst = qf if m < 16 else kf
                r = (m % 16) * 128
                for half in range(2):
                    sy.wait_ge(c.s["EV"], evh(m, half))
                    sy.dma_start(dst[r:r + 128, half * 2048:(half + 1) * 2048],
                                 qstg[half][:]).then_inc(c.s["SS"], 16)

        c.n["LW"] = lw0 + 16 * 32
        c.n["MM"] = mm0 + 64
        c.n["EV"] = ev0 + 64
        c.n["SS"] = ss0 + 16 * 64
        q_done = ss0 + 16 * 32
        qk_done = c.n["SS"]

        wvt = st.enter_context(nc.sbuf_tensor("qk_wvt", [128, KC * 512], BF16))
        vstg = [st.enter_context(nc.sbuf_tensor(f"qk_vstg{j}", [128, 2048], BF16)) for j in (0, 1)]
        lw0 = c.n["LW"]
        mm0 = c.n["MM"]
        ev0 = c.n["EV"]
        ss0 = c.n["SS"]

        def vmm(vg, tg):
            return mm0 + (vg * 8 + tg) + 1

        def vev(vg, tg):
            return ev0 + (vg * 8 + tg) + 1

        def vss(vg, tg):
            return ss0 + 16 * ((vg * 8 + tg) + 1)

        @c.block.gpsimd
        def _(g):
            for vg in range(4):
                if vg > 0:
                    g.wait_ge(c.s["MM"], vmm(vg - 1, 7))
                for k in range(KC):
                    g.dma_start(wvt[:, k * 512:(k + 1) * 512],
                                wv[k * 128:(k + 1) * 128, vg * 512:(vg + 1) * 512]
                                ).then_inc(c.s["LW"], 16)

        @c.block.tensor
        def _(t):
            for vg in range(4):
                t.wait_ge(c.s["LW"], lw0 + 16 * KC * (vg + 1))
                for tg in range(8):
                    gi = vg * 8 + tg
                    if gi >= 2:
                        t.wait_ge(c.s["EV"], vev((gi - 2) // 8, (gi - 2) % 8))
                    for k in range(KC):
                        for ti in range(4):
                            tile = tg * 4 + ti
                            bank = (tg % 2) * 2048 + ti * 512
                            mm = t.matmul(psA[:, bank:bank + 512],
                                          xnfm[:, k * T + tile * 128:k * T + (tile + 1) * 128],
                                          wvt[:, k * 512:(k + 1) * 512],
                                          start=(k == 0), stop=(k == KC - 1))
                    mm.then_inc(c.s["MM"], 1)

        @c.block.vector
        def _(v):
            for vg in range(4):
                for tg in range(8):
                    gi = vg * 8 + tg
                    j = gi % 2
                    if gi >= 2:
                        v.wait_ge(c.s["SS"], vss((gi - 2) // 8, (gi - 2) % 8))
                    v.wait_ge(c.s["MM"], vmm(vg, tg))
                    for ti in range(4):
                        bank = (tg % 2) * 2048 + ti * 512
                        op = v.tensor_tensor(vstg[j][:, ti * 512:(ti + 1) * 512],
                                             psA[:, bank:bank + 512],
                                             bvb[:, vg * 512:(vg + 1) * 512], ALU.add)
                    op.then_inc(c.s["EV"], 1)

        @c.block.sync
        def _(sy):
            for vg in range(4):
                for tg in range(8):
                    gi = vg * 8 + tg
                    sy.wait_ge(c.s["EV"], vev(vg, tg))
                    dst = bass.AP(vt.tensor, (tg * 512) * D + vg * 512,
                                  [[D, 128], [128 * D, 4], [1, 512]])
                    sy.dma_start(dst, vstg[gi % 2].ap().rearrange(
                        "p (ti c) -> p ti c", ti=4)).then_inc(c.s["SS"], 16)

        c.n["LW"] = lw0 + 16 * KC * 4
        c.n["MM"] = mm0 + 32
        c.n["EV"] = ev0 + 32
        c.n["SS"] = ss0 + 16 * 32
        v_done = c.n["SS"]
    return q_done, qk_done, v_done



def _phase_attn(c, tb, qf, kf, vt, v_done, wproj_d, dout):
    nc = c.nc
    st = contextlib.ExitStack()
    with st:
        qx = [st.enter_context(nc.sbuf_tensor(f"at_qx{j}", [128, SEQ], BF16)) for j in (0, 1)]
        kx = [st.enter_context(nc.sbuf_tensor(f"at_kx{j}", [128, SEQ], BF16)) for j in (0, 1)]
        vx = [st.enter_context(nc.sbuf_tensor(f"at_vx{j}", [128, 16 * 65], BF16)) for j in (0, 1)]
        probs = [st.enter_context(nc.sbuf_tensor(f"at_pr{j}", [128, 512], BF16)) for j in (0, 1)]
        probsd = st.enter_context(nc.sbuf_tensor("at_prd", [128, 128], BF16))
        dmsb = [st.enter_context(nc.sbuf_tensor(f"at_dmsb{j}", [128, 128], F32)) for j in (0, 1)]
        dclamp = st.enter_context(nc.sbuf_tensor("at_dclamp", [128, 128], F32))
        trash_e = st.enter_context(nc.sbuf_tensor("at_trash", [128, 512], BF16))
        mx4 = st.enter_context(nc.sbuf_tensor("at_mx4", [128, 8], F32))
        scol = st.enter_context(nc.sbuf_tensor("at_scol", [128, 8], F32))
        nm1 = st.enter_context(nc.sbuf_tensor("at_nm1", [128, 4], F32))
        nmls = st.enter_context(nc.sbuf_tensor("at_nmls", [128, 16], F32))
        nmrow = st.enter_context(nc.sbuf_tensor("at_nmrow", [16, 128], F32))
        vcols = st.enter_context(nc.sbuf_tensor("at_vcols", [128, 16], F32))
        rowvf = st.enter_context(nc.sbuf_tensor("at_rowvf", [1, SEQ], F32))
        row66 = st.enter_context(nc.sbuf_tensor("at_row66", [1, SEQ], BF16))
        row67 = st.enter_context(nc.sbuf_tensor("at_row67", [1, SEQ], BF16))
        attn_fm = st.enter_context(nc.sbuf_tensor("at_afm", [128, KC * SEQ], BF16))
        wpj = st.enter_context(nc.sbuf_tensor("at_wpj", [128, KC * 2048], BF16))
        pstg = [st.enter_context(nc.sbuf_tensor(f"at_pstg{j}", [128, D], F32)) for j in (0, 1)]
        PS = st.enter_context(nc.psum_tensor("at_PS", [128, 4096], F32))

        def PBs(pb, n=512):
            return PS[:, pb * 512: pb * 512 + n]

        def P1s(p, n=512):
            return PS[:, 1024 + p * 512: 1024 + p * 512 + n]

        def P2s(a, b):
            return PS[0:65, 2048 + a: 2048 + b]

        def P2evac():
            return PS[0:64, 2048:4096]

        ops = {k: [] for k in ("tensor", "vector", "scalar", "sync", "gpsimd")}

        def emit(eng, fn):
            ops[eng].append(fn)

        def retag(eng, sem):
            last = ops[eng].pop()
            emit(eng, lambda e, last=last: last(e).then_inc(c.s[sem], 1))
            return c.inc(sem)

        def W(eng, sem, val):
            if val > 0:
                emit(eng, lambda e, s=sem, v=val: e.wait_ge(c.s[s], v))

        _CHMAP = {"tensor": "CHT", "vector": "CHV", "scalar": "CHS",
                  "gpsimd": "CHP"}

        def chain(eng):
            val = retag(eng, _CHMAP[eng])
            W(eng, _CHMAP[eng], val)

        ev_unit = [0] * (SPC * H)
        proj_done = []

        for j in (0, 1):
            for dst, srcd in [(lambda jj: qx[jj][64:65, :], "onesrowb_d"),
                              (lambda jj: qx[jj][65:66, :], "onesrowb_d"),
                              (lambda jj: kx[jj][64:65, :], "miota_hi_d"),
                              (lambda jj: kx[jj][65:66, :], "miota_lo_d"),
                              (lambda jj: kx[jj][66:67, :], "onesrowb_d"),
                              (lambda jj: kx[jj][67:68, :], "onesrowb_d")]:
                emit("sync", lambda sy, j=j, dst=dst, srcd=srcd: sy.dma_start(
                    dst(j), tb[srcd][:]).then_inc(c.s["LA"], 16))
                c.inc("LA", 16)
            emit("sync", lambda sy, j=j: sy.dma_start(
                bass.AP(vx[j], 64, [[65 * 16, 128], [65, 16]]),
                tb["onestile_d"][:],
                allow_slow_non_contiguous=True).then_inc(c.s["LA"], 16))
            c.inc("LA", 16)
        for k in range(KC):
            emit("gpsimd", lambda g, k=k: g.dma_start(
                wpj[:, k * 2048:(k + 1) * 2048],
                wproj_d[k * 128:(k + 1) * 128, :]).then_inc(c.s["LW"], 16))
            c.inc("LW", 16)
        lw_wpj = c.n["LW"]
        W("sync", "SS", v_done)
        W("gpsimd", "SS", v_done)

        p1_free = [("AC", 0), ("AC", 0)]
        pb_free = [("AC", 0), ("AC", 0)]
        probs_free = [("MM", 0), ("MM", 0)]
        p2_free = ("EV", 0)
        dms_free = [("AC", 0), ("AC", 0)]
        probsd_free = ("MM", 0)
        dclamp_free = ("AC", 0)
        proj_ps_free = [("EV", 0), ("EV", 0)]

        for s in range(SPC):
            for h in range(H):
                u = s * H + h
                j = u % 2
                kc2 = h // 2
                hrow = (h % 2) * 64

                if u >= 2:
                    W("sync", "EV", ev_unit[u - 2])
                    W("gpsimd", "EV", ev_unit[u - 2])
                emit("sync", lambda sy, j=j, h=h, s=s: sy.dma_start(
                    qx[j][0:64, :], qf[h * 64:(h + 1) * 64, s * SEQ:(s + 1) * SEQ]
                    ).then_inc(c.s["LA"], 16)); c.inc("LA", 16)
                emit("sync", lambda sy, j=j: sy.dma_start(
                    qx[j][66:67, :], tb["negmiota_hi_d"][:]).then_inc(c.s["LA"], 16))
                c.inc("LA", 16)
                emit("sync", lambda sy, j=j: sy.dma_start(
                    qx[j][67:68, :], tb["negmiota_lo_d"][:]).then_inc(c.s["LA"], 16))
                c.inc("LA", 16)
                emit("sync", lambda sy, j=j, h=h, s=s: sy.dma_start(
                    kx[j][0:64, :], kf[h * 64:(h + 1) * 64, s * SEQ:(s + 1) * SEQ]
                    ).then_inc(c.s["LA"], 16)); c.inc("LA", 16)
                la_u = c.n["LA"]
                emit("gpsimd", lambda g, j=j, h=h, s=s: g.dma_start(
                    bass.AP(vx[j], 0, [[65 * 16, 128], [65, 16], [1, 64]]),
                    bass.AP(vt.tensor, s * SEQ * D + h * 64,
                            [[D, 128], [128 * D, 16], [1, 64]])
                    ).then_inc(c.s["LG"], 16))
                lg_u = c.inc("LG", 16)
                W("tensor", "LA", la_u)
                W("tensor", "LG", lg_u)

                for qi in range(16):
                    span = (qi + 1) * 128
                    chunks = []
                    t0 = 0
                    while t0 < span:
                        chunks.append((t0, min(t0 + 512, span)))
                        t0 += 512
                    np_ = 0
                    pa_seq = []
                    for ci, (a, b) in enumerate(chunks):
                        pidx = (qi + ci) % 2
                        pa_seq.append(pidx)
                        sem, val = p1_free[pidx]
                        W("tensor", sem, val)
                        emit("tensor", lambda t, j=j, qi=qi, a=a, b=b, pidx=pidx:
                             t.matmul(P1s(pidx, b - a),
                                      qx[j][0:68, qi * 128:(qi + 1) * 128],
                                      kx[j][0:68, a:b], start=True, stop=True
                                      ).then_inc(c.s["MM"], 1))
                        mmv = c.inc("MM")
                        W("vector", "MM", mmv)
                        if b == span:
                            dstart = qi * 128 - a
                            if dstart > 0:
                                emit("vector", lambda v, pidx=pidx, dstart=dstart, n=np_:
                                     v.reduce_max(mx4[:, n:n + 1], P1s(pidx, dstart),
                                                  axis=AX.X))
                                np_ += 1
                            sem, val = dms_free[qi % 2]
                            W("vector", sem, val)
                            emit("vector", lambda v, pidx=pidx, dstart=dstart, qi=qi:
                                 v.tensor_tensor(dmsb[qi % 2][:],
                                                 PS[:, 1024 + pidx * 512 + dstart:
                                                     1024 + pidx * 512 + dstart + 128],
                                                 tb["diagmask"][:], ALU.add))
                            chain("vector")
                            emit("vector", lambda v, n=np_, qi=qi:
                                 v.reduce_max(mx4[:, n:n + 1], dmsb[qi % 2][:],
                                              axis=AX.X))
                            np_ += 1
                            ev = retag("vector", "EV")
                            p1_free[pidx] = ("EV", ev)
                        else:
                            emit("vector", lambda v, pidx=pidx, n=np_, a=a, b=b:
                                 v.reduce_max(mx4[:, n:n + 1], P1s(pidx, b - a),
                                              axis=AX.X))
                            np_ += 1
                            ev = retag("vector", "EV")
                            p1_free[pidx] = ("EV", ev)
                    qp = qi % 2
                    W("vector", "EV", c.n["EV"])
                    emit("vector", lambda v, n=np_, qp=qp:
                         v.reduce_max(nm1[:, qp:qp + 1], mx4[:, 0:n], axis=AX.X,
                                      negate=True))
                    ev_nm = retag("vector", "EV")
                    ns = 0
                    for ci, (a, b) in enumerate(chunks):
                        pidx = pa_seq[ci]
                        sem, val = p1_free[pidx]
                        W("tensor", sem, val)
                        emit("tensor", lambda t, j=j, qi=qi, a=a, b=b, pidx=pidx:
                             t.matmul(P1s(pidx, b - a),
                                      qx[j][0:68, qi * 128:(qi + 1) * 128],
                                      kx[j][0:68, a:b], start=True, stop=True
                                      ).then_inc(c.s["MM"], 1))
                        mmv = c.inc("MM")
                        W("scalar", "MM", mmv)
                        W("scalar", "EV", ev_nm)
                        if b == span:
                            dstart = qi * 128 - a
                            if dstart > 0:
                                emit("scalar", lambda sc, pidx=pidx, dstart=dstart, n=ns, qp=qp:
                                     sc.activation(trash_e[:, 0:dstart], P1s(pidx, dstart),
                                                   AF.Exp, bias=nm1[:, qp:qp + 1], scale=1.0,
                                                   accum_out=scol[:, n:n + 1]))
                                ns += 1
                            emit("scalar", lambda sc, n=ns, qi=qi, qp=qp:
                                 sc.activation(trash_e[:, 0:128], dmsb[qi % 2][:],
                                               AF.Exp, bias=nm1[:, qp:qp + 1], scale=1.0,
                                               accum_out=scol[:, n:n + 1]))
                            ns += 1
                            ac = retag("scalar", "AC")
                            dms_free[qi % 2] = ("AC", ac)
                            p1_free[pidx] = ("AC", ac)
                        else:
                            emit("scalar", lambda sc, pidx=pidx, a=a, b=b, n=ns, qp=qp:
                                 sc.activation(trash_e[:, 0:b - a], P1s(pidx, b - a),
                                               AF.Exp, bias=nm1[:, qp:qp + 1], scale=1.0,
                                               accum_out=scol[:, n:n + 1]))
                            ns += 1
                            ac = retag("scalar", "AC")
                            p1_free[pidx] = ("AC", ac)
                    W("vector", "AC", c.n["AC"])
                    emit("vector", lambda v, n=ns, qp=qp:
                         v.reduce_sum(nm1[:, 2 + qp:3 + qp], scol[:, 0:n], axis=AX.X))
                    ev_s = retag("vector", "EV")
                    W("scalar", "EV", ev_s)
                    emit("scalar", lambda sc, qp=qp: sc.activation(
                        scol[:, 6 + qp:7 + qp], nm1[:, 2 + qp:3 + qp], AF.Ln,
                        bias=0.0, scale=1.0))
                    ac_l = retag("scalar", "AC")
                    W("vector", "AC", ac_l)
                    emit("vector", lambda v, qi=qi, qp=qp: v.tensor_tensor(
                        nmls[:, qi:qi + 1], nm1[:, qp:qp + 1], scol[:, 6 + qp:7 + qp],
                        ALU.subtract))
                    c.n["EV"] = c.n["EV"]
                    retag("vector", "EV")

                mm_passA_last = c.n["MM"]
                W("vector", "EV", c.n["EV"])
                emit("vector", lambda v: v.tensor_tensor(
                    vcols[:], nmls[:], tb["negmiotacols"][:], ALU.add))
                ev_vc = retag("vector", "EV")
                W("tensor", "EV", ev_vc)
                sem, val = p1_free[0]
                W("tensor", sem, val)
                emit("tensor", lambda t: t.transpose(
                    PS[0:16, 1024:1152], vcols[:], tb["eye128f"][:]
                    ).then_inc(c.s["MM"], 1))
                mm_tr = c.inc("MM")
                W("vector", "MM", mm_tr)
                emit("vector", lambda v: v.tensor_copy(nmrow[:], PS[0:16, 1024:1152]))
                ev_nr = retag("vector", "EV")
                p1_free[0] = ("EV", ev_nr)
                W("gpsimd", "EV", ev_nr)
                emit("gpsimd", lambda g: g.dma_start(
                    rowvf[0:1, :], bass.AP(nmrow, 0, [[128, 16], [1, 128]])
                    ).then_inc(c.s["LG"], 16))
                lg_r = c.inc("LG", 16)
                W("vector", "LG", lg_r)
                emit("vector", lambda v: v.tensor_copy(row66[0:1, :], rowvf[0:1, :]))
                ev66 = retag("vector", "EV")
                W("vector", "EV", ev66)
                emit("vector", lambda v: v.tensor_tensor(
                    row67[0:1, :], rowvf[0:1, :], row66[0:1, :], ALU.subtract))
                ev_rows = retag("vector", "EV")
                W("gpsimd", "EV", ev_rows)
                W("gpsimd", "MM", mm_passA_last)
                emit("gpsimd", lambda g, j=j: g.dma_start(
                    qx[j][66:67, :], row66[0:1, :]).then_inc(c.s["LG"], 16))
                c.inc("LG", 16)
                emit("gpsimd", lambda g, j=j: g.dma_start(
                    qx[j][67:68, :], row67[0:1, :]).then_inc(c.s["LG"], 16))
                lg_r65 = c.inc("LG", 16)

                W("tensor", "LG", lg_r65)
                sem, val = p2_free
                W("tensor", sem, val)
                pb_idx = 0
                mm_last_av = 0
                for ki in range(16):
                    for qc in range(ki // 4, 4):
                        a = max(qc * 512, ki * 128)
                        b = (qc + 1) * 512
                        ln = b - a
                        pb = pb_idx % 2
                        pb_idx += 1
                        sem, val = pb_free[pb]
                        W("tensor", sem, val)
                        emit("tensor", lambda t, j=j, ki=ki, a=a, b=b, pb=pb:
                             t.matmul(PBs(pb, b - a),
                                      kx[j][0:68, ki * 128:(ki + 1) * 128],
                                      qx[j][0:68, a:b], start=True, stop=True
                                      ).then_inc(c.s["MM"], 1))
                        mmv = c.inc("MM")
                        first = (qc == ki // 4)
                        stop = (ki == min(4 * qc + 3, 15))
                        if first:
                            W("vector", "MM", mmv)
                            sem, val = dclamp_free
                            W("vector", sem, val)
                            emit("vector", lambda v, pb=pb: v.tensor_scalar(
                                dclamp[:], PBs(pb, 128), 30.0, None, ALU.min))
                            ev_cl = retag("vector", "EV")
                            W("scalar", "EV", ev_cl)
                        W("scalar", "MM", mmv)
                        sem, val = probs_free[pb]
                        W("scalar", sem, val)
                        if first:
                            emit("scalar", lambda sc, pb=pb:
                                 sc.activation(probs[pb][:, 0:128], dclamp[:],
                                               AF.Exp, bias=0.0, scale=1.0))
                            if ln > 128:
                                emit("scalar", lambda sc, pb=pb, ln=ln:
                                     sc.activation(probs[pb][:, 128:ln],
                                                   PS[:, pb * 512 + 128: pb * 512 + ln],
                                                   AF.Exp, bias=0.0, scale=1.0))
                        else:
                            emit("scalar", lambda sc, pb=pb, ln=ln:
                                 sc.activation(probs[pb][:, 0:ln], PBs(pb, ln),
                                               AF.Exp, bias=0.0, scale=1.0))
                        ac = retag("scalar", "AC")
                        pb_free[pb] = ("AC", ac)
                        if first:
                            dclamp_free = ("AC", ac)
                            W("vector", "AC", ac)
                            sem, val = probsd_free
                            W("vector", sem, val)
                            emit("vector", lambda v, pb=pb: v.tensor_tensor(
                                probsd[:], probs[pb][:, 0:128], tb["tri128"][:],
                                ALU.mult))
                            ev_d = retag("vector", "EV")
                            W("tensor", "EV", ev_d)
                            if not _DBG_SKIP_DIAG:
                                emit("tensor", lambda t, j=j, ki=ki, a=a:
                                     t.matmul(P2s(a, a + 128),
                                              vx[j][:, ki * 65:(ki + 1) * 65],
                                              probsd[:, 0:128], start=(ki == 0),
                                              stop=(4 * (a // 512) + 3 == ki)
                                              ).then_inc(c.s["MM"], 1))
                                probsd_free = ("MM", c.inc("MM"))
                                mm_last_av = c.n["MM"]
                            elif ki == 0:
                                emit("tensor", lambda t, j=j, ki=ki, a=a:
                                     t.matmul(P2s(a, a + 128),
                                              vx[j][:, ki * 65:(ki + 1) * 65],
                                              probsd[:, 0:128], start=True,
                                              stop=False).then_inc(c.s["MM"], 1))
                                probsd_free = ("MM", c.inc("MM"))
                                mm_last_av = c.n["MM"]
                            if ln > 128:
                                W("tensor", "AC", ac)
                                emit("tensor",
                                     lambda t, j=j, ki=ki, a=a, b=b, pb=pb, stop=stop:
                                     t.matmul(P2s(a + 128, b),
                                              vx[j][:, ki * 65:(ki + 1) * 65],
                                              probs[pb][:, 128:b - a],
                                              start=(ki == 0), stop=stop
                                              ).then_inc(c.s["MM"], 1))
                                probs_free[pb] = ("MM", c.inc("MM"))
                                mm_last_av = c.n["MM"]
                            else:
                                probs_free[pb] = ("MM", c.n["MM"])
                        else:
                            W("tensor", "AC", ac)
                            emit("tensor", lambda t, j=j, ki=ki, a=a, b=b, pb=pb, stop=stop:
                                 t.matmul(P2s(a, b),
                                          vx[j][:, ki * 65:(ki + 1) * 65],
                                          probs[pb][:, 0:b - a], start=(ki == 0),
                                          stop=stop).then_inc(c.s["MM"], 1))
                            probs_free[pb] = ("MM", c.inc("MM"))
                            mm_last_av = c.n["MM"]

                W("vector", "MM", mm_last_av)
                emit("vector", lambda v, kc2=kc2, hrow=hrow: v.tensor_copy(
                    attn_fm[hrow:hrow + 64, kc2 * SEQ:(kc2 + 1) * SEQ], P2evac()))
                ev_unit[u] = retag("vector", "EV")
                p2_free = ("EV", ev_unit[u])

            if s == 0:
                W("tensor", "LW", lw_wpj)
            for i in range(16):
                gi = s * 16 + i
                pr = i % 2
                base = pr * 2048
                if i < 2:
                    W("tensor", "EV", ev_unit[s * H + H - 1])
                else:
                    sem, val = proj_ps_free[pr]
                    W("tensor", sem, val)
                for k in range(KC):
                    for tc in range(4):
                        emit("tensor", lambda t, k=k, i=i, tc=tc, base=base:
                             t.matmul(PS[:, base + tc * 512: base + (tc + 1) * 512],
                                      attn_fm[:, k * SEQ + i * 128:k * SEQ + (i + 1) * 128],
                                      wpj[:, k * 2048 + tc * 512:k * 2048 + (tc + 1) * 512],
                                      start=(k == 0), stop=(k == KC - 1)))
                mm_p = retag("tensor", "MM")
                W("vector", "MM", mm_p)
                if gi >= 2:
                    W("vector", "SS", proj_done[gi - 2])
                emit("vector", lambda v, pr=pr, base=base: v.tensor_copy(
                    pstg[pr][:], PS[:, base:base + 2048]))
                ev_p = retag("vector", "EV")
                proj_ps_free[pr] = ("EV", ev_p)
                W("sync", "EV", ev_p)
                emit("sync", lambda sy, pr=pr, gi=gi: sy.dma_start(
                    dout[gi * 128:(gi + 1) * 128, :], pstg[pr][:]
                    ).then_inc(c.s["SS"], 16))
                proj_done.append(c.inc("SS", 16))
            evp = c.n["EV"]
            p1_free = [("EV", evp), ("EV", evp)]
            pb_free = [("EV", evp), ("EV", evp)]
            p2_free = ("EV", evp)
            proj_ps_free = [("EV", evp), ("EV", evp)]
            dms_free = [("AC", c.n["AC"]), ("AC", c.n["AC"])]
            probsd_free = ("MM", c.n["MM"])
            dclamp_free = ("AC", c.n["AC"])
            probs_free = [("MM", c.n["MM"]), ("MM", c.n["MM"])]

        if "passA" in _TAPS:
            for nm_, th_ in [("nmls", nmls), ("rowvf", rowvf), ("mx4", mx4),
                             ("scol", scol), ("nm1", nm1), ("row66", row66),
                             ("row67", row67), ("dmsb1", dmsb[1]), ("dmsb0", dmsb[0]),
                             ("trashe", trash_e)]:
                dbg_ = nc.dram_tensor("dbg_" + nm_, list(th_.shape), th_.dtype,
                                      kind="ExternalOutput").ap()
                W("gpsimd", "SS", proj_done[-1] if proj_done else 0)
                emit("gpsimd", lambda g, dbg_=dbg_, th_=th_: g.dma_start(
                    dbg_, th_[:]).then_inc(c.s["SG"], 16))
                c.inc("SG", 16)
        if "afm" in _TAPS:
            afm_dbg = nc.dram_tensor("dbg_afm", [128, KC * SEQ], BF16,
                                     kind="ExternalOutput").ap()
            W("gpsimd", "SS", proj_done[-1])
            emit("gpsimd", lambda g: g.dma_start(afm_dbg[:], attn_fm[:]
                                                 ).then_inc(c.s["SG"], 16))
            c.inc("SG", 16)

        @c.block.tensor
        def _(t):
            for f in ops["tensor"]:
                f(t)

        @c.block.vector
        def _(v):
            for f in ops["vector"]:
                f(v)

        @c.block.scalar
        def _(sc):
            for f in ops["scalar"]:
                f(sc)

        @c.block.sync
        def _(sy):
            for f in ops["sync"]:
                f(sy)

        @c.block.gpsimd
        def _(g):
            for f in ops["gpsimd"]:
                f(g)
    return proj_done



def _phase_ffn(c, tb, xn_d, xn_done, wg, wu, wd, fu, dout):
    nc = c.nc
    u_ = _uid()
    st = contextlib.ExitStack()
    st_gu = contextlib.ExitStack()
    with st:
        xnfm = st_gu.enter_context(nc.sbuf_tensor(f"ff{u_}_xnfm", [128, KC * T], BF16))
        stg = [st_gu.enter_context(nc.sbuf_tensor(f"ff{u_}_stg{j}", [128, D], BF16)) for j in (0, 1)]
        with (nc.psum_tensor(f"ff{u_}_trp0", [128, 1024], BF16) as trp0,
              nc.psum_tensor(f"ff{u_}_trp1", [128, 1024], BF16) as trp1):
            mm_fm, ev_fm = _emit_fm_rebuild(c, tb, xn_d, xn_done, xnfm, stg, [trp0, trp1])
        psA = st.enter_context(nc.psum_tensor(f"ff{u_}_psA", [128, 4096], F32))
        wt = [st_gu.enter_context(nc.sbuf_tensor(f"ff{u_}_wt{j}", [128, KC * 128], BF16)) for j in (0, 1)]
        gbuf = st_gu.enter_context(nc.sbuf_tensor(f"ff{u_}_g", [128, T], BF16))
        ubuf = st_gu.enter_context(nc.sbuf_tensor(f"ff{u_}_u", [128, T], BF16))
        fub = [st_gu.enter_context(nc.sbuf_tensor(f"ff{u_}_fu{j}", [128, T], BF16)) for j in (0, 1)]

        lw0 = c.n["LW"]
        mm0 = c.n["MM"]
        ss0 = c.n["SS"]
        evac_done = {}

        @c.block.gpsimd
        def _(g):
            for mi in range(2 * MI):
                m = mi // 2
                src = wg if mi % 2 == 0 else wu
                if mi >= 2:
                    g.wait_ge(c.s["MM"], mm0 + 2 * (mi - 1))
                g.dma_start(wt[mi % 2][:], src[m]).then_inc(c.s["LW"], 16)

        mmh = {}
        for mi in range(2 * MI):
            for half in range(2):
                mmh[(mi, half)] = mm0 + 2 * mi + half + 1
        ac0 = c.n["AC"]
        ev0 = c.n["EV"]
        nac = 0
        nev = 0
        for mi in range(2 * MI):
            for half in range(2):
                if mi % 2 == 0:
                    nac += 1
                    evac_done[(mi, half)] = ("AC", ac0 + nac)
                else:
                    nev += 1
                    evac_done[(mi, half)] = ("EV", ev0 + nev)
            if mi % 2 == 1:
                nev += 2
        ev_fuse = {}
        nev2 = 0
        for mi2 in range(MI):
            nev2 += 2 + 3
        ev_seq = ev0
        ev_up = {}
        ev_fu = {}
        for m in range(MI):
            ev_up[(m, 0)] = ev_seq + 1
            ev_up[(m, 1)] = ev_seq + 2
            ev_fu[m] = ev_seq + 3
            ev_seq += 3
        for mi in range(2 * MI):
            for half in range(2):
                if mi % 2 == 1:
                    evac_done[(mi, half)] = ("EV", ev_up[(mi // 2, half)])

        def ss_fu(m):
            return ss0 + 16 * 2 * (m + 1)

        @c.block.tensor
        def _(t):
            t.wait_ge(c.s["EV"], ev_fm)
            for mi in range(2 * MI):
                t.wait_ge(c.s["LW"], lw0 + 16 * (mi + 1))
                for half in range(2):
                    if mi > 0:
                        sem, val = evac_done[(mi - 1, half)]
                        t.wait_ge(c.s[sem], val)
                    for k in range(KC):
                        lt = wt[mi % 2][:, k * 128:(k + 1) * 128]
                        for tc in range(4):
                            col = half * 2048 + tc * 512
                            mm = t.matmul(psA[:, col:col + 512], lt,
                                          xnfm[:, k * T + col:k * T + col + 512],
                                          start=(k == 0), stop=(k == KC - 1))
                    mm.then_inc(c.s["MM"], 1)

        @c.block.scalar
        def _(s):
            for m in range(MI):
                mi = 2 * m
                for half in range(2):
                    s.wait_ge(c.s["MM"], mmh[(mi, half)])
                    if m >= 1:
                        s.wait_ge(c.s["EV"], ev_fu[m - 1])
                    sl = slice(half * 2048, (half + 1) * 2048)
                    s.activation(gbuf[:, sl], psA[:, sl], AF.Silu, bias=0.0,
                                 scale=tb["sg"][:, m:m + 1]).then_inc(c.s["AC"], 1)

        @c.block.vector
        def _(v):
            for m in range(MI):
                mi = 2 * m + 1
                for half in range(2):
                    v.wait_ge(c.s["MM"], mmh[(mi, half)])
                    if m >= 1:
                        v.wait_ge(c.s["EV"], ev_fu[m - 1])
                    sl = slice(half * 2048, (half + 1) * 2048)
                    v.tensor_scalar(ubuf[:, sl], psA[:, sl], tb["su"][:, m:m + 1],
                                    None, ALU.mult).then_inc(c.s["EV"], 1)
                v.wait_ge(c.s["AC"], ac0 + 2 * (m + 1))
                v.wait_ge(c.s["CHV"], c.n["CHV"])
                v.wait_ge(c.s["EV"], ev_up[(m, 1)])
                if m >= 2:
                    v.wait_ge(c.s["SS"], ss_fu(m - 2))
                fb = fub[m % 2]
                v.tensor_tensor(fb[:, 0:2048], gbuf[:, 0:2048], ubuf[:, 0:2048],
                                ALU.mult)
                v.tensor_tensor(fb[:, 2048:4096], gbuf[:, 2048:4096],
                                ubuf[:, 2048:4096], ALU.mult).then_inc(c.s["EV"], 1)

        @c.block.sync
        def _(sy):
            for m in range(MI):
                sy.wait_ge(c.s["EV"], ev_fu[m])
                fb = fub[m % 2]
                for part in range(2):
                    dst = bass.AP(fu.tensor,
                                  part * 16 * (128 * MI * 128) + m * 128,
                                  [[MI * 128, 128], [128 * MI * 128, 16], [1, 128]])
                    srcap = bass.AP(fb, part * 2048,
                                    [[T, 128], [128, 16], [1, 128]])
                    sy.dma_start(dst, srcap).then_inc(c.s["SS"], 16)

        c.n["LW"] = lw0 + 16 * 2 * MI
        c.n["MM"] = mm0 + 4 * MI
        c.n["AC"] = ac0 + 2 * MI
        c.n["EV"] = ev_seq
        c.n["SS"] = ss0 + 16 * 2 * MI

        c.fence()
        st_gu.close()

        wdb = st.enter_context(nc.sbuf_tensor(f"ff{u_}_wdb", [128, MI * 1024], BF16))
        fulhs = [st.enter_context(nc.sbuf_tensor(f"ff{u_}_fl{j}", [128, MI * 128], BF16)) for j in (0, 1)]
        dstage = [st.enter_context(nc.sbuf_tensor(f"ff{u_}_ds{j}", [128, 1024], F32)) for j in (0, 1)]
        sdb = st.enter_context(nc.sbuf_tensor(f"ff{u_}_sdb", [128, D], F32))
        delta_done = [0] * NTT

        lg_sd = c.inc("LG", 16)

        @c.block.gpsimd
        def _(g):
            g.dma_start(sdb[:], tb["sdownb_d"][:]).then_inc(c.s["LG"], 16)

        for oh in range(2):
            lw0 = c.n["LW"]
            mm0 = c.n["MM"]
            ev0 = c.n["EV"]
            ss0 = c.n["SS"]

            la0 = c.n["LG"]

            @c.block.gpsimd
            def _(g, oh=oh, lw0=lw0, mm0=mm0):
                if oh == 1:
                    g.wait_ge(c.s["MM"], mm0)
                for ki in range(MI):
                    g.dma_start(wdb[:, ki * 1024:(ki + 1) * 1024],
                                wd[ki * 128:(ki + 1) * 128,
                                   oh * 1024:(oh + 1) * 1024]).then_inc(c.s["LW"], 16)
                for i in range(NTT):
                    if i >= 2:
                        g.wait_ge(c.s["MM"], mm0 + i - 1)
                    g.dma_start(fulhs[i % 2][:], fu[i].rearrange("p mi c -> p (mi c)")
                                ).then_inc(c.s["LG"], 16)

            @c.block.tensor
            def _(t, oh=oh, lw0=lw0, la0=la0, ev0=ev0):
                t.wait_ge(c.s["LW"], lw0 + 16 * MI)
                for i in range(NTT):
                    t.wait_ge(c.s["LG"], la0 + 16 * (i + 1))
                    base = (i % 2) * 1024
                    if i >= 2:
                        t.wait_ge(c.s["EV"], ev0 + i - 1)
                    for ki in range(MI):
                        lt = fulhs[i % 2][:, ki * 128:(ki + 1) * 128]
                        for tc in range(2):
                            mm = t.matmul(psA[:, base + tc * 512:base + (tc + 1) * 512],
                                          lt, wdb[:, ki * 1024 + tc * 512:
                                                  ki * 1024 + (tc + 1) * 512],
                                          start=(ki == 0), stop=(ki == MI - 1))
                    mm.then_inc(c.s["MM"], 1)

            @c.block.vector
            def _(v, oh=oh, mm0=mm0, ev0=ev0, ss0=ss0):
                if oh == 0:
                    v.wait_ge(c.s["LG"], lg_sd)
                for i in range(NTT):
                    v.wait_ge(c.s["MM"], mm0 + i + 1)
                    if i >= 2:
                        v.wait_ge(c.s["SS"], ss0 + 16 * (i - 1))
                    base = (i % 2) * 1024
                    v.tensor_tensor(dstage[i % 2][:], psA[:, base:base + 1024],
                                    sdb[:, oh * 1024:(oh + 1) * 1024], ALU.mult
                                    ).then_inc(c.s["EV"], 1)

            @c.block.sync
            def _(sy, oh=oh, ev0=ev0):
                for i in range(NTT):
                    sy.wait_ge(c.s["EV"], ev0 + i + 1)
                    sy.dma_start(dout[i * 128:(i + 1) * 128,
                                      oh * 1024:(oh + 1) * 1024],
                                 dstage[i % 2][:]).then_inc(c.s["SS"], 16)
                    if oh == 1:
                        delta_done[i] = c.n["SS"] + 16 * (i + 1)

            c.n["LW"] = lw0 + 16 * MI
            c.n["LG"] = la0 + 16 * NTT
            c.n["MM"] = mm0 + NTT
            c.n["EV"] = ev0 + NTT
            c.n["SS"] = ss0 + 16 * NTT
        c.fence()
    return delta_done



def _phase_ssm(c, tb, xn_d, xn_done, L):
    nc = c.nc
    u_ = _uid()
    wz, wbc, wdt = L["wz"], L["wbc"], L["wdt"]
    zt, xst, bfm, cfm, btm = L["zt"], L["xst"], L["bfm"], L["cfm"], L["btm"]
    yt_d, y2_d, o1, dout = L["yt_d"], L["y2_d"], L["o1"], L["dout"]
    wout, wo = L["wout"], L["wo"]

    ops = {k: [] for k in ("tensor", "vector", "scalar", "sync", "gpsimd")}

    def emit(eng, fn):
        ops[eng].append(fn)

    def retag(eng, sem):
        last = ops[eng].pop()
        emit(eng, lambda e, last=last: last(e).then_inc(c.s[sem], 1))
        return c.inc(sem)

    def W(eng, sem, val=None):
        if val is None:
            sem, val = sem
        if val > 0:
            emit(eng, lambda e, s=sem, v=val: e.wait_ge(c.s[s], v))

    _CM = {"tensor": "CHT", "vector": "CHV", "scalar": "CHS", "gpsimd": "CHP"}

    def chain(eng):
        val = retag(eng, _CM[eng])
        W(eng, _CM[eng], val)

    def flush():
        for eng in ("tensor", "vector", "scalar", "sync", "gpsimd"):
            sec = ops[eng]
            if sec:
                @getattr(c.block, eng)
                def _(e, sec=sec):
                    for f in sec:
                        f(e)
            ops[eng] = []

    st = contextlib.ExitStack()
    st_ab = contextlib.ExitStack()
    with st:
        segL = st_ab.enter_context(nc.sbuf_tensor(f"sm{u_}_segL", [33, T], BF16))
        dtfm = st_ab.enter_context(nc.sbuf_tensor(f"sm{u_}_dtfm", [SH, T], BF16))
        dtdec = st_ab.enter_context(nc.sbuf_tensor(f"sm{u_}_dtdec", [SH, T], BF16))
        expac = st_ab.enter_context(nc.sbuf_tensor(f"sm{u_}_expac", [SH, T], BF16))
        acsb = st_ab.enter_context(nc.sbuf_tensor(f"sm{u_}_acsb", [SH, T], BF16))
        onesb = st_ab.enter_context(nc.sbuf_tensor(f"sm{u_}_ones", [1, 128], BF16))

        emit("vector", lambda v: v.memset(onesb[:], 1.0))
        retag("vector", "EV")
        emit("vector", lambda v: v.memset(segL[32:33, :], 1.0))
        retag("vector", "EV")

        st_a = contextlib.ExitStack()
        st_a1 = contextlib.ExitStack()
        trpp = [st_a.enter_context(nc.psum_tensor(f"sm{u_}_trq{j}", [128, 1024], BF16)) for j in (0, 1)]
        psA = st_a.enter_context(nc.psum_tensor(f"sm{u_}_psA", [128, 2048], F32))
        acc1 = st_a.enter_context(nc.sbuf_tensor(f"sm{u_}_acc1", [128, SEQ], BF16))
        acc2 = st_a.enter_context(nc.sbuf_tensor(f"sm{u_}_acc2", [128, SEQ], BF16))
        xnfm = st_a1.enter_context(nc.sbuf_tensor(f"sm{u_}_xnfm", [128, KC * T], BF16))
        flush()
        with (nc.sbuf_tensor(f"sm{u_}_stg0", [128, D], BF16) as stg0,
              nc.sbuf_tensor(f"sm{u_}_stg1", [128, D], BF16) as stg1,
              nc.psum_tensor(f"sm{u_}_trp0", [128, 1024], BF16) as trp0,
              nc.psum_tensor(f"sm{u_}_trp1", [128, 1024], BF16) as trp1):
            mm_fm, ev_fm = _emit_fm_rebuild(c, tb, xn_d, xn_done, xnfm, [stg0, stg1],
                                            [trp0, trp1])
        wt = [st_a1.enter_context(nc.sbuf_tensor(f"sm{u_}_wt{j}", [128, KC * 128], BF16)) for j in (0, 1)]
        ztp0 = st_a1.enter_context(nc.sbuf_tensor(f"sm{u_}_ztp0", [128, 16 * 128], BF16))
        ztp = [ztp0, ztp0]
        cvin0 = st_a1.enter_context(nc.sbuf_tensor(f"sm{u_}_cvin0", [128, 2051], BF16))
        cvin = [cvin0, cvin0]
        cvout = st_a1.enter_context(nc.sbuf_tensor(f"sm{u_}_cvo", [128, SEQ], BF16))

        emit("vector", lambda v: v.memset(cvin[0][:, 0:3], 0.0))
        retag("vector", "EV")
        W("tensor", "EV", c.n["EV"])

        NM = 16 + CCC + 1
        NM_RUN = NM if _SSM_NM is None else _SSM_NM
        pa_free = ("EV", 0)
        ztp_free = [("SS", 0)]
        cvin_free = [("EV", 0)]
        trpq_free = [("EV", 0), ("EV", 0)]
        tq_idx = [0]
        cvout_free = ("MM", 0)
        acc_free = ("AC", 0)
        mm_mat = {}

        def emit_transposes_and_scatter(srcbuf, buf, dst_tensor, dst_off,
                                        dst_pstride, deps):
            buf = 0
            sem2, val2 = ztp_free[buf]
            W("tensor", sem2, val2)
            for dep in deps:
                W("tensor", dep)
            mm_t = 0
            for ti in range(16):
                tq = tq_idx[0] % 2
                tq_idx[0] += 1
                W("tensor", trpq_free[tq])
                emit("tensor", lambda t, ti=ti, tq=tq, srcbuf=srcbuf:
                     t.transpose(trpp[tq][:, 0:128],
                                 srcbuf[:, ti * 128:(ti + 1) * 128],
                                 tb["eye128b"][:]))
                mm_t = retag("tensor", "MM")
                W("vector", "MM", mm_t)
                emit("vector", lambda v, ti=ti, tq=tq, buf=buf:
                     v.tensor_copy(ztp[buf][:, ti * 128:(ti + 1) * 128],
                                   trpp[tq][:, 0:128]))
                ev_t = retag("vector", "EV")
                trpq_free[tq] = ("EV", ev_t)
            W("sync", "EV", c.n["EV"])
            emit("sync", lambda sy, dst_tensor=dst_tensor, dst_off=dst_off,
                 dst_pstride=dst_pstride, buf=buf: sy.dma_start(
                     bass.AP(dst_tensor, dst_off,
                             [[dst_pstride, 128], [128 * dst_pstride, 16], [1, 128]]),
                     bass.AP(ztp[buf], 0, [[16 * 128, 128], [128, 16], [1, 128]])
                     ).then_inc(c.s["SS"], 16))
            ztp_free[buf] = ("SS", c.inc("SS", 16))
            return mm_t

        W("gpsimd", "EV", ev_fm)
        for m in range(NM_RUN):
            is_dt = (m == NM - 1)
            if m >= 2:
                W("gpsimd", "MM", mm_mat[(m - 2, 1)])
            if m < 16:
                emit("gpsimd", lambda g, m=m: g.dma_start(wt[m % 2][:], wz[m]
                                                          ).then_inc(c.s["LW"], 16))
            elif not is_dt:
                emit("gpsimd", lambda g, m=m: g.dma_start(wt[m % 2][:], wbc[m - 16]
                                                          ).then_inc(c.s["LW"], 16))
            else:
                emit("gpsimd", lambda g: g.dma_start(
                    wt[m % 2][:, 0:KC * SH],
                    wdt.rearrange("p k s -> p (k s)")).then_inc(c.s["LW"], 16))
            lw_m = c.inc("LW", 16)
            W("tensor", "LW", lw_m)

            for half in range(2):
                s_ = half
                W("tensor", pa_free)
                for k in range(KC):
                    for tc in range(4):
                        col = half * 2048 + tc * 512
                        if is_dt:
                            emit("tensor", lambda t, m=m, k=k, col=col, tc=tc:
                                 t.matmul(psA[0:SH, tc * 512:(tc + 1) * 512],
                                          wt[m % 2][:, k * SH:(k + 1) * SH],
                                          xnfm[:, k * T + col:k * T + col + 512],
                                          start=(k == 0), stop=(k == KC - 1)))
                        else:
                            emit("tensor", lambda t, m=m, k=k, col=col, tc=tc:
                                 t.matmul(psA[:, tc * 512:(tc + 1) * 512],
                                          wt[m % 2][:, k * 128:(k + 1) * 128],
                                          xnfm[:, k * T + col:k * T + col + 512],
                                          start=(k == 0), stop=(k == KC - 1)))
                mm_v = retag("tensor", "MM")
                mm_mat[(m, half)] = mm_v

                if m < 16:
                    W("vector", "MM", mm_v)
                    W("vector", cvout_free)
                    emit("vector", lambda v: v.tensor_copy(cvout[:], psA[:]))
                    ev_z = retag("vector", "EV")
                    pa_free = ("EV", ev_z)
                    mm_t = emit_transposes_and_scatter(
                        cvout, half, zt.tensor, s_ * SEQ * D + m * 128, D,
                        [("EV", ev_z)])
                    cvout_free = ("MM", mm_t)
                    mm_mat[(m, half)] = mm_t
                elif not is_dt:
                    cc = m - 16
                    jj = 0
                    W("vector", "MM", mm_v)
                    W("vector", cvin_free[jj])
                    emit("vector", lambda v, jj=jj: v.tensor_copy(
                        cvin[jj][:, 3:2051], psA[:]))
                    ev_c = retag("vector", "EV")
                    pa_free = ("EV", ev_c)
                    W("vector", "EV", ev_c)
                    W("vector", acc_free)
                    emit("vector", lambda v, jj=jj, cc=cc: v.tensor_scalar(
                        acc1[:], cvin[jj][:, 0:2048],
                        tb["convw"][:, cc * 4:cc * 4 + 1], None, ALU.mult))
                    chain("vector")
                    emit("vector", lambda v, jj=jj, cc=cc: v.scalar_tensor_tensor(
                        acc2[:], cvin[jj][:, 1:2049],
                        tb["convw"][:, cc * 4 + 1:cc * 4 + 2], acc1[:],
                        ALU.mult, ALU.add))
                    chain("vector")
                    emit("vector", lambda v, jj=jj, cc=cc: v.scalar_tensor_tensor(
                        acc1[:], cvin[jj][:, 2:2050],
                        tb["convw"][:, cc * 4 + 2:cc * 4 + 3], acc2[:],
                        ALU.mult, ALU.add))
                    chain("vector")
                    emit("vector", lambda v, jj=jj, cc=cc: v.scalar_tensor_tensor(
                        acc2[:], cvin[jj][:, 3:2051],
                        tb["convw"][:, cc * 4 + 3:cc * 4 + 4], acc1[:],
                        ALU.mult, ALU.add))
                    ev_tap = retag("vector", "EV")
                    cvin_free[jj] = ("EV", ev_tap)
                    W("scalar", "EV", ev_tap)
                    W("scalar", cvout_free)
                    emit("scalar", lambda s, cc=cc: s.activation(
                        cvout[:], acc2[:], AF.Silu,
                        bias=tb["convb"][:, cc:cc + 1], scale=1.0))
                    ac_cv = retag("scalar", "AC")
                    acc_free = ("AC", ac_cv)
                    if cc < 16:
                        mm_t = emit_transposes_and_scatter(
                            cvout, half, xst.tensor, s_ * SEQ * D + cc * 128, D,
                            [("AC", ac_cv)])
                        mm_mat[(m, half)] = mm_t
                        cvout_free = ("MM", mm_t)
                    else:
                        dstt = bfm if cc < 18 else cfm
                        nh = cc - (16 if cc < 18 else 18)
                        W("sync", "AC", ac_cv)
                        emit("sync", lambda sy, dstt=dstt, s_=s_, nh=nh: sy.dma_start(
                            dstt[s_][nh * 128:(nh + 1) * 128, :], cvout[:]
                            ).then_inc(c.s["SS"], 16))
                        ss_fm = c.inc("SS", 16)
                        cvout_free = ("SS", ss_fm)
                        if cc < 18:
                            mm_t = emit_transposes_and_scatter(
                                cvout, half, btm.tensor, s_ * SEQ * SN + nh * 128,
                                SN, [("AC", ac_cv)])
                            mm_mat[(m, half)] = mm_t
                            cvout_free = ("MM", mm_t)
                else:
                    W("vector", "MM", mm_v)
                    W("vector", acc_free)
                    dst_ = acc1 if half == 0 else acc2
                    emit("vector", lambda v, dst_=dst_: v.tensor_copy(
                        dst_[0:SH, :], psA[0:SH, :]))
                    ev_d = retag("vector", "EV")
                    pa_free = ("EV", ev_d)

        flush()
        c.fence()
        st_a1.close()
        if _SSM_NM is not None and _SSM_NM < NM:
            st_a.close()
            c.fence()
            st_ab.close()
            return None
        with (nc.sbuf_tensor(f"sm{u_}_t1", [SH, T], F32) as t1f,
              nc.sbuf_tensor(f"sm{u_}_u1", [SH, T], F32) as u1f,
              nc.sbuf_tensor(f"sm{u_}_u2", [SH, T], F32) as u2f,
              nc.sbuf_tensor(f"sm{u_}_acs", [SH, T], F32) as acsf,
              nc.sbuf_tensor(f"sm{u_}_dtm", [SH, T], F32) as dtmf):
            W("vector", "EV", c.n["EV"])
            emit("vector", lambda v: v.tensor_scalar(
                t1f[:, 0:2048], acc1[0:SH, :], tb["dtb"][:], None, ALU.add))
            chain("vector")
            emit("vector", lambda v: v.tensor_scalar(
                t1f[:, 2048:4096], acc2[0:SH, :], tb["dtb"][:], None, ALU.add))
            ev_t1 = retag("vector", "EV")
            W("scalar", "EV", ev_t1)
            emit("scalar", lambda s: s.activation(u1f[:], t1f[:], AF.Abs,
                                                  bias=0.0, scale=1.0))
            chain("scalar")
            emit("scalar", lambda s: s.activation(u2f[:], u1f[:], AF.Exp,
                                                  bias=0.0, scale=tb["misc"][0:SH, 3:4]))
            chain("scalar")
            emit("scalar", lambda s: s.activation(u1f[:], u2f[:], AF.Ln,
                                                  bias=1.0, scale=1.0))
            ac_sp = retag("scalar", "AC")
            emit("vector", lambda v: v.tensor_scalar(dtmf[:], t1f[:], 0.0, None,
                                                     ALU.max))
            chain("vector")
            W("vector", "AC", ac_sp)
            emit("vector", lambda v: v.tensor_tensor(t1f[:], dtmf[:], u1f[:],
                                                     ALU.add))
            chain("vector")
            emit("vector", lambda v: v.tensor_copy(dtfm[:], t1f[:]))
            retag("vector", "EV")
            emit("vector", lambda v: v.tensor_scalar(dtmf[:], t1f[:],
                                                     tb["anegcol"][:], None, ALU.mult))
            chain("vector")
            for s_ in range(SPC):
                for cc in range(NCHUNK):
                    off = s_ * SEQ + cc * CH
                    emit("vector", lambda v, off=off: v.tensor_tensor_scan(
                        acsf[:, off:off + CH], dtmf[:, off:off + CH],
                        dtmf[:, off:off + CH], 0.0, ALU.add, ALU.bypass))
            chain("vector")
            emit("vector", lambda v: v.tensor_scalar(segL[0:SH, :], acsf[:], -1.0,
                                                     None, ALU.mult))
            retag("vector", "EV")
            emit("vector", lambda v: v.tensor_copy(acsb[:], acsf[:]))
            ev_acsb = retag("vector", "EV")
            W("scalar", "EV", ev_acsb)
            emit("scalar", lambda s: s.activation(expac[:], acsf[:], AF.Exp,
                                                  bias=0.0, scale=1.0))
            retag("scalar", "AC")
            for s_ in range(SPC):
                for cc in range(NCHUNK):
                    off = s_ * SEQ + cc * CH
                    emit("vector", lambda v, off=off: v.tensor_scalar(
                        dtmf[:, off:off + CH], acsf[:, off:off + CH],
                        acsf[:, off + CH - 1:off + CH], None, ALU.subtract))
            chain("vector")
            W("scalar", "CHV", c.n["CHV"])
            emit("scalar", lambda s: s.activation(u1f[:], dtmf[:], AF.Exp,
                                                  bias=0.0, scale=tb["misc"][0:SH, 3:4]))
            ac_dec = retag("scalar", "AC")
            W("vector", "AC", ac_dec)
            emit("vector", lambda v: v.tensor_tensor(dtdec[:], dtfm[:], u1f[:],
                                                     ALU.mult))
            retag("vector", "EV")
            flush()
        st_a.close()
        c.fence()
        if _SSM_STOP == "A":
            st_ab.close()
            return None

        st_b = contextlib.ExitStack()
        xs_sb = [st_b.enter_context(nc.sbuf_tensor(f"sm{u_}_xs{j}", [64, D], BF16)) for j in (0, 1)]
        Bfm = [st_b.enter_context(nc.sbuf_tensor(f"sm{u_}_Bf{j}", [128, 128], BF16)) for j in (0, 1)]
        Cfm = [st_b.enter_context(nc.sbuf_tensor(f"sm{u_}_Cf{j}", [128, 128], BF16)) for j in (0, 1)]
        Btm = [st_b.enter_context(nc.sbuf_tensor(f"sm{u_}_Bt{j}", [64, SN], BF16)) for j in (0, 1)]
        segR = [st_b.enter_context(nc.sbuf_tensor(f"sm{u_}_sgR{j}", [33, SH * CH], BF16)) for j in (0, 1)]
        Sst = [st_b.enter_context(nc.sbuf_tensor(f"sm{u_}_S{j}", [128, 2 * 2048], F32)) for j in (0, 1)]
        Sb = [st_b.enter_context(nc.sbuf_tensor(f"sm{u_}_Sb{j}", [128, 2 * 2048], BF16)) for j in (0, 1)]
        Stmp = st_b.enter_context(nc.sbuf_tensor(f"sm{u_}_Stmp", [128, 2048], F32))
        decbc = [st_b.enter_context(nc.sbuf_tensor(f"sm{u_}_dbc{j}", [128, 2048], BF16)) for j in (0, 1)]
        CBm = st_b.enter_context(nc.sbuf_tensor(f"sm{u_}_CBm", [64, 64], BF16))
        segm = st_b.enter_context(nc.sbuf_tensor(f"sm{u_}_segm", [64, SH * CH], BF16))
        expsg = st_b.enter_context(nc.sbuf_tensor(f"sm{u_}_expsg", [64, SH * CH], BF16))
        e1 = st_b.enter_context(nc.sbuf_tensor(f"sm{u_}_e1", [64, SH * CH], BF16))
        e2 = st_b.enter_context(nc.sbuf_tensor(f"sm{u_}_e2", [64, SH * CH], BF16))
        e3 = st_b.enter_context(nc.sbuf_tensor(f"sm{u_}_e3", [64, SH * CH], BF16))
        xtd = st_b.enter_context(nc.sbuf_tensor(f"sm{u_}_xtd", [64, SH * CH], BF16))
        f3sb = st_b.enter_context(nc.sbuf_tensor(f"sm{u_}_f3", [64, SH * CH], BF16))
        yoffs = st_b.enter_context(nc.sbuf_tensor(f"sm{u_}_yo", [64, SH * CH], BF16))
        ysb = [st_b.enter_context(nc.sbuf_tensor(f"sm{u_}_y{j}", [64, SH * CH], BF16)) for j in (0, 1)]
        dcrow = st_b.enter_context(nc.sbuf_tensor(f"sm{u_}_dcr", [1, 2048], BF16))
        psB = st_b.enter_context(nc.psum_tensor(f"sm{u_}_psB", [128, 4096], F32))

        for jj in (0, 1):
            emit("sync", lambda sy, jj=jj: sy.dma_start(
                segR[jj][0:32, :], tb["eh32_d"][0:32, :]).then_inc(c.s["LA"], 16))
            c.inc("LA", 16)

        in_free = {"xs": [("MM", 0), ("MM", 0)], "Bfm": [("MM", 0), ("MM", 0)],
                   "Cfm": [("MM", 0), ("MM", 0)], "Btm": [("MM", 0), ("MM", 0)],
                   "segR": [("MM", 0), ("MM", 0)]}
        ysb_free = [("SS", 0), ("SS", 0)]
        pa_free = ("EV", 0)
        pb_free = ("EV", 0)
        s_upd = ("EV", 0)

        for s_ in range(SPC):
            W("vector", "PL", c.n["PL"])
            W("vector", "MM", c.n["MM"])
            emit("vector", lambda v: v.memset(Sst[0][:], 0.0))
            emit("vector", lambda v: v.memset(Sb[0][:], 0.0))
            emit("vector", lambda v: v.memset(decbc[0][:], 0.0))
            retag("vector", "EV")
            ev_init = c.n["EV"]
            s_upd = ("EV", ev_init)
            pa_free = ("EV", ev_init)
            W("gpsimd", "EV", ev_init)

            for cc in range(NCHUNK):
                jj = cc % 2
                tok0 = s_ * SEQ + cc * CH
                W("sync", in_free["xs"][jj])
                emit("sync", lambda sy, jj=jj, tok0=tok0: sy.dma_start(
                    xs_sb[jj][:], xst[tok0:tok0 + CH, :]).then_inc(c.s["LA"], 16))
                c.inc("LA", 16)
                W("sync", in_free["Bfm"][jj])
                for nh in range(2):
                    emit("sync", lambda sy, jj=jj, nh=nh, s_=s_, cc=cc: sy.dma_start(
                        Bfm[jj][:, nh * 64:(nh + 1) * 64],
                        bfm[s_][nh * 128:(nh + 1) * 128, cc * CH:(cc + 1) * CH]
                        ).then_inc(c.s["LA"], 16))
                    c.inc("LA", 16)
                W("sync", in_free["Cfm"][jj])
                for nh in range(2):
                    emit("sync", lambda sy, jj=jj, nh=nh, s_=s_, cc=cc: sy.dma_start(
                        Cfm[jj][:, nh * 64:(nh + 1) * 64],
                        cfm[s_][nh * 128:(nh + 1) * 128, cc * CH:(cc + 1) * CH]
                        ).then_inc(c.s["LA"], 16))
                    c.inc("LA", 16)
                W("sync", in_free["Btm"][jj])
                emit("sync", lambda sy, jj=jj, tok0=tok0: sy.dma_start(
                    Btm[jj][:], bass.AP(btm.tensor, tok0 * SN, [[SN, CH], [1, SN]])
                    ).then_inc(c.s["LA"], 16))
                la_in = c.inc("LA", 16)
                W("gpsimd", in_free["segR"][jj])
                emit("gpsimd", lambda g, jj=jj, tok0=tok0: g.dma_start(
                    segR[jj][32:33, :],
                    bass.AP(acsb, tok0, [[T, 32], [1, CH]])).then_inc(c.s["LG"], 16))
                lg_sr = c.inc("LG", 16)

                W("tensor", "LA", la_in)
                W("tensor", "LG", lg_sr)
                W("tensor", pa_free)
                for nh in range(2):
                    emit("tensor", lambda t, jj=jj, nh=nh: t.matmul(
                        psB[0:64, 0:64], Cfm[jj][:, nh * 64:(nh + 1) * 64],
                        Bfm[jj][:, nh * 64:(nh + 1) * 64],
                        start=(nh == 0), stop=(nh == 1)))
                mm_cb = retag("tensor", "MM")
                W("vector", "MM", mm_cb)
                emit("vector", lambda v: v.tensor_tensor(CBm[:], psB[0:64, 0:64],
                                                         tb["lt64"][:], ALU.mult))
                ev_cbm = retag("vector", "EV")
                W("tensor", "EV", ev_cbm)
                for tc in range(4):
                    emit("tensor", lambda t, jj=jj, tok0=tok0, tc=tc: t.matmul(
                        psB[0:64, tc * 512:(tc + 1) * 512],
                        segL[:, tok0:tok0 + CH],
                        segR[jj][:, tc * 512:(tc + 1) * 512],
                        start=True, stop=True))
                mm_seg = retag("tensor", "MM")
                in_free["segR"][jj] = ("MM", mm_seg)
                in_free["Bfm"][jj] = ("MM", mm_seg)
                W("vector", "MM", mm_seg)
                emit("vector", lambda v: v.tensor_scalar(segm[:], psB[0:64, 0:2048],
                                                         0.0, None, ALU.min))
                ev_sgm = retag("vector", "EV")
                W("scalar", "EV", ev_sgm)
                emit("scalar", lambda s: s.activation(expsg[:], segm[:], AF.Exp,
                                                      bias=0.0, scale=1.0))
                ac_es = retag("scalar", "AC")
                W("tensor", "EV", ev_sgm)
                for tc in range(4):
                    emit("tensor", lambda t, tc=tc: t.matmul(
                        psB[0:64, tc * 512:(tc + 1) * 512], CBm[:],
                        tb["eyetb"][:, tc * 512:(tc + 1) * 512],
                        start=True, stop=True))
                mm_cbt = retag("tensor", "MM")
                W("vector", "MM", mm_cbt)
                W("vector", "AC", ac_es)
                emit("vector", lambda v: v.tensor_tensor(e1[:], expsg[:],
                                                         psB[0:64, 0:2048], ALU.mult))
                ev_e1 = retag("vector", "EV")
                W("tensor", "EV", ev_e1)
                for tc in range(4):
                    emit("tensor", lambda t, jj=jj, tok0=tok0, tc=tc: t.matmul(
                        psB[0:64, tc * 512:(tc + 1) * 512],
                        dtfm[:, tok0:tok0 + CH],
                        segR[jj][0:32, tc * 512:(tc + 1) * 512],
                        start=True, stop=True))
                mm_f1 = retag("tensor", "MM")
                W("vector", "MM", mm_f1)
                emit("vector", lambda v: v.tensor_tensor(e2[:], e1[:],
                                                         psB[0:64, 0:2048], ALU.mult))
                ev_e2 = retag("vector", "EV")
                W("vector", "EV", ev_e2)
                emit("vector", lambda v: v.tensor_tensor(e3[:], e2[:],
                                                         tb["dskd"][:], ALU.add))
                ev_e3 = retag("vector", "EV")
                W("tensor", "EV", ev_e2)
                for tc in range(4):
                    emit("tensor", lambda t, jj=jj, tok0=tok0, tc=tc: t.matmul(
                        psB[0:64, tc * 512:(tc + 1) * 512],
                        dtdec[:, tok0:tok0 + CH],
                        segR[jj][0:32, tc * 512:(tc + 1) * 512],
                        start=True, stop=True))
                mm_f2 = retag("tensor", "MM")
                W("vector", "MM", mm_f2)
                W("vector", "LA", la_in)
                emit("vector", lambda v, jj=jj: v.tensor_tensor(
                    xtd[:], xs_sb[jj][:], psB[0:64, 0:2048], ALU.mult))
                ev_xtd = retag("vector", "EV")
                W("tensor", "EV", ev_xtd)
                for tc in range(4):
                    emit("tensor", lambda t, jj=jj, tok0=tok0, tc=tc: t.matmul(
                        psB[0:64, tc * 512:(tc + 1) * 512],
                        expac[:, tok0:tok0 + CH],
                        segR[jj][0:32, tc * 512:(tc + 1) * 512],
                        start=True, stop=True))
                mm_f3 = retag("tensor", "MM")
                W("scalar", "MM", mm_f3)
                emit("scalar", lambda s: s.activation(f3sb[:], psB[0:64, 0:2048],
                                                      AF.Copy, bias=0.0, scale=1.0))
                ac_f3 = retag("scalar", "AC")
                W("tensor", "EV", ev_e3)
                W("tensor", pb_free)
                for h in range(SH):
                    emit("tensor", lambda t, jj=jj, h=h: t.matmul(
                        psB[0:64, 2048 + h * 64:2048 + (h + 1) * 64],
                        e3[:, h * 64:(h + 1) * 64],
                        xs_sb[jj][:, h * 64:(h + 1) * 64],
                        start=True, stop=True))
                mm_yd = retag("tensor", "MM")
                in_free["xs"][jj] = ("MM", mm_yd)
                W("tensor", "AC", ac_f3)
                W("tensor", s_upd)
                for nh in range(2):
                    for tc in range(4):
                        emit("tensor", lambda t, jj=jj, nh=nh, tc=tc, cc=cc: t.matmul(
                            psB[0:64, tc * 512:(tc + 1) * 512],
                            Cfm[jj][:, nh * 64:(nh + 1) * 64],
                            Sb[cc % 2][:, nh * 2048 + tc * 512:
                                       nh * 2048 + (tc + 1) * 512],
                            start=(nh == 0), stop=(nh == 1)))
                mm_yo = retag("tensor", "MM")
                in_free["Cfm"][jj] = ("MM", mm_yo)
                W("vector", "MM", mm_yo)
                emit("vector", lambda v: v.tensor_tensor(yoffs[:], psB[0:64, 0:2048],
                                                         f3sb[:], ALU.mult))
                ev_yo = retag("vector", "EV")
                W("vector", "MM", mm_yd)
                W("vector", "EV", ev_yo)
                W("vector", ysb_free[jj])
                emit("vector", lambda v, jj=jj: v.tensor_tensor(
                    ysb[jj][:], psB[0:64, 2048:4096], yoffs[:], ALU.add))
                ev_y = retag("vector", "EV")
                pb_free = ("EV", ev_y)
                W("sync", "EV", ev_y)
                emit("sync", lambda sy, jj=jj, tok0=tok0: sy.dma_start(
                    yt_d[tok0:tok0 + CH, :], ysb[jj][:]).then_inc(c.s["SS"], 16))
                ysb_free[jj] = ("SS", c.inc("SS", 16))
                last_upd = 0
                for nh in range(2):
                    W("tensor", "EV", ev_yo if nh == 0 else last_upd)
                    for tc in range(4):
                        emit("tensor", lambda t, jj=jj, nh=nh, tc=tc: t.matmul(
                            psB[:, tc * 512:(tc + 1) * 512],
                            Btm[jj][:, nh * 128:(nh + 1) * 128],
                            xtd[:, tc * 512:(tc + 1) * 512],
                            start=True, stop=True))
                    mm_st = retag("tensor", "MM")
                    if nh == 1:
                        in_free["Btm"][jj] = ("MM", mm_st)
                    W("gpsimd", "EV", c.n["EV"])
                    emit("gpsimd", lambda g, cc=cc, nh=nh: g.tensor_tensor(
                        Stmp[:], Sst[cc % 2][:, nh * 2048:(nh + 1) * 2048],
                        decbc[cc % 2][:], ALU.mult))
                    pl_v = retag("gpsimd", "PL")
                    W("vector", "MM", mm_st)
                    W("vector", "PL", pl_v)
                    emit("vector", lambda v, cc=cc, nh=nh: v.tensor_tensor(
                        Sst[(cc + 1) % 2][:, nh * 2048:(nh + 1) * 2048],
                        Stmp[:], psB[:, 0:2048], ALU.add))
                    chain("vector")
                    emit("vector", lambda v, cc=cc, nh=nh: v.tensor_copy(
                        Sb[(cc + 1) % 2][:, nh * 2048:(nh + 1) * 2048],
                        Sst[(cc + 1) % 2][:, nh * 2048:(nh + 1) * 2048]))
                    last_upd = retag("vector", "EV")
                s_upd = ("EV", last_upd)
                pa_free = ("EV", last_upd)
                if cc < NCHUNK - 1:
                    W("tensor", pa_free)
                    col = s_ * SEQ + (cc + 1) * CH + CH - 1
                    for tc in range(4):
                        emit("tensor", lambda t, col=col, jj=jj, tc=tc: t.matmul(
                            psB[0:1, tc * 512:(tc + 1) * 512],
                            expac[:, col:col + 1],
                            segR[jj][0:32, tc * 512:(tc + 1) * 512],
                            start=True, stop=True))
                    mm_dr = retag("tensor", "MM")
                    W("scalar", "MM", mm_dr)
                    emit("scalar", lambda s: s.activation(dcrow[:], psB[0:1, 0:2048],
                                                          AF.Copy, bias=0.0, scale=1.0))
                    ac_dr = retag("scalar", "AC")
                    W("tensor", "AC", ac_dr)
                    for tc in range(4):
                        emit("tensor", lambda t, tc=tc: t.matmul(
                            psB[:, tc * 512:(tc + 1) * 512], onesb[:],
                            dcrow[:, tc * 512:(tc + 1) * 512],
                            start=True, stop=True))
                    mm_db = retag("tensor", "MM")
                    W("vector", "MM", mm_db)
                    W("vector", "PL", c.n["PL"])
                    emit("vector", lambda v, cc=cc: v.tensor_copy(
                        decbc[(cc + 1) % 2][:], psB[:, 0:2048]))
                    ev_db = retag("vector", "EV")
                    pa_free = ("EV", ev_db)
        flush()
        st_b.close()
        st_ab.close()
        c.fence()
        if _SSM_STOP == "B":
            return None

        st_c = contextlib.ExitStack()
        ya = [st_c.enter_context(nc.sbuf_tensor(f"sm{u_}_ya{j}", [128, D], BF16)) for j in (0, 1)]
        za = [st_c.enter_context(nc.sbuf_tensor(f"sm{u_}_za{j}", [128, D], BF16)) for j in (0, 1)]
        zs = st_c.enter_context(nc.sbuf_tensor(f"sm{u_}_zs", [128, D], BF16))
        yg = [st_c.enter_context(nc.sbuf_tensor(f"sm{u_}_yg{j}", [128, D], BF16)) for j in (0, 1)]
        y2s = [st_c.enter_context(nc.sbuf_tensor(f"sm{u_}_y2{j}", [128, D], BF16)) for j in (0, 1)]
        gtrash = st_c.enter_context(nc.sbuf_tensor(f"sm{u_}_gt", [128, D], BF16))
        gcols = [st_c.enter_context(nc.sbuf_tensor(f"sm{u_}_gc{j}", [128, 4], F32)) for j in (0, 1)]
        gnw = st_c.enter_context(nc.sbuf_tensor(f"sm{u_}_gnw", [128, D], F32))

        emit("gpsimd", lambda g: g.dma_start(gnw[:], tb["gnormwb_d"][:]
                                             ).then_inc(c.s["LG"], 16))
        lg_gn = c.inc("LG", 16)
        W("vector", "LG", lg_gn)
        ev0g = c.n["EV"]
        ac0g = c.n["AC"]
        ss0g = c.n["SS"]

        def ev_yg(i):
            return ev0g + 2 * i + 1

        def ev_y2(i):
            return ev0g + 2 * i + 2

        def ac_z(i):
            return ac0g + 2 * i + 1

        def ac_r(i):
            return ac0g + 2 * i + 2

        y2_done = []
        for i in range(NTT):
            jj = i % 2
            if i >= 2:
                W("sync", "EV", ev_y2(i - 2))
            emit("sync", lambda sy, jj=jj, i=i: sy.dma_start(
                ya[jj][:], yt_d[i * 128:(i + 1) * 128, :]).then_inc(c.s["LA"], 16))
            c.inc("LA", 16)
            emit("sync", lambda sy, jj=jj, i=i: sy.dma_start(
                za[jj][:], zt[i * 128:(i + 1) * 128, :]).then_inc(c.s["LA"], 16))
            la_i = c.inc("LA", 16)
            W("scalar", "LA", la_i)
            if i >= 1:
                W("scalar", "EV", ev_yg(i - 1))
            emit("scalar", lambda s, jj=jj: s.activation(zs[:], za[jj][:], AF.Silu,
                                                         bias=0.0, scale=1.0))
            assert retag("scalar", "AC") == ac_z(i)
            W("vector", "AC", ac_z(i))
            emit("vector", lambda v, jj=jj: v.tensor_tensor(yg[jj][:], ya[jj][:],
                                                            zs[:], ALU.mult))
            assert retag("vector", "EV") == ev_yg(i)
            W("scalar", "EV", ev_yg(i))
            emit("scalar", lambda s, jj=jj: s.activation(
                gtrash[:], yg[jj][:], AF.Square, bias=0.0, scale=1.0,
                accum_out=gcols[jj][:, 0:1]))
            chain("scalar")
            emit("scalar", lambda s, jj=jj: s.activation(
                gcols[jj][:, 1:2], gcols[jj][:, 0:1], AF.Ln,
                bias=tb["misc"][:, 0:1], scale=tb["misc"][:, 1:2]))
            chain("scalar")
            emit("scalar", lambda s, jj=jj: s.activation(
                gcols[jj][:, 2:3], gcols[jj][:, 1:2], AF.Exp,
                bias=0.0, scale=tb["misc"][:, 2:3]))
            assert retag("scalar", "AC") == ac_r(i)
            W("vector", "AC", ac_r(i))
            if i >= 2:
                W("vector", "SS", ss0g + 16 * (i - 1))
            emit("vector", lambda v, jj=jj: v.tensor_scalar(
                gtrash[:], yg[jj][:], gcols[jj][:, 2:3], None, ALU.mult))
            chain("vector")
            emit("vector", lambda v, jj=jj: v.tensor_tensor(
                y2s[jj][:], gtrash[:], gnw[:], ALU.mult))
            assert retag("vector", "EV") == ev_y2(i)
            W("sync", "EV", ev_y2(i))
            emit("sync", lambda sy, jj=jj, i=i: sy.dma_start(
                y2_d[i * 128:(i + 1) * 128, :], y2s[jj][:]).then_inc(c.s["SS"], 16))
            y2_done.append(c.inc("SS", 16))
        flush()
        st_c.close()
        c.fence()
        if _SSM_STOP == "C":
            return None

        st_d = contextlib.ExitStack()
        y2fm = st_d.enter_context(nc.sbuf_tensor(f"sm{u_}_y2fm", [128, KC * T], BF16))
        stg2 = [st_d.enter_context(nc.sbuf_tensor(f"sm{u_}_st2{j}", [128, D], BF16)) for j in (0, 1)]
        with (nc.psum_tensor(f"sm{u_}_tr20", [128, 1024], BF16) as tr20,
              nc.psum_tensor(f"sm{u_}_tr21", [128, 1024], BF16) as tr21):
            mm_fm2, ev_fm2 = _emit_fm_rebuild(c, tb, y2_d, y2_done, y2fm, stg2,
                                              [tr20, tr21])
        psD = st_d.enter_context(nc.psum_tensor(f"sm{u_}_psD", [128, 2048], F32))
        wt2 = [st_d.enter_context(nc.sbuf_tensor(f"sm{u_}_w2{j}", [128, KC * 128], BF16)) for j in (0, 1)]
        ostg = [st_d.enter_context(nc.sbuf_tensor(f"sm{u_}_os{j}", [128, T], BF16)) for j in (0, 1)]
        lw0 = c.n["LW"]
        mm0 = c.n["MM"]
        ev0 = c.n["EV"]
        ss0 = c.n["SS"]

        @c.block.gpsimd
        def _(g):
            for m in range(KC):
                if m >= 2:
                    g.wait_ge(c.s["MM"], mm0 + 2 * (m - 1))
                g.dma_start(wt2[m % 2][:], wout[m]).then_inc(c.s["LW"], 16)

        @c.block.tensor
        def _(t):
            t.wait_ge(c.s["EV"], ev_fm2)
            for m in range(KC):
                t.wait_ge(c.s["LW"], lw0 + 16 * (m + 1))
                for half in range(2):
                    idx = 2 * m + half
                    if idx > 0:
                        t.wait_ge(c.s["EV"], ev0 + idx)
                    for k in range(KC):
                        for tc in range(4):
                            col = half * 2048 + tc * 512
                            mm = t.matmul(psD[:, tc * 512:(tc + 1) * 512],
                                          wt2[m % 2][:, k * 128:(k + 1) * 128],
                                          y2fm[:, k * T + col:k * T + col + 512],
                                          start=(k == 0), stop=(k == KC - 1))
                    mm.then_inc(c.s["MM"], 1)

        @c.block.vector
        def _(v):
            for m in range(KC):
                for half in range(2):
                    v.wait_ge(c.s["MM"], mm0 + 2 * m + half + 1)
                    if m >= 2 and half == 0:
                        v.wait_ge(c.s["SS"], ss0 + 16 * 2 * (m - 1))
                    sl = slice(half * 2048, (half + 1) * 2048)
                    v.tensor_copy(ostg[m % 2][:, sl], psD[:]).then_inc(c.s["EV"], 1)

        @c.block.sync
        def _(sy):
            for m in range(KC):
                sy.wait_ge(c.s["EV"], ev0 + 2 * m + 2)
                for part in range(2):
                    dst = bass.AP(o1.tensor,
                                  part * 16 * (128 * KC * 128) + m * 128,
                                  [[KC * 128, 128], [128 * KC * 128, 16], [1, 128]])
                    srcap = bass.AP(ostg[m % 2], part * 2048,
                                    [[T, 128], [128, 16], [1, 128]])
                    sy.dma_start(dst, srcap).then_inc(c.s["SS"], 16)

        c.n["LW"] = lw0 + 16 * KC
        c.n["MM"] = mm0 + 2 * KC
        c.n["EV"] = ev0 + 2 * KC
        c.n["SS"] = ss0 + 16 * 2 * KC
        c.fence()
        st_d.close()

        st_e = contextlib.ExitStack()
        wob = st_e.enter_context(nc.sbuf_tensor(f"sm{u_}_wob", [128, KC * 2048], BF16))
        olhs = [st_e.enter_context(nc.sbuf_tensor(f"sm{u_}_ol{j}", [128, KC * 128], BF16)) for j in (0, 1)]
        dstg = [st_e.enter_context(nc.sbuf_tensor(f"sm{u_}_dg{j}", [128, D], F32)) for j in (0, 1)]
        psE = st_e.enter_context(nc.psum_tensor(f"sm{u_}_psE", [128, 4096], F32))
        lw0 = c.n["LW"]
        lg0 = c.n["LG"]
        mm0 = c.n["MM"]
        ev0 = c.n["EV"]
        ss0 = c.n["SS"]
        delta_done = [ss0 + 16 * (i + 1) for i in range(NTT)]

        @c.block.gpsimd
        def _(g):
            for k in range(KC):
                g.dma_start(wob[:, k * 2048:(k + 1) * 2048],
                            wo[k * 128:(k + 1) * 128, :]).then_inc(c.s["LW"], 16)
            for i in range(NTT):
                if i >= 2:
                    g.wait_ge(c.s["MM"], mm0 + i - 1)
                g.dma_start(olhs[i % 2][:], o1[i].rearrange("p k c -> p (k c)")
                            ).then_inc(c.s["LG"], 16)

        @c.block.tensor
        def _(t):
            t.wait_ge(c.s["LW"], lw0 + 16 * KC)
            for i in range(NTT):
                t.wait_ge(c.s["LG"], lg0 + 16 * (i + 1))
                base = (i % 2) * 2048
                if i >= 2:
                    t.wait_ge(c.s["EV"], ev0 + i - 1)
                for k in range(KC):
                    for tc in range(4):
                        mm = t.matmul(psE[:, base + tc * 512:base + (tc + 1) * 512],
                                      olhs[i % 2][:, k * 128:(k + 1) * 128],
                                      wob[:, k * 2048 + tc * 512:k * 2048 + (tc + 1) * 512],
                                      start=(k == 0), stop=(k == KC - 1))
                mm.then_inc(c.s["MM"], 1)

        @c.block.vector
        def _(v):
            for i in range(NTT):
                v.wait_ge(c.s["MM"], mm0 + i + 1)
                if i >= 2:
                    v.wait_ge(c.s["SS"], ss0 + 16 * (i - 1))
                base = (i % 2) * 2048
                v.tensor_copy(dstg[i % 2][:], psE[:, base:base + 2048]
                              ).then_inc(c.s["EV"], 1)

        @c.block.sync
        def _(sy):
            for i in range(NTT):
                sy.wait_ge(c.s["EV"], ev0 + i + 1)
                sy.dma_start(dout[i * 128:(i + 1) * 128, :], dstg[i % 2][:]
                             ).then_inc(c.s["SS"], 16)

        c.n["LW"] = lw0 + 16 * KC
        c.n["LG"] = lg0 + 16 * NTT
        c.n["MM"] = mm0 + NTT
        c.n["EV"] = ev0 + NTT
        c.n["SS"] = ss0 + 16 * NTT
        c.fence()
        st_e.close()
    return delta_done



def build_nc(stop_after=None):
    nc = bass.Bass("TRN2", target_bir_lowering=False)
    dt_ = nc.dram_tensor

    x_in = dt_("x", [T, D], F32, kind="ExternalInput").ap()
    wqk = dt_("wqk", [32, 128, KC, 128], F32, kind="ExternalInput").ap()
    wv = dt_("wv", [D, D], F32, kind="ExternalInput").ap()
    battn = dt_("battn", [3 * D], F32, kind="ExternalInput").ap()
    wproj = dt_("wproj", [D, D], F32, kind="ExternalInput").ap()
    bprojb_d = dt_("bprojb", [128, D], F32, kind="ExternalInput").ap()
    wg = dt_("wg", [MI, 128, KC, 128], I8, kind="ExternalInput").ap()
    wu = dt_("wu", [MI, 128, KC, 128], I8, kind="ExternalInput").ap()
    wd = dt_("wd", [INTER, D], I8, kind="ExternalInput").ap()
    sg_in = dt_("sg", [INTER], F32, kind="ExternalInput").ap()
    su_in = dt_("su", [INTER], F32, kind="ExternalInput").ap()
    sdownb_d = dt_("sdownb", [128, D], F32, kind="ExternalInput").ap()
    normwb_d = dt_("normwb", [128, D], F32, kind="ExternalInput").ap()
    gnormwb_d = dt_("gnormwb", [128, D], F32, kind="ExternalInput").ap()
    wz = dt_("wz", [16, 128, KC, 128], F32, kind="ExternalInput").ap()
    wbc = dt_("wbc", [CCC, 128, KC, 128], F32, kind="ExternalInput").ap()
    wdt = dt_("wdt", [128, KC, SH], F32, kind="ExternalInput").ap()
    convw_d = dt_("convw", [CONVD, KCONV], F32, kind="ExternalInput").ap()
    convb_d = dt_("convb", [CONVD], F32, kind="ExternalInput").ap()
    dtb_d = dt_("dtb", [SH, 1], F32, kind="ExternalInput").ap()
    alog_d = dt_("alog", [SH, 1], F32, kind="ExternalInput").ap()
    dsk_d = dt_("dsk", [SH, 1], F32, kind="ExternalInput").ap()
    wout = dt_("wout", [16, 128, KC, 128], F32, kind="ExternalInput").ap()
    wo = dt_("wo", [D, D], F32, kind="ExternalInput").ap()
    eye128b_d = dt_("eye128b", [128, 128], BF16, kind="ExternalInput").ap()
    eye128f_d = dt_("eye128f", [128, 128], F32, kind="ExternalInput").ap()
    eyetb_d = dt_("eyetb", [64, SH * CH], BF16, kind="ExternalInput").ap()
    eh32_d = dt_("eh32", [33, SH * CH], BF16, kind="ExternalInput").ap()
    miota_hi_d = dt_("miota_hi", [1, SEQ], BF16, kind="ExternalInput").ap()
    miota_lo_d = dt_("miota_lo", [1, SEQ], BF16, kind="ExternalInput").ap()
    negmiota_hi_d = dt_("negmiota_hi", [1, SEQ], BF16, kind="ExternalInput").ap()
    negmiota_lo_d = dt_("negmiota_lo", [1, SEQ], BF16, kind="ExternalInput").ap()
    negmiotacols_d = dt_("negmiotacols", [128, 16], F32, kind="ExternalInput").ap()
    onesrowb_d = dt_("onesrowb", [1, SEQ], BF16, kind="ExternalInput").ap()
    onestile_d = dt_("onestile", [128, 16], BF16, kind="ExternalInput").ap()
    diagmask_d = dt_("diagmask", [128, 128], F32, kind="ExternalInput").ap()
    tri128_d = dt_("tri128", [128, 128], BF16, kind="ExternalInput").ap()
    lt64_d = dt_("lt64", [64, 64], BF16, kind="ExternalInput").ap()
    ones3264_d = dt_("ones3264", [SH, 64], BF16, kind="ExternalInput").ap()

    y_out = dt_("y", [T, D], F32, kind="ExternalOutput").ap()

    xres = dt_("xres", [T, D], F32).ap()
    xn_d = dt_("xn", [T, D], BF16).ap()
    qf = dt_("qf", [D, T], BF16).ap()
    kf = dt_("kf", [D, T], BF16).ap()
    vt = dt_("vt", [T, D], BF16).ap()
    fu = dt_("fu", [NTT, 128, MI, 128], BF16).ap()
    dout = dt_("dout", [T, D], F32).ap()
    zt = dt_("zt", [T, D], BF16).ap()
    xst = dt_("xst", [T, D], BF16).ap()
    bfm = dt_("bfm", [SPC, SN, SEQ], BF16).ap()
    cfm = dt_("cfm", [SPC, SN, SEQ], BF16).ap()
    btm = dt_("btm", [SPC, SEQ, SN], BF16).ap()
    yt_d = dt_("yt", [T, D], BF16).ap()
    y2_d = dt_("y2", [T, D], BF16).ap()
    o1 = dt_("o1", [NTT, 128, KC, 128], BF16).ap()

    stack = contextlib.ExitStack()
    with stack:
        sems = {name: stack.enter_context(nc.semaphore(name)) for name in SEMS}
        block = stack.enter_context(nc.Block())
        c = C(nc, block, sems)

        tb = {}
        for name, shape, dty in [
            ("bq", [128, 16], F32), ("bk", [128, 16], F32),
            ("sg", [128, MI], F32), ("su", [128, MI], F32),
            ("eye128b", [128, 128], BF16), ("eye128f", [128, 128], F32),
            ("eyetb", [64, SH * CH], BF16),
            ("negmiotacols", [128, 16], F32), ("diagmask", [128, 128], F32),
            ("tri128", [128, 128], BF16), ("lt64", [64, 64], BF16),
            ("convw", [128, CCC * KCONV], F32), ("convb", [128, CCC], F32),
            ("dtb", [SH, 1], F32), ("alogcol", [SH, 1], F32),
            ("anegcol", [SH, 1], F32), ("dskcol", [SH, 1], F32),
            ("dskd", [64, SH * CH], BF16), ("misc", [128, 8], F32),
        ]:
            tb[name] = stack.enter_context(nc.sbuf_tensor("t_" + name, shape, dty))
        tb["battn_d"] = battn
        tb["normwb_d"] = normwb_d
        tb["gnormwb_d"] = gnormwb_d
        tb["bprojb_d"] = bprojb_d
        tb["sdownb_d"] = sdownb_d
        tb["onesrowb_d"] = onesrowb_d
        tb["eh32_d"] = eh32_d
        tb["miota_hi_d"] = miota_hi_d
        tb["miota_lo_d"] = miota_lo_d
        tb["negmiota_hi_d"] = negmiota_hi_d
        tb["negmiota_lo_d"] = negmiota_lo_d
        tb["onestile_d"] = onestile_d

        L = {"battn": battn, "sg_in": sg_in, "su_in": su_in, "convw": convw_d,
             "convb": convb_d, "eye128b": eye128b_d, "eye128f": eye128f_d,
             "eyetb": eyetb_d, "eh32": eh32_d, "negmiotacols": negmiotacols_d,
             "diagmask": diagmask_d, "tri128": tri128_d, "lt64": lt64_d,
             "dtb": dtb_d, "alog": alog_d, "dsk": dsk_d,
             "ones3264_in": ones3264_d}
        ini = _phase_init_tables(c, tb, L)

        for eng in ("tensor", "vector", "scalar", "sync", "gpsimd"):
            @getattr(c.block, eng)
            def _(e, la=ini["la"], lg=ini["lg"], ev=ini["ev"], ac=ini["ac"]):
                e.wait_ge(c.s["LA"], la)
                e.wait_ge(c.s["LG"], lg)
                e.wait_ge(c.s["EV"], ev)
                e.wait_ge(c.s["AC"], ac)

        _, xn_done = _phase_resln(c, tb, src_x=x_in, src_delta=None,
                                  delta_done=None, src_done=None,
                                  dst_x=None, xn_dst=xn_d, extra_rms=False)
        c.fence()
        q_done, qk_done, v_done = _phase_qkv(c, tb, xn_d, xn_done, wqk, wv, qf, kf, vt)
        c.fence()
        pd = _phase_attn(c, tb, qf, kf, vt, v_done, wproj, dout)
        c.fence()
        if stop_after == "attn":
            _phase_resln(c, tb, src_x=x_in, src_delta=dout,
                         delta_done=pd, src_done=None,
                         dst_x=y_out, xn_dst=None, extra_rms=False, final=True,
                         bias_d=bprojb_d)
            _emit_taps(c, {"xn": xn_d, "dout": dout, "qf": qf, "kf": kf, "vt": vt})
            return nc
        _, xn_done = _phase_resln(c, tb, src_x=x_in, src_delta=dout,
                                  delta_done=pd, src_done=None,
                                  dst_x=xres, xn_dst=xn_d, extra_rms=False,
                                  bias_d=bprojb_d)
        c.fence()
        fd = _phase_ffn(c, tb, xn_d, xn_done, wg, wu, wd, fu, dout)
        if stop_after == "ffn1":
            _phase_resln(c, tb, src_x=xres, src_delta=dout, delta_done=fd,
                         src_done=None, dst_x=y_out, xn_dst=None,
                         extra_rms=False, final=True)
            _emit_taps(c, {"xn": xn_d, "dout": dout, "xres": xres})
            return nc
        xd2, xn_done = _phase_resln(c, tb, src_x=xres, src_delta=dout,
                                    delta_done=fd, src_done=None,
                                    dst_x=xres, xn_dst=xn_d, extra_rms=True)
        c.fence()
        Lssm = {"wz": wz, "wbc": wbc, "wdt": wdt, "zt": zt, "xst": xst,
                "bfm": bfm, "cfm": cfm, "btm": btm, "yt_d": yt_d, "y2_d": y2_d,
                "o1": o1, "dout": dout, "wout": wout, "wo": wo}
        sd_ = _phase_ssm(c, tb, xn_d, xn_done, Lssm)
        if sd_ is None:
            _phase_resln(c, tb, src_x=xres, src_delta=None, delta_done=None,
                         src_done=xd2, dst_x=y_out, xn_dst=None,
                         extra_rms=False, final=True)
            _emit_taps(c, {"xn": xn_d, "zt": zt, "xst": xst, "yt": yt_d,
                           "y2": y2_d, "bfm": bfm, "cfm": cfm, "btm": btm})
            return nc
        if stop_after == "ssm":
            _phase_resln(c, tb, src_x=xres, src_delta=dout, delta_done=sd_,
                         src_done=xd2, dst_x=y_out, xn_dst=None,
                         extra_rms=False, final=True)
            _emit_taps(c, {"xn": xn_d, "dout": dout, "zt": zt, "xst": xst,
                           "yt": yt_d, "y2": y2_d, "bfm": bfm, "cfm": cfm,
                           "btm": btm})
            return nc
        xd3, xn_done = _phase_resln(c, tb, src_x=xres, src_delta=dout,
                                    delta_done=sd_, src_done=xd2,
                                    dst_x=xres, xn_dst=xn_d, extra_rms=False)
        c.fence()
        fd = _phase_ffn(c, tb, xn_d, xn_done, wg, wu, wd, fu, dout)
        _phase_resln(c, tb, src_x=xres, src_delta=dout, delta_done=fd,
                     src_done=xd3, dst_x=y_out, xn_dst=None,
                     extra_rms=False, final=True)
        return nc
    return nc


_TAPS = []
_DBG_SKIP_DIAG = False
_SSM_STOP = None
_SSM_NM = None
_RMS_TEST = False


def _emit_taps(c, avail):
    if not _TAPS:
        return
    nc = c.nc
    out_final = c.n["OUT"]
    dbgs = []
    for nm in _TAPS:
        if nm not in avail:
            continue
        srcap = avail[nm]
        th = srcap.tensor
        dbg = nc.dram_tensor("dbg_" + nm, list(th.shape), th.dtype,
                             kind="ExternalOutput").ap()
        dbgs.append((dbg, srcap))

    @c.block.gpsimd
    def _(g):
        g.wait_ge(c.s["OUT"], out_final)
        g.wait_ge(c.s["SS"], c.n["SS"])
        for dbg, srcap in dbgs:
            g.dma_start(dbg, srcap).then_inc(c.s["SG"], 16)
            c.inc("SG", 16)
        g.wait_ge(c.s["SG"], c.n["SG"])


_NC_CACHE = {}
_LAST_RES = [None]


def kernel(_stop_after=None, **inputs):
    shared = _prep_shared(inputs)
    x = np.asarray(inputs["L0_attn_a_hidden_states"], dtype=np.float32)
    key = _stop_after
    if key not in _NC_CACHE:
        _NC_CACHE[key] = build_nc(stop_after=_stop_after)
    nc = _NC_CACHE[key]
    in_maps = []
    for cid in range(NCORES):
        m = dict(shared)
        m["x"] = np.ascontiguousarray(
            x[SPC * cid:SPC * (cid + 1)].reshape(T, D))
        in_maps.append(m)
    res = run_bass_kernel_spmd(nc, in_maps, core_ids=list(range(NCORES)))
    _LAST_RES[0] = res
    out = np.stack([np.asarray(res.results[cid]["y"]).reshape(SPC, SEQ, D)
                    for cid in range(NCORES)])
    return out.reshape(BATCH, SEQ, D)


# version 02
# speedup vs baseline: 1.1787x; 1.1787x over previous
import contextlib
import numpy as np
import concourse.bass as bass
import concourse.mybir as mybir
from concourse.bass_utils import run_bass_kernel_spmd

F32 = mybir.dt.float32
BF16 = mybir.dt.bfloat16
I8 = mybir.dt.int8
AF = mybir.ActivationFunctionType
ALU = mybir.AluOpType
AX = mybir.AxisListType

NCORES = 8
BATCH = 16
SEQ = 2048
D = 2048
SPC = BATCH // NCORES
T = SPC * SEQ
KC = D // 128
NTT = T // 128
NTS = SEQ // 128
H = 32
HD = 64
M_ALIBI = 2.0 ** (-0.25)
INTER = 8192
MI = INTER // 128
EPS = 1e-5
SH = 32
SN = 256
KCONV = 4
CH = 64
NCHUNK = SEQ // CH
CONVD = D + 2 * SN
CCC = CONVD // 128
MASKNEG = -100000.0


def _blk_lhsT(w):
    din, dout = w.shape
    kc, mc = din // 128, dout // 128
    return np.ascontiguousarray(w.reshape(kc, 128, mc, 128).transpose(2, 1, 0, 3))


def _prep_shared(inp):
    import ml_dtypes
    bf = ml_dtypes.bfloat16
    f = np.ascontiguousarray
    W_attn = np.asarray(inp["L0_attn_a_W_attn"], dtype=np.float32)
    w_in = np.asarray(inp["L0_attn_b_w_in"], dtype=np.float32)
    s = {}
    s["wqk"] = f(np.concatenate([_blk_lhsT(W_attn[:, 0:D]),
                                 _blk_lhsT(W_attn[:, D:2 * D])], axis=0))
    s["wv"] = f(W_attn[:, 2 * D:3 * D])
    s["battn"] = np.asarray(inp["L0_attn_a_b_attn"], dtype=np.float32)
    s["wproj"] = f(np.asarray(inp["L0_attn_a_W_proj"], dtype=np.float32))
    s["bprojb"] = f(np.broadcast_to(
        np.asarray(inp["L0_attn_a_b_proj"], dtype=np.float32), (128, D)).copy())
    s["wg"] = _blk_lhsT(np.asarray(inp["L0_ffn_w_gate"]))
    s["wu"] = _blk_lhsT(np.asarray(inp["L0_ffn_w_up"]))
    s["wd"] = f(np.asarray(inp["L0_ffn_w_down"]))
    s["sg"] = np.asarray(inp["L0_ffn_s_gate"], dtype=np.float32)
    s["su"] = np.asarray(inp["L0_ffn_s_up"], dtype=np.float32)
    s["sdownb"] = f(np.broadcast_to(
        np.asarray(inp["L0_ffn_s_down"], dtype=np.float32), (128, D)).copy())
    s["normwb"] = f(np.broadcast_to(
        np.asarray(inp["L0_attn_b_norm_w"], dtype=np.float32), (128, D)).copy())
    s["gnormwb"] = f(np.broadcast_to(
        np.asarray(inp["L0_attn_b_gnorm_w"], dtype=np.float32), (128, D)).copy())
    s["wz"] = _blk_lhsT(w_in[:, 0:D])
    s["wbc"] = _blk_lhsT(w_in[:, D:D + CONVD])
    s["wdt"] = f(w_in[:, D + CONVD:].reshape(KC, 128, SH).transpose(1, 0, 2))
    s["convw"] = np.asarray(inp["L0_attn_b_conv_w"], dtype=np.float32)
    s["convb"] = np.asarray(inp["L0_attn_b_conv_b"], dtype=np.float32)
    s["dtb"] = np.asarray(inp["L0_attn_b_dt_bias"], dtype=np.float32).reshape(SH, 1)
    s["alog"] = np.asarray(inp["L0_attn_b_A_log"], dtype=np.float32).reshape(SH, 1)
    s["dsk"] = np.asarray(inp["L0_attn_b_D_skip"], dtype=np.float32).reshape(SH, 1)
    s["wout"] = _blk_lhsT(np.asarray(inp["L0_attn_b_w_out"], dtype=np.float32))
    s["wo"] = f(np.asarray(inp["L0_attn_b_wo_w"], dtype=np.float32))
    s["eye128b"] = np.eye(128, dtype=bf)
    s["eye128f"] = np.eye(128, dtype=np.float32)
    s["eyetb"] = f(np.tile(np.eye(64, dtype=np.float32), (1, SH)).astype(bf))
    eh = np.zeros((33, SH * CH), dtype=np.float32)
    for h in range(SH):
        eh[h, h * CH:(h + 1) * CH] = 1.0
    s["eh32"] = f(eh.astype(bf))
    ar = np.arange(SEQ, dtype=np.float32)

    def hilo(v):
        hi = v.astype(bf)
        lo = (v - hi.astype(np.float32)).astype(bf)
        return hi.reshape(1, SEQ), lo.reshape(1, SEQ)

    s["miota_hi"], s["miota_lo"] = hilo(M_ALIBI * ar)
    s["negmiota_hi"], s["negmiota_lo"] = hilo(-M_ALIBI * ar)
    s["negmiotacols"] = f((-M_ALIBI * (np.arange(16)[None, :] * 128
                                       + np.arange(128)[:, None])).astype(np.float32))
    s["onesrowb"] = np.ones((1, SEQ), dtype=bf)
    s["onestile"] = np.ones((128, 16), dtype=bf)
    s["diagmask"] = f(np.where(np.arange(128)[None, :] <= np.arange(128)[:, None],
                               0.0, MASKNEG).astype(np.float32))
    s["tri128"] = f(np.where(np.arange(128)[:, None] <= np.arange(128)[None, :],
                             1.0, 0.0).astype(bf))
    s["lt64"] = f(np.where(np.arange(64)[None, :] <= np.arange(64)[:, None],
                           1.0, 0.0).astype(bf))
    s["ones3264"] = np.ones((SH, 64), dtype=bf)
    return s



class C:
    def __init__(self, nc, block, sems):
        self.nc = nc
        self.block = block
        self.s = sems
        self.n = {k: 0 for k in sems}

    def inc(self, name, k=1):
        self.n[name] += k
        return self.n[name]

    def fence(self):
        vals = {k: self.n[k] for k in SEMS}
        for eng in ("tensor", "vector", "scalar", "sync", "gpsimd"):
            @getattr(self.block, eng)
            def _(e, vals=vals):
                for k, v in vals.items():
                    if v > 0:
                        e.wait_ge(self.s[k], v)


SEMS = ["LW", "LA", "LG", "SS", "SG", "MM", "EV", "AC", "PL", "OUT",
        "CHT", "CHV", "CHS", "CHP"]

_UNIQ = [0]


def _uid():
    _UNIQ[0] += 1
    return _UNIQ[0]



def _phase_resln(c, tb, src_x, src_delta, delta_done, src_done,
                 dst_x, xn_dst, extra_rms, final=False, bias_d=None):
    nc = c.nc
    u_ = _uid()
    st = contextlib.ExitStack()
    with st:
        xa = [st.enter_context(nc.sbuf_tensor(f"rl{u_}_xa{j}", [128, D], F32)) for j in (0, 1)]
        da = [st.enter_context(nc.sbuf_tensor(f"rl{u_}_da{j}", [128, D], F32)) for j in (0, 1)]
        xs = [st.enter_context(nc.sbuf_tensor(f"rl{u_}_xs{j}", [128, D], F32)) for j in (0, 1)]
        xc = [st.enter_context(nc.sbuf_tensor(f"rl{u_}_xc{j}", [128, D], F32)) for j in (0, 1)]
        xnb = [st.enter_context(nc.sbuf_tensor(f"rl{u_}_xnb{j}", [128, D], BF16)) for j in (0, 1)]
        xnb2 = [st.enter_context(nc.sbuf_tensor(f"rl{u_}_xnb2{j}", [128, D], BF16)) for j in (0, 1)]
        trash = st.enter_context(nc.sbuf_tensor(f"rl{u_}_trash", [128, D], BF16))
        trash2 = [st.enter_context(nc.sbuf_tensor(f"rl{u_}_tr2{j}", [128, D], BF16)) for j in (0, 1)]
        cols = [st.enter_context(nc.sbuf_tensor(f"rl{u_}_col{j}", [128, 8], F32)) for j in (0, 1)]
        if extra_rms:
            normwb = st.enter_context(nc.sbuf_tensor(f"rl{u_}_nw", [128, D], F32))
            lg_nw = c.inc("LG", 16)

            @c.block.gpsimd
            def _(g):
                g.dma_start(normwb[:], tb["normwb_d"][:]).then_inc(c.s["LG"], 16)
        if bias_d is not None:
            biasb = st.enter_context(nc.sbuf_tensor(f"rl{u_}_bias", [128, D], F32))
            lg_bias = c.inc("LG", 16)

            @c.block.gpsimd
            def _(g):
                g.dma_start(biasb[:], bias_d[:]).then_inc(c.s["LG"], 16)

        has_delta = src_delta is not None
        n_loads = 2 if has_delta else 1
        la0 = c.n["LA"]
        ss0 = c.n["SS"]
        out0 = c.n["OUT"]
        n_stores = (1 if dst_x is not None else 0) + (1 if xn_dst is not None else 0)
        la_tile = [la0 + 16 * n_loads * (i + 1) for i in range(NTT)]
        ev0 = c.n["EV"]
        ac0 = c.n["AC"]
        n_ev_tile = 3 if extra_rms else 2
        n_ac_tile = 2 if extra_rms else 1
        if final:
            n_ev_tile, n_ac_tile = 1, 0

        def ev_sq(i):
            return ev0 + n_ev_tile * i + 1

        def ev_ms2(i):
            return ev0 + n_ev_tile * i + 2

        def ev_last(i):
            return ev0 + n_ev_tile * (i + 1)

        def ac_rstd(i):
            return ac0 + n_ac_tile * i + 1

        def ac_rrms(i):
            return ac0 + n_ac_tile * i + 2

        def ss_tile(i):
            return ss0 + 16 * n_stores * (i + 1)

        xnew_done = []
        xn_done = []
        for i in range(NTT):
            if final:
                xnew_done.append(out0 + 16 * (i + 1))
                xn_done.append(None)
            else:
                base = ss0 + 16 * n_stores * i
                xnew_done.append(base + 16 if dst_x is not None else None)
                xn_done.append(ss_tile(i) if xn_dst is not None else None)

        @c.block.sync
        def _(sy):
            for i in range(NTT):
                j = i % 2
                if i >= 2:
                    sy.wait_ge(c.s["EV"], ev_last(i - 2))
                if src_done is not None:
                    sy.wait_ge(c.s["SS"], src_done[i])
                sy.dma_start(xa[j][:], src_x[i * 128:(i + 1) * 128, :]
                             ).then_inc(c.s["LA"], 16)
                if has_delta:
                    if delta_done is not None:
                        sy.wait_ge(c.s["SS"], delta_done[i])
                    sy.dma_start(da[j][:], src_delta[i * 128:(i + 1) * 128, :]
                                 ).then_inc(c.s["LA"], 16)
                sy.wait_ge(c.s["EV"], ev_last(i))
                if final:
                    sy.dma_start(dst_x[i * 128:(i + 1) * 128, :], xs[j][:]
                                 ).then_inc(c.s["OUT"], 16)
                else:
                    if dst_x is not None:
                        sy.dma_start(dst_x[i * 128:(i + 1) * 128, :], xs[j][:]
                                     ).then_inc(c.s["SS"], 16)
                    if xn_dst is not None:
                        sy.dma_start(xn_dst[i * 128:(i + 1) * 128, :],
                                     (xnb2[j] if extra_rms else xnb[j])[:]
                                     ).then_inc(c.s["SS"], 16)
            if final:
                sy.wait_ge(c.s["OUT"], out0 + 16 * NTT)

        @c.block.vector
        def _(v):
            if extra_rms:
                v.wait_ge(c.s["LG"], lg_nw)
            if bias_d is not None:
                v.wait_ge(c.s["LG"], lg_bias)
            for i in range(NTT):
                j = i % 2
                v.wait_ge(c.s["LA"], la_tile[i])
                if i >= 2:
                    if final:
                        v.wait_ge(c.s["OUT"], out0 + 16 * (i - 1))
                    else:
                        v.wait_ge(c.s["SS"], ss_tile(i - 2))
                if has_delta and bias_d is not None:
                    v.tensor_tensor(xc[j][:], xa[j][:], da[j][:], ALU.add
                                    ).then_inc(c.s["CHV"], 1)
                    v.wait_ge(c.s["CHV"], c.inc("CHV"))
                    op = v.tensor_tensor(xs[j][:], xc[j][:], biasb[:], ALU.add)
                elif has_delta:
                    op = v.tensor_tensor(xs[j][:], xa[j][:], da[j][:], ALU.add)
                else:
                    op = v.tensor_copy(xs[j][:], xa[j][:])
                if final:
                    op.then_inc(c.s["EV"], 1)
                    continue
                op.then_inc(c.s["CHV"], 1)
                v.wait_ge(c.s["CHV"], c.inc("CHV"))
                v.reduce_sum(cols[j][:, 0:1], xs[j][:], axis=AX.X, negate=True
                             ).then_inc(c.s["CHV"], 1)
                v.wait_ge(c.s["CHV"], c.inc("CHV"))
                v.tensor_scalar(cols[j][:, 1:2], cols[j][:, 0:1], 1.0 / D, None, ALU.mult
                                ).then_inc(c.s["CHV"], 1)
                v.wait_ge(c.s["CHV"], c.inc("CHV"))
                v.tensor_scalar(xc[j][:], xs[j][:], cols[j][:, 1:2], None, ALU.add
                                ).then_inc(c.s["EV"], 1)
                v.wait_ge(c.s["AC"], ac_rstd(i))
                if not extra_rms:
                    v.tensor_scalar(xnb[j][:], xc[j][:], cols[j][:, 4:5], None, ALU.mult
                                    ).then_inc(c.s["EV"], 1)
                else:
                    v.tensor_scalar(xnb[j][:], xc[j][:], cols[j][:, 4:5], None, ALU.mult
                                    ).then_inc(c.s["EV"], 1)
                    v.wait_ge(c.s["AC"], ac_rrms(i))
                    v.tensor_scalar(trash2[j][:], xnb[j][:], cols[j][:, 7:8], None, ALU.mult
                                    ).then_inc(c.s["CHV"], 1)
                    v.wait_ge(c.s["CHV"], c.inc("CHV"))
                    v.tensor_tensor(xnb2[j][:], trash2[j][:], normwb[:], ALU.mult
                                    ).then_inc(c.s["EV"], 1)

        if not final:
            @c.block.scalar
            def _(s):
                for i in range(NTT):
                    j = i % 2
                    s.wait_ge(c.s["EV"], ev_sq(i))
                    s.activation(trash[:], xc[j][:], AF.Square, bias=0.0, scale=1.0,
                                 accum_out=cols[j][:, 2:3]).then_inc(c.s["CHS"], 1)
                    s.wait_ge(c.s["CHS"], c.inc("CHS"))
                    s.activation(cols[j][:, 3:4], cols[j][:, 2:3], AF.Ln,
                                 bias=tb["misc"][:, 0:1], scale=tb["misc"][:, 1:2]
                                 ).then_inc(c.s["CHS"], 1)
                    s.wait_ge(c.s["CHS"], c.inc("CHS"))
                    s.activation(cols[j][:, 4:5], cols[j][:, 3:4], AF.Exp,
                                 bias=0.0, scale=tb["misc"][:, 2:3]).then_inc(c.s["AC"], 1)
                    if extra_rms:
                        s.wait_ge(c.s["EV"], ev_ms2(i))
                        s.activation(trash[:], xnb[j][:], AF.Square, bias=0.0,
                                     scale=1.0, accum_out=cols[j][:, 5:6]
                                     ).then_inc(c.s["CHS"], 1)
                        s.wait_ge(c.s["CHS"], c.inc("CHS"))
                        s.activation(cols[j][:, 6:7], cols[j][:, 5:6], AF.Ln,
                                     bias=tb["misc"][:, 0:1], scale=tb["misc"][:, 1:2]
                                     ).then_inc(c.s["CHS"], 1)
                        s.wait_ge(c.s["CHS"], c.inc("CHS"))
                        s.activation(cols[j][:, 7:8], cols[j][:, 6:7], AF.Exp,
                                     bias=0.0, scale=tb["misc"][:, 2:3]).then_inc(c.s["AC"], 1)

        c.n["LA"] = la0 + 16 * n_loads * NTT
        c.n["EV"] = ev0 + n_ev_tile * NTT
        c.n["AC"] = ac0 + n_ac_tile * NTT
        if final:
            c.n["OUT"] = out0 + 16 * NTT
        else:
            c.n["SS"] = ss0 + 16 * n_stores * NTT
    return xnew_done, xn_done



def _phase_init_tables(c, tb, L):
    nc = c.nc
    st = contextlib.ExitStack()
    with st:
        dskrow = st.enter_context(nc.sbuf_tensor("ti_dskrow", [SH, 64], BF16))
        eh32sb = st.enter_context(nc.sbuf_tensor("ti_eh32", [33, SH * CH], BF16))
        tb["eh32"] = eh32sb
        ones3264 = st.enter_context(nc.sbuf_tensor("ti_o3264", [SH, 64], BF16))
        tmpc = st.enter_context(nc.sbuf_tensor("ti_tmpc", [SH, 1], F32))
        ps2 = st.enter_context(nc.psum_tensor("ti_ps2", [64, SH * CH], F32))

        @c.block.gpsimd
        def _(g):
            for dst, src in [
                (tb["bq"][:], L["battn"][0:D].rearrange("(m p) -> p m", p=128)),
                (tb["bk"][:], L["battn"][D:2 * D].rearrange("(m p) -> p m", p=128)),
                (tb["sg"][:], L["sg_in"].rearrange("(m p) -> p m", p=128)),
                (tb["su"][:], L["su_in"].rearrange("(m p) -> p m", p=128)),
                (tb["convw"].ap().rearrange("p (cc k) -> p cc k", cc=CCC),
                 L["convw"].rearrange("(cc p) k -> p cc k", p=128)),
                (tb["convb"][:], L["convb"].rearrange("(cc p) -> p cc", p=128)),
            ]:
                g.dma_start(dst, src, allow_slow_non_contiguous=True
                            ).then_inc(c.s["LG"], 16); c.inc("LG", 16)

        @c.block.sync
        def _(sy):
            for name, src in [("eye128b", L["eye128b"]), ("eye128f", L["eye128f"]),
                              ("eyetb", L["eyetb"]), ("eh32", L["eh32"]),
                              ("negmiotacols", L["negmiotacols"]),
                              ("diagmask", L["diagmask"]),
                              ("tri128", L["tri128"]), ("lt64", L["lt64"]),
                              ("dtb", L["dtb"]), ("alogcol", L["alog"]),
                              ("dskcol", L["dsk"]), ("ones3264i", None)]:
                if name == "ones3264i":
                    sy.dma_start(ones3264[:], L["ones3264_in"][:]
                                 ).then_inc(c.s["LA"], 16); c.inc("LA", 16)
                else:
                    sy.dma_start(tb[name][:], src[:]).then_inc(c.s["LA"], 16)
                    c.inc("LA", 16)
        la_t = c.n["LA"]
        lg_t = c.n["LG"]

        @c.block.scalar
        def _(s):
            s.wait_ge(c.s["LA"], la_t)
            s.activation(tmpc[:], tb["alogcol"][:], AF.Exp, bias=0.0, scale=1.0
                         ).then_inc(c.s["AC"], 1); c.inc("AC")
        ac_a = c.n["AC"]

        @c.block.vector
        def _(v):
            v.memset(tb["misc"][:, 0:1], EPS)
            v.memset(tb["misc"][:, 1:2], 1.0 / D)
            v.memset(tb["misc"][:, 2:3], -0.5)
            v.memset(tb["misc"][:, 3:4], -1.0)
            v.memset(tb["misc"][:, 4:5], 1.0)
            v.wait_ge(c.s["LA"], la_t)
            v.wait_ge(c.s["LG"], lg_t)
            v.wait_ge(c.s["AC"], ac_a)
            v.tensor_scalar(tb["anegcol"][:], tmpc[:], -1.0, None, ALU.mult)
            v.tensor_scalar(dskrow[:], ones3264[:], tb["dskcol"][:], None, ALU.mult
                            ).then_inc(c.s["EV"], 1); c.inc("EV")
        ev_dr = c.n["EV"]

        @c.block.tensor
        def _(t):
            t.wait_ge(c.s["EV"], ev_dr)
            for r in range(4):
                mm = t.matmul(ps2[:, r * 512:(r + 1) * 512], dskrow[:],
                              tb["eh32"][0:SH, r * 512:(r + 1) * 512],
                              start=True, stop=True)
            mm.then_inc(c.s["MM"], 1); c.inc("MM")
        mm_f = c.n["MM"]

        @c.block.vector
        def _(v):
            v.wait_ge(c.s["MM"], mm_f)
            v.tensor_tensor(tb["dskd"][:], ps2[:], tb["eyetb"][:], ALU.mult
                            ).then_inc(c.s["EV"], 1); c.inc("EV")
    return {"la": la_t, "lg": lg_t, "ev": c.n["EV"], "ac": c.n["AC"], "mm": c.n["MM"]}



def _emit_fm_rebuild(c, tb, src_dram, src_done, fmbuf, stg, trp, ntt=NTT, row0=0):
    tw = ntt * 128
    la0 = c.n["LA"]
    mm0 = c.n["MM"]
    ev0 = c.n["EV"]

    def mm_tile(i):
        return mm0 + KC * (i + 1)

    def ev_tile(i):
        return ev0 + KC * (i + 1)

    @c.block.sync
    def _(sy):
        for i in range(ntt):
            if i >= 2:
                sy.wait_ge(c.s["MM"], mm_tile(i - 2))
            if src_done is not None:
                sy.wait_ge(c.s["SS"], src_done[i])
            sy.dma_start(stg[i % 2][:],
                         src_dram[(row0 + i) * 128:(row0 + i + 1) * 128, :]
                         ).then_inc(c.s["LA"], 16)

    @c.block.tensor
    def _(t):
        for i in range(ntt):
            t.wait_ge(c.s["LA"], la0 + 16 * (i + 1))
            for k in range(KC):
                idx = i * KC + k
                if idx >= 2:
                    t.wait_ge(c.s["EV"], ev0 + idx - 1)
                t.transpose(trp[idx % 2][:, 0:128],
                            stg[i % 2][:, k * 128:(k + 1) * 128],
                            tb["eye128b"][:]).then_inc(c.s["MM"], 1)

    @c.block.vector
    def _(v):
        for i in range(ntt):
            for k in range(KC):
                idx = i * KC + k
                v.wait_ge(c.s["MM"], mm0 + idx + 1)
                v.tensor_copy(fmbuf[:, k * tw + i * 128: k * tw + (i + 1) * 128],
                              trp[idx % 2][:, 0:128]).then_inc(c.s["EV"], 1)

    c.n["LA"] = la0 + 16 * ntt
    c.n["MM"] = mm0 + KC * ntt
    c.n["EV"] = ev0 + KC * ntt
    return c.n["MM"], c.n["EV"]



def _phase_qkv(c, tb, xn_d, xn_done, wqk, wv, qf, kf, vt):
    nc = c.nc
    st = contextlib.ExitStack()
    with st:
        xnfm = st.enter_context(nc.sbuf_tensor("qk_xnfm", [128, KC * T], BF16))
        stg = [st.enter_context(nc.sbuf_tensor(f"qk_stg{j}", [128, D], BF16)) for j in (0, 1)]
        wt = [st.enter_context(nc.sbuf_tensor(f"qk_wt{j}", [128, KC * 128], BF16)) for j in (0, 1)]
        qstg = [st.enter_context(nc.sbuf_tensor(f"qk_qstg{j}", [128, 2048], BF16)) for j in (0, 1)]
        bvb = st.enter_context(nc.sbuf_tensor("qk_bvb", [128, D], F32))
        onescol = st.enter_context(nc.sbuf_tensor("qk_ones", [1, 128], F32))

        with (nc.psum_tensor("qk_trp0", [128, 1024], BF16) as trp0,
              nc.psum_tensor("qk_trp1", [128, 1024], BF16) as trp1,
              nc.psum_tensor("qk_psbv", [128, D], F32) as psbv):
            trp = [trp0, trp1]
            la_bv = c.inc("LA", 16)

            @c.block.sync
            def _(sy):
                sy.dma_start(bvb[0:1, :],
                             tb["battn_d"][2 * D:3 * D].rearrange("(o d) -> o d", o=1)
                             ).then_inc(c.s["LA"], 16)

            @c.block.vector
            def _(v):
                v.memset(onescol[:], 1.0).then_inc(c.s["EV"], 1)
            ev_ones = c.inc("EV")

            @c.block.tensor
            def _(t):
                t.wait_ge(c.s["LA"], la_bv)
                t.wait_ge(c.s["EV"], ev_ones)
                for r in range(4):
                    mm = t.matmul(psbv[:, r * 512:(r + 1) * 512], onescol[:],
                                  bvb[0:1, r * 512:(r + 1) * 512],
                                  start=True, stop=True)
                mm.then_inc(c.s["MM"], 1)
            mm_bv = c.inc("MM")

            @c.block.vector
            def _(v):
                v.wait_ge(c.s["MM"], mm_bv)
                v.tensor_copy(bvb[:], psbv[:]).then_inc(c.s["EV"], 1)
            ev_bvb = c.inc("EV")

            mm_fm, ev_fm = _emit_fm_rebuild(c, tb, xn_d, xn_done, xnfm, stg, trp)
        psA = st.enter_context(nc.psum_tensor("qk_psA", [128, 4096], F32))

        lw0 = c.n["LW"]
        mm0 = c.n["MM"]
        ev0 = c.n["EV"]
        ss0 = c.n["SS"]

        def mmh(m, half):
            return mm0 + 2 * m + half + 1

        def evh(m, half):
            return ev0 + 2 * m + half + 1

        def ssh(m, half):
            return ss0 + 16 * (2 * m + half + 1)

        @c.block.gpsimd
        def _(g):
            for m in range(32):
                if m >= 2:
                    g.wait_ge(c.s["MM"], mmh(m - 1, 1))
                g.dma_start(wt[m % 2][:], wqk[m]).then_inc(c.s["LW"], 16)

        @c.block.tensor
        def _(t):
            t.wait_ge(c.s["EV"], ev_fm)
            for m in range(32):
                t.wait_ge(c.s["LW"], lw0 + 16 * (m + 1))
                for half in range(2):
                    if m > 0:
                        t.wait_ge(c.s["EV"], evh(m - 1, half))
                    for k in range(KC):
                        lt = wt[m % 2][:, k * 128:(k + 1) * 128]
                        for tc in range(4):
                            col = half * 2048 + tc * 512
                            mm = t.matmul(psA[:, col:col + 512], lt,
                                          xnfm[:, k * T + col:k * T + col + 512],
                                          start=(k == 0), stop=(k == KC - 1))
                    mm.then_inc(c.s["MM"], 1)

        @c.block.vector
        def _(v):
            for m in range(32):
                for half in range(2):
                    if m >= 1:
                        v.wait_ge(c.s["SS"], ssh(m - 1, half))
                    v.wait_ge(c.s["MM"], mmh(m, half))
                    sl = slice(half * 2048, (half + 1) * 2048)
                    if m < 16:
                        v.tensor_scalar(qstg[half][:], psA[:, sl],
                                        tb["bq"][:, m:m + 1], 0.125,
                                        ALU.add, ALU.mult).then_inc(c.s["EV"], 1)
                    else:
                        v.tensor_scalar(qstg[half][:], psA[:, sl],
                                        tb["bk"][:, m - 16:m - 15], None,
                                        ALU.add).then_inc(c.s["EV"], 1)

        @c.block.sync
        def _(sy):
            for m in range(32):
                dst = qf if m < 16 else kf
                r = (m % 16) * 128
                for half in range(2):
                    sy.wait_ge(c.s["EV"], evh(m, half))
                    sy.dma_start(dst[r:r + 128, half * 2048:(half + 1) * 2048],
                                 qstg[half][:]).then_inc(c.s["SS"], 16)

        c.n["LW"] = lw0 + 16 * 32
        c.n["MM"] = mm0 + 64
        c.n["EV"] = ev0 + 64
        c.n["SS"] = ss0 + 16 * 64
        q_done = ss0 + 16 * 32
        qk_done = c.n["SS"]

        wvt = st.enter_context(nc.sbuf_tensor("qk_wvt", [128, KC * 512], BF16))
        vstg = [st.enter_context(nc.sbuf_tensor(f"qk_vstg{j}", [128, 2048], BF16)) for j in (0, 1)]
        lw0 = c.n["LW"]
        mm0 = c.n["MM"]
        ev0 = c.n["EV"]
        ss0 = c.n["SS"]

        def vmm(vg, tg):
            return mm0 + (vg * 8 + tg) + 1

        def vev(vg, tg):
            return ev0 + (vg * 8 + tg) + 1

        def vss(vg, tg):
            return ss0 + 16 * ((vg * 8 + tg) + 1)

        @c.block.gpsimd
        def _(g):
            for vg in range(4):
                if vg > 0:
                    g.wait_ge(c.s["MM"], vmm(vg - 1, 7))
                for k in range(KC):
                    g.dma_start(wvt[:, k * 512:(k + 1) * 512],
                                wv[k * 128:(k + 1) * 128, vg * 512:(vg + 1) * 512]
                                ).then_inc(c.s["LW"], 16)

        @c.block.tensor
        def _(t):
            for vg in range(4):
                t.wait_ge(c.s["LW"], lw0 + 16 * KC * (vg + 1))
                for tg in range(8):
                    gi = vg * 8 + tg
                    if gi >= 2:
                        t.wait_ge(c.s["EV"], vev((gi - 2) // 8, (gi - 2) % 8))
                    for k in range(KC):
                        for ti in range(4):
                            tile = tg * 4 + ti
                            bank = (tg % 2) * 2048 + ti * 512
                            mm = t.matmul(psA[:, bank:bank + 512],
                                          xnfm[:, k * T + tile * 128:k * T + (tile + 1) * 128],
                                          wvt[:, k * 512:(k + 1) * 512],
                                          start=(k == 0), stop=(k == KC - 1))
                    mm.then_inc(c.s["MM"], 1)

        @c.block.vector
        def _(v):
            for vg in range(4):
                for tg in range(8):
                    gi = vg * 8 + tg
                    j = gi % 2
                    if gi >= 2:
                        v.wait_ge(c.s["SS"], vss((gi - 2) // 8, (gi - 2) % 8))
                    v.wait_ge(c.s["MM"], vmm(vg, tg))
                    for ti in range(4):
                        bank = (tg % 2) * 2048 + ti * 512
                        op = v.tensor_tensor(vstg[j][:, ti * 512:(ti + 1) * 512],
                                             psA[:, bank:bank + 512],
                                             bvb[:, vg * 512:(vg + 1) * 512], ALU.add)
                    op.then_inc(c.s["EV"], 1)

        @c.block.sync
        def _(sy):
            for vg in range(4):
                for tg in range(8):
                    gi = vg * 8 + tg
                    sy.wait_ge(c.s["EV"], vev(vg, tg))
                    dst = bass.AP(vt.tensor, (tg * 512) * D + vg * 512,
                                  [[D, 128], [128 * D, 4], [1, 512]])
                    sy.dma_start(dst, vstg[gi % 2].ap().rearrange(
                        "p (ti c) -> p ti c", ti=4)).then_inc(c.s["SS"], 16)

        c.n["LW"] = lw0 + 16 * KC * 4
        c.n["MM"] = mm0 + 32
        c.n["EV"] = ev0 + 32
        c.n["SS"] = ss0 + 16 * 32
        v_done = c.n["SS"]
    return q_done, qk_done, v_done



def _phase_attn(c, tb, qf, kf, vt, v_done, wproj_d, dout):
    nc = c.nc
    st = contextlib.ExitStack()
    with st:
        qx = [st.enter_context(nc.sbuf_tensor(f"at_qx{j}", [128, SEQ], BF16)) for j in (0, 1)]
        kx = [st.enter_context(nc.sbuf_tensor(f"at_kx{j}", [128, SEQ], BF16)) for j in (0, 1)]
        vx = [st.enter_context(nc.sbuf_tensor(f"at_vx{j}", [128, 16 * 65], BF16)) for j in (0, 1)]
        probs = [st.enter_context(nc.sbuf_tensor(f"at_pr{j}", [128, 512], BF16)) for j in (0, 1, 2, 3)]
        probsd = st.enter_context(nc.sbuf_tensor("at_prd", [128, 128], BF16))
        dmsb = [st.enter_context(nc.sbuf_tensor(f"at_dmsb{j}", [128, 128], F32)) for j in (0, 1)]
        dclamp = st.enter_context(nc.sbuf_tensor("at_dclamp", [128, 128], F32))
        trash_e = st.enter_context(nc.sbuf_tensor("at_trash", [128, 2048], BF16))
        mx4 = st.enter_context(nc.sbuf_tensor("at_mx4", [128, 8], F32))
        scol = st.enter_context(nc.sbuf_tensor("at_scol", [128, 8], F32))
        nm1 = st.enter_context(nc.sbuf_tensor("at_nm1", [128, 4], F32))
        nmls = st.enter_context(nc.sbuf_tensor("at_nmls", [128, 16], F32))
        nmrow = st.enter_context(nc.sbuf_tensor("at_nmrow", [16, 128], F32))
        vcols = st.enter_context(nc.sbuf_tensor("at_vcols", [128, 16], F32))
        rowvf = st.enter_context(nc.sbuf_tensor("at_rowvf", [1, SEQ], F32))
        row66 = st.enter_context(nc.sbuf_tensor("at_row66", [1, SEQ], BF16))
        row67 = st.enter_context(nc.sbuf_tensor("at_row67", [1, SEQ], BF16))
        attn_fm = st.enter_context(nc.sbuf_tensor("at_afm", [128, KC * SEQ], BF16))
        wpj = st.enter_context(nc.sbuf_tensor("at_wpj", [128, KC * 2048], BF16))
        pstg = [st.enter_context(nc.sbuf_tensor(f"at_pstg{j}", [128, D], F32)) for j in (0, 1)]
        PS = st.enter_context(nc.psum_tensor("at_PS", [128, 4096], F32))

        def PBs(pb, n=512):
            return PS[:, pb * 512: pb * 512 + n]

        def P1s(p, n=512):
            return PS[:, 1024 + p * 512: 1024 + p * 512 + n]

        def P2s(a, b):
            return PS[0:65, 2048 + a: 2048 + b]

        def P2evac():
            return PS[0:64, 2048:4096]

        ops = {k: [] for k in ("tensor", "vector", "scalar", "sync", "gpsimd")}

        def emit(eng, fn):
            ops[eng].append(fn)

        def retag(eng, sem):
            last = ops[eng].pop()
            emit(eng, lambda e, last=last: last(e).then_inc(c.s[sem], 1))
            return c.inc(sem)

        def W(eng, sem, val=None):
            if val is None:
                sem, val = sem
            if val > 0:
                emit(eng, lambda e, s=sem, v=val: e.wait_ge(c.s[s], v))

        _CHMAP = {"tensor": "CHT", "vector": "CHV", "scalar": "CHS",
                  "gpsimd": "CHP"}

        def chain(eng):
            val = retag(eng, _CHMAP[eng])
            W(eng, _CHMAP[eng], val)

        ev_unit = [0] * (SPC * H)
        proj_done = []

        for j in (0, 1):
            for dst, srcd in [(lambda jj: qx[jj][64:65, :], "onesrowb_d"),
                              (lambda jj: qx[jj][65:66, :], "onesrowb_d"),
                              (lambda jj: kx[jj][64:65, :], "miota_hi_d"),
                              (lambda jj: kx[jj][65:66, :], "miota_lo_d"),
                              (lambda jj: kx[jj][66:67, :], "onesrowb_d"),
                              (lambda jj: kx[jj][67:68, :], "onesrowb_d")]:
                emit("sync", lambda sy, j=j, dst=dst, srcd=srcd: sy.dma_start(
                    dst(j), tb[srcd][:]).then_inc(c.s["LA"], 16))
                c.inc("LA", 16)
            emit("sync", lambda sy, j=j: sy.dma_start(
                bass.AP(vx[j], 64, [[65 * 16, 128], [65, 16]]),
                tb["onestile_d"][:],
                allow_slow_non_contiguous=True).then_inc(c.s["LA"], 16))
            c.inc("LA", 16)
        for k in range(KC):
            emit("gpsimd", lambda g, k=k: g.dma_start(
                wpj[:, k * 2048:(k + 1) * 2048],
                wproj_d[k * 128:(k + 1) * 128, :]).then_inc(c.s["LW"], 16))
            c.inc("LW", 16)
        lw_wpj = c.n["LW"]
        W("sync", "SS", v_done)
        W("gpsimd", "SS", v_done)

        p1_free = [("AC", 0), ("AC", 0)]
        pb_free = [("AC", 0)] * 4
        probs_free = [("MM", 0)] * 4
        p2_free = ("EV", 0)
        dms_free = [("AC", 0), ("AC", 0)]
        probsd_free = ("MM", 0)
        dclamp_free = ("AC", 0)
        proj_ps_free = [("EV", 0), ("EV", 0)]

        for s in range(SPC):
            for h in range(H):
                u = s * H + h
                j = u % 2
                kc2 = h // 2
                hrow = (h % 2) * 64

                if u >= 2:
                    W("sync", "EV", ev_unit[u - 2])
                    W("gpsimd", "EV", ev_unit[u - 2])
                emit("sync", lambda sy, j=j, h=h, s=s: sy.dma_start(
                    qx[j][0:64, :], qf[h * 64:(h + 1) * 64, s * SEQ:(s + 1) * SEQ]
                    ).then_inc(c.s["LA"], 16)); c.inc("LA", 16)
                emit("sync", lambda sy, j=j: sy.dma_start(
                    qx[j][66:67, :], tb["negmiota_hi_d"][:]).then_inc(c.s["LA"], 16))
                c.inc("LA", 16)
                emit("sync", lambda sy, j=j: sy.dma_start(
                    qx[j][67:68, :], tb["negmiota_lo_d"][:]).then_inc(c.s["LA"], 16))
                c.inc("LA", 16)
                emit("sync", lambda sy, j=j, h=h, s=s: sy.dma_start(
                    kx[j][0:64, :], kf[h * 64:(h + 1) * 64, s * SEQ:(s + 1) * SEQ]
                    ).then_inc(c.s["LA"], 16)); c.inc("LA", 16)
                la_u = c.n["LA"]
                emit("gpsimd", lambda g, j=j, h=h, s=s: g.dma_start(
                    bass.AP(vx[j], 0, [[65 * 16, 128], [65, 16], [1, 64]]),
                    bass.AP(vt.tensor, s * SEQ * D + h * 64,
                            [[D, 128], [128 * D, 16], [1, 64]])
                    ).then_inc(c.s["LG"], 16))
                lg_u = c.inc("LG", 16)
                W("tensor", "LA", la_u)
                W("tensor", "LG", lg_u)

                for qi in range(16):
                    span = (qi + 1) * 128
                    qp = qi % 2
                    ndiag = qi * 128

                    def PA(off, n, qp=qp):
                        return PS[:, qp * 2048 + off: qp * 2048 + off + n]
                    sem, val = p1_free[qp]
                    W("tensor", sem, val)
                    t0 = 0
                    while t0 < span:
                        b = min(t0 + 512, span)
                        emit("tensor", lambda t, j=j, qi=qi, a=t0, b=b, qp=qp:
                             t.matmul(PA(a, b - a, qp),
                                      qx[j][0:68, qi * 128:(qi + 1) * 128],
                                      kx[j][0:68, a:b], start=True, stop=True))
                        t0 = b
                    mmv = retag("tensor", "MM")
                    W("vector", "MM", mmv)
                    sem, val = dms_free[qp]
                    W("vector", sem, val)
                    emit("vector", lambda v, qi=qi, qp=qp, ndiag=ndiag:
                         v.tensor_tensor(dmsb[qp][:], PA(ndiag, 128, qp),
                                         tb["diagmask"][:], ALU.add))
                    if ndiag > 0:
                        emit("vector", lambda v, qi=qi, qp=qp, ndiag=ndiag:
                             v.reduce_max(mx4[:, 0:1], PA(0, ndiag, qp), axis=AX.X))
                    chain("vector")
                    emit("vector", lambda v, qp=qp:
                         v.reduce_max(mx4[:, 1:2], dmsb[qp][:], axis=AX.X))
                    chain("vector")
                    emit("vector", lambda v, qp=qp, n=(2 if ndiag > 0 else None):
                         v.reduce_max(nm1[:, qp:qp + 1],
                                      mx4[:, (0 if n else 1):2], axis=AX.X,
                                      negate=True))
                    ev_nm = retag("vector", "EV")
                    W("tensor", "EV", ev_nm)
                    t0 = 0
                    while t0 < span:
                        b = min(t0 + 512, span)
                        emit("tensor", lambda t, j=j, qi=qi, a=t0, b=b, qp=qp:
                             t.matmul(PA(a, b - a, qp),
                                      qx[j][0:68, qi * 128:(qi + 1) * 128],
                                      kx[j][0:68, a:b], start=True, stop=True))
                        t0 = b
                    mmv2 = retag("tensor", "MM")
                    W("scalar", "MM", mmv2)
                    if ndiag > 0:
                        emit("scalar", lambda sc, qp=qp, ndiag=ndiag:
                             sc.activation(trash_e[:, 0:ndiag], PA(0, ndiag, qp),
                                           AF.Exp, bias=nm1[:, qp:qp + 1], scale=1.0,
                                           accum_out=scol[:, 0:1]))
                    emit("scalar", lambda sc, qp=qp:
                         sc.activation(trash_e[:, 0:128], dmsb[qp][:], AF.Exp,
                                       bias=nm1[:, qp:qp + 1], scale=1.0,
                                       accum_out=scol[:, 1:2]))
                    ac_e = retag("scalar", "AC")
                    p1_free[qp] = ("AC", ac_e)
                    dms_free[qp] = ("AC", ac_e)
                    W("vector", "AC", ac_e)
                    emit("vector", lambda v, qp=qp, first=(ndiag == 0):
                         v.reduce_sum(nm1[:, 2 + qp:3 + qp],
                                      scol[:, (1 if first else 0):2], axis=AX.X))
                    ev_s = retag("vector", "EV")
                    W("scalar", "EV", ev_s)
                    emit("scalar", lambda sc, qp=qp: sc.activation(
                        scol[:, 6 + qp:7 + qp], nm1[:, 2 + qp:3 + qp], AF.Ln,
                        bias=0.0, scale=1.0))
                    ac_l = retag("scalar", "AC")
                    W("vector", "AC", ac_l)
                    emit("vector", lambda v, qi=qi, qp=qp: v.tensor_tensor(
                        nmls[:, qi:qi + 1], nm1[:, qp:qp + 1], scol[:, 6 + qp:7 + qp],
                        ALU.subtract))
                    retag("vector", "EV")

                mm_passA_last = c.n["MM"]
                W("vector", "EV", c.n["EV"])
                emit("vector", lambda v: v.tensor_tensor(
                    vcols[:], nmls[:], tb["negmiotacols"][:], ALU.add))
                ev_vc = retag("vector", "EV")
                W("tensor", "EV", ev_vc)
                sem, val = p1_free[0]
                W("tensor", sem, val)
                emit("tensor", lambda t: t.transpose(
                    PS[0:16, 1024:1152], vcols[:], tb["eye128f"][:]
                    ).then_inc(c.s["MM"], 1))
                mm_tr = c.inc("MM")
                W("vector", "MM", mm_tr)
                emit("vector", lambda v: v.tensor_copy(nmrow[:], PS[0:16, 1024:1152]))
                ev_nr = retag("vector", "EV")
                p1_free[0] = ("EV", ev_nr)
                W("gpsimd", "EV", ev_nr)
                emit("gpsimd", lambda g: g.dma_start(
                    rowvf[0:1, :], bass.AP(nmrow, 0, [[128, 16], [1, 128]])
                    ).then_inc(c.s["LG"], 16))
                lg_r = c.inc("LG", 16)
                W("vector", "LG", lg_r)
                emit("vector", lambda v: v.tensor_copy(row66[0:1, :], rowvf[0:1, :]))
                ev66 = retag("vector", "EV")
                W("vector", "EV", ev66)
                emit("vector", lambda v: v.tensor_tensor(
                    row67[0:1, :], rowvf[0:1, :], row66[0:1, :], ALU.subtract))
                ev_rows = retag("vector", "EV")
                W("gpsimd", "EV", ev_rows)
                W("gpsimd", "MM", mm_passA_last)
                emit("gpsimd", lambda g, j=j: g.dma_start(
                    qx[j][66:67, :], row66[0:1, :]).then_inc(c.s["LG"], 16))
                c.inc("LG", 16)
                emit("gpsimd", lambda g, j=j: g.dma_start(
                    qx[j][67:68, :], row67[0:1, :]).then_inc(c.s["LG"], 16))
                lg_r65 = c.inc("LG", 16)

                W("tensor", "LG", lg_r65)
                sem, val = p2_free
                W("tensor", sem, val)
                W("tensor", p1_free[0])
                W("tensor", p1_free[1])
                pb_idx = 0
                mm_last_av = 0
                for ki in range(16):
                    for qc in range(ki // 4, 4):
                        a = max(qc * 512, ki * 128)
                        b = (qc + 1) * 512
                        ln = b - a
                        pb = pb_idx % 4
                        pb_idx += 1
                        sem, val = pb_free[pb]
                        W("tensor", sem, val)
                        emit("tensor", lambda t, j=j, ki=ki, a=a, b=b, pb=pb:
                             t.matmul(PBs(pb, b - a),
                                      kx[j][0:68, ki * 128:(ki + 1) * 128],
                                      qx[j][0:68, a:b], start=True, stop=True
                                      ).then_inc(c.s["MM"], 1))
                        mmv = c.inc("MM")
                        first = (qc == ki // 4)
                        stop = (ki == min(4 * qc + 3, 15))
                        if first:
                            W("vector", "MM", mmv)
                            sem, val = dclamp_free
                            W("vector", sem, val)
                            emit("vector", lambda v, pb=pb: v.tensor_scalar(
                                dclamp[:], PBs(pb, 128), 30.0, None, ALU.min))
                            ev_cl = retag("vector", "EV")
                            W("scalar", "EV", ev_cl)
                        W("scalar", "MM", mmv)
                        sem, val = probs_free[pb]
                        W("scalar", sem, val)
                        if first:
                            emit("scalar", lambda sc, pb=pb:
                                 sc.activation(probs[pb][:, 0:128], dclamp[:],
                                               AF.Exp, bias=0.0, scale=1.0))
                            if ln > 128:
                                emit("scalar", lambda sc, pb=pb, ln=ln:
                                     sc.activation(probs[pb][:, 128:ln],
                                                   PS[:, pb * 512 + 128: pb * 512 + ln],
                                                   AF.Exp, bias=0.0, scale=1.0))
                        else:
                            emit("scalar", lambda sc, pb=pb, ln=ln:
                                 sc.activation(probs[pb][:, 0:ln], PBs(pb, ln),
                                               AF.Exp, bias=0.0, scale=1.0))
                        ac = retag("scalar", "AC")
                        pb_free[pb] = ("AC", ac)
                        if first:
                            dclamp_free = ("AC", ac)
                            W("vector", "AC", ac)
                            sem, val = probsd_free
                            W("vector", sem, val)
                            emit("vector", lambda v, pb=pb: v.tensor_tensor(
                                probsd[:], probs[pb][:, 0:128], tb["tri128"][:],
                                ALU.mult))
                            ev_d = retag("vector", "EV")
                            W("tensor", "EV", ev_d)
                            if not _DBG_SKIP_DIAG:
                                emit("tensor", lambda t, j=j, ki=ki, a=a:
                                     t.matmul(P2s(a, a + 128),
                                              vx[j][:, ki * 65:(ki + 1) * 65],
                                              probsd[:, 0:128], start=(ki == 0),
                                              stop=(4 * (a // 512) + 3 == ki)
                                              ).then_inc(c.s["MM"], 1))
                                probsd_free = ("MM", c.inc("MM"))
                                mm_last_av = c.n["MM"]
                            elif ki == 0:
                                emit("tensor", lambda t, j=j, ki=ki, a=a:
                                     t.matmul(P2s(a, a + 128),
                                              vx[j][:, ki * 65:(ki + 1) * 65],
                                              probsd[:, 0:128], start=True,
                                              stop=False).then_inc(c.s["MM"], 1))
                                probsd_free = ("MM", c.inc("MM"))
                                mm_last_av = c.n["MM"]
                            if ln > 128:
                                W("tensor", "AC", ac)
                                emit("tensor",
                                     lambda t, j=j, ki=ki, a=a, b=b, pb=pb, stop=stop:
                                     t.matmul(P2s(a + 128, b),
                                              vx[j][:, ki * 65:(ki + 1) * 65],
                                              probs[pb][:, 128:b - a],
                                              start=(ki == 0), stop=stop
                                              ).then_inc(c.s["MM"], 1))
                                probs_free[pb] = ("MM", c.inc("MM"))
                                mm_last_av = c.n["MM"]
                            else:
                                probs_free[pb] = ("MM", c.n["MM"])
                        else:
                            W("tensor", "AC", ac)
                            emit("tensor", lambda t, j=j, ki=ki, a=a, b=b, pb=pb, stop=stop:
                                 t.matmul(P2s(a, b),
                                          vx[j][:, ki * 65:(ki + 1) * 65],
                                          probs[pb][:, 0:b - a], start=(ki == 0),
                                          stop=stop).then_inc(c.s["MM"], 1))
                            probs_free[pb] = ("MM", c.inc("MM"))
                            mm_last_av = c.n["MM"]

                W("vector", "MM", mm_last_av)
                emit("vector", lambda v, kc2=kc2, hrow=hrow: v.tensor_copy(
                    attn_fm[hrow:hrow + 64, kc2 * SEQ:(kc2 + 1) * SEQ], P2evac()))
                ev_unit[u] = retag("vector", "EV")
                p2_free = ("EV", ev_unit[u])

            if s == 0:
                W("tensor", "LW", lw_wpj)
            for i in range(16):
                gi = s * 16 + i
                pr = i % 2
                base = pr * 2048
                if i < 2:
                    W("tensor", "EV", ev_unit[s * H + H - 1])
                else:
                    sem, val = proj_ps_free[pr]
                    W("tensor", sem, val)
                for k in range(KC):
                    for tc in range(4):
                        emit("tensor", lambda t, k=k, i=i, tc=tc, base=base:
                             t.matmul(PS[:, base + tc * 512: base + (tc + 1) * 512],
                                      attn_fm[:, k * SEQ + i * 128:k * SEQ + (i + 1) * 128],
                                      wpj[:, k * 2048 + tc * 512:k * 2048 + (tc + 1) * 512],
                                      start=(k == 0), stop=(k == KC - 1)))
                mm_p = retag("tensor", "MM")
                W("vector", "MM", mm_p)
                if gi >= 2:
                    W("vector", "SS", proj_done[gi - 2])
                emit("vector", lambda v, pr=pr, base=base: v.tensor_copy(
                    pstg[pr][:], PS[:, base:base + 2048]))
                ev_p = retag("vector", "EV")
                proj_ps_free[pr] = ("EV", ev_p)
                W("sync", "EV", ev_p)
                emit("sync", lambda sy, pr=pr, gi=gi: sy.dma_start(
                    dout[gi * 128:(gi + 1) * 128, :], pstg[pr][:]
                    ).then_inc(c.s["SS"], 16))
                proj_done.append(c.inc("SS", 16))
            evp = c.n["EV"]
            p1_free = [("EV", evp), ("EV", evp)]
            pb_free = [("EV", evp)] * 4
            p2_free = ("EV", evp)
            proj_ps_free = [("EV", evp), ("EV", evp)]
            dms_free = [("AC", c.n["AC"]), ("AC", c.n["AC"])]
            probsd_free = ("MM", c.n["MM"])
            dclamp_free = ("AC", c.n["AC"])
            probs_free = [("MM", c.n["MM"])] * 4

        if "passA" in _TAPS:
            for nm_, th_ in [("nmls", nmls), ("rowvf", rowvf), ("mx4", mx4),
                             ("scol", scol), ("nm1", nm1), ("row66", row66),
                             ("row67", row67), ("dmsb1", dmsb[1]), ("dmsb0", dmsb[0]),
                             ("trashe", trash_e)]:
                dbg_ = nc.dram_tensor("dbg_" + nm_, list(th_.shape), th_.dtype,
                                      kind="ExternalOutput").ap()
                W("gpsimd", "SS", proj_done[-1] if proj_done else 0)
                emit("gpsimd", lambda g, dbg_=dbg_, th_=th_: g.dma_start(
                    dbg_, th_[:]).then_inc(c.s["SG"], 16))
                c.inc("SG", 16)
        if "afm" in _TAPS:
            afm_dbg = nc.dram_tensor("dbg_afm", [128, KC * SEQ], BF16,
                                     kind="ExternalOutput").ap()
            W("gpsimd", "SS", proj_done[-1])
            emit("gpsimd", lambda g: g.dma_start(afm_dbg[:], attn_fm[:]
                                                 ).then_inc(c.s["SG"], 16))
            c.inc("SG", 16)

        @c.block.tensor
        def _(t):
            for f in ops["tensor"]:
                f(t)

        @c.block.vector
        def _(v):
            for f in ops["vector"]:
                f(v)

        @c.block.scalar
        def _(sc):
            for f in ops["scalar"]:
                f(sc)

        @c.block.sync
        def _(sy):
            for f in ops["sync"]:
                f(sy)

        @c.block.gpsimd
        def _(g):
            for f in ops["gpsimd"]:
                f(g)
    return proj_done



def _phase_ffn(c, tb, xn_d, xn_done, wg, wu, wd, fu, dout):
    nc = c.nc
    u_ = _uid()
    st = contextlib.ExitStack()
    st_gu = contextlib.ExitStack()
    with st:
        xnfm = st_gu.enter_context(nc.sbuf_tensor(f"ff{u_}_xnfm", [128, KC * T], BF16))
        stg = [st_gu.enter_context(nc.sbuf_tensor(f"ff{u_}_stg{j}", [128, D], BF16)) for j in (0, 1)]
        with (nc.psum_tensor(f"ff{u_}_trp0", [128, 1024], BF16) as trp0,
              nc.psum_tensor(f"ff{u_}_trp1", [128, 1024], BF16) as trp1):
            mm_fm, ev_fm = _emit_fm_rebuild(c, tb, xn_d, xn_done, xnfm, stg, [trp0, trp1])
        psA = st.enter_context(nc.psum_tensor(f"ff{u_}_psA", [128, 4096], F32))
        wt = [st_gu.enter_context(nc.sbuf_tensor(f"ff{u_}_wt{j}", [128, KC * 128], BF16)) for j in (0, 1)]
        gbuf = st_gu.enter_context(nc.sbuf_tensor(f"ff{u_}_g", [128, T], BF16))
        ubuf = st_gu.enter_context(nc.sbuf_tensor(f"ff{u_}_u", [128, T], BF16))
        fub = [st_gu.enter_context(nc.sbuf_tensor(f"ff{u_}_fu{j}", [128, T], BF16)) for j in (0, 1)]

        lw0 = c.n["LW"]
        mm0 = c.n["MM"]
        ss0 = c.n["SS"]
        evac_done = {}

        @c.block.gpsimd
        def _(g):
            for mi in range(2 * MI):
                m = mi // 2
                src = wg if mi % 2 == 0 else wu
                if mi >= 2:
                    g.wait_ge(c.s["MM"], mm0 + 2 * (mi - 1))
                g.dma_start(wt[mi % 2][:], src[m]).then_inc(c.s["LW"], 16)

        mmh = {}
        for mi in range(2 * MI):
            for half in range(2):
                mmh[(mi, half)] = mm0 + 2 * mi + half + 1
        ac0 = c.n["AC"]
        ev0 = c.n["EV"]
        nac = 0
        nev = 0
        for mi in range(2 * MI):
            for half in range(2):
                if mi % 2 == 0:
                    nac += 1
                    evac_done[(mi, half)] = ("AC", ac0 + nac)
                else:
                    nev += 1
                    evac_done[(mi, half)] = ("EV", ev0 + nev)
            if mi % 2 == 1:
                nev += 2
        ev_fuse = {}
        nev2 = 0
        for mi2 in range(MI):
            nev2 += 2 + 3
        ev_seq = ev0
        ev_up = {}
        ev_fu = {}
        for m in range(MI):
            ev_up[(m, 0)] = ev_seq + 1
            ev_up[(m, 1)] = ev_seq + 2
            ev_fu[m] = ev_seq + 3
            ev_seq += 3
        for mi in range(2 * MI):
            for half in range(2):
                if mi % 2 == 1:
                    evac_done[(mi, half)] = ("EV", ev_up[(mi // 2, half)])

        def ss_fu(m):
            return ss0 + 16 * 2 * (m + 1)

        @c.block.tensor
        def _(t):
            t.wait_ge(c.s["EV"], ev_fm)
            for mi in range(2 * MI):
                t.wait_ge(c.s["LW"], lw0 + 16 * (mi + 1))
                for half in range(2):
                    if mi > 0:
                        sem, val = evac_done[(mi - 1, half)]
                        t.wait_ge(c.s[sem], val)
                    for k in range(KC):
                        lt = wt[mi % 2][:, k * 128:(k + 1) * 128]
                        for tc in range(4):
                            col = half * 2048 + tc * 512
                            mm = t.matmul(psA[:, col:col + 512], lt,
                                          xnfm[:, k * T + col:k * T + col + 512],
                                          start=(k == 0), stop=(k == KC - 1))
                    mm.then_inc(c.s["MM"], 1)

        @c.block.scalar
        def _(s):
            for m in range(MI):
                mi = 2 * m
                for half in range(2):
                    s.wait_ge(c.s["MM"], mmh[(mi, half)])
                    if m >= 1:
                        s.wait_ge(c.s["EV"], ev_fu[m - 1])
                    sl = slice(half * 2048, (half + 1) * 2048)
                    s.activation(gbuf[:, sl], psA[:, sl], AF.Silu, bias=0.0,
                                 scale=tb["sg"][:, m:m + 1]).then_inc(c.s["AC"], 1)

        @c.block.vector
        def _(v):
            for m in range(MI):
                mi = 2 * m + 1
                for half in range(2):
                    v.wait_ge(c.s["MM"], mmh[(mi, half)])
                    if m >= 1:
                        v.wait_ge(c.s["EV"], ev_fu[m - 1])
                    sl = slice(half * 2048, (half + 1) * 2048)
                    v.tensor_scalar(ubuf[:, sl], psA[:, sl], tb["su"][:, m:m + 1],
                                    None, ALU.mult).then_inc(c.s["EV"], 1)
                v.wait_ge(c.s["AC"], ac0 + 2 * (m + 1))
                v.wait_ge(c.s["CHV"], c.n["CHV"])
                v.wait_ge(c.s["EV"], ev_up[(m, 1)])
                if m >= 2:
                    v.wait_ge(c.s["SS"], ss_fu(m - 2))
                fb = fub[m % 2]
                v.tensor_tensor(fb[:, 0:2048], gbuf[:, 0:2048], ubuf[:, 0:2048],
                                ALU.mult)
                v.tensor_tensor(fb[:, 2048:4096], gbuf[:, 2048:4096],
                                ubuf[:, 2048:4096], ALU.mult).then_inc(c.s["EV"], 1)

        @c.block.sync
        def _(sy):
            for m in range(MI):
                sy.wait_ge(c.s["EV"], ev_fu[m])
                fb = fub[m % 2]
                for part in range(2):
                    dst = bass.AP(fu.tensor,
                                  part * 16 * (128 * MI * 128) + m * 128,
                                  [[MI * 128, 128], [128 * MI * 128, 16], [1, 128]])
                    srcap = bass.AP(fb, part * 2048,
                                    [[T, 128], [128, 16], [1, 128]])
                    sy.dma_start(dst, srcap).then_inc(c.s["SS"], 16)

        c.n["LW"] = lw0 + 16 * 2 * MI
        c.n["MM"] = mm0 + 4 * MI
        c.n["AC"] = ac0 + 2 * MI
        c.n["EV"] = ev_seq
        c.n["SS"] = ss0 + 16 * 2 * MI

        c.fence()
        st_gu.close()

        wdb = st.enter_context(nc.sbuf_tensor(f"ff{u_}_wdb", [128, MI * 1024], BF16))
        fulhs = [st.enter_context(nc.sbuf_tensor(f"ff{u_}_fl{j}", [128, MI * 128], BF16)) for j in (0, 1)]
        dstage = [st.enter_context(nc.sbuf_tensor(f"ff{u_}_ds{j}", [128, 1024], F32)) for j in (0, 1)]
        sdb = st.enter_context(nc.sbuf_tensor(f"ff{u_}_sdb", [128, D], F32))
        delta_done = [0] * NTT

        lg_sd = c.inc("LG", 16)

        @c.block.gpsimd
        def _(g):
            g.dma_start(sdb[:], tb["sdownb_d"][:]).then_inc(c.s["LG"], 16)

        for oh in range(2):
            lw0 = c.n["LW"]
            mm0 = c.n["MM"]
            ev0 = c.n["EV"]
            ss0 = c.n["SS"]

            la0 = c.n["LG"]

            @c.block.gpsimd
            def _(g, oh=oh, lw0=lw0, mm0=mm0):
                if oh == 1:
                    g.wait_ge(c.s["MM"], mm0)
                for ki in range(MI):
                    g.dma_start(wdb[:, ki * 1024:(ki + 1) * 1024],
                                wd[ki * 128:(ki + 1) * 128,
                                   oh * 1024:(oh + 1) * 1024]).then_inc(c.s["LW"], 16)
                for i in range(NTT):
                    if i >= 2:
                        g.wait_ge(c.s["MM"], mm0 + i - 1)
                    g.dma_start(fulhs[i % 2][:], fu[i].rearrange("p mi c -> p (mi c)")
                                ).then_inc(c.s["LG"], 16)

            @c.block.tensor
            def _(t, oh=oh, lw0=lw0, la0=la0, ev0=ev0):
                t.wait_ge(c.s["LW"], lw0 + 16 * MI)
                for i in range(NTT):
                    t.wait_ge(c.s["LG"], la0 + 16 * (i + 1))
                    base = (i % 2) * 1024
                    if i >= 2:
                        t.wait_ge(c.s["EV"], ev0 + i - 1)
                    for ki in range(MI):
                        lt = fulhs[i % 2][:, ki * 128:(ki + 1) * 128]
                        for tc in range(2):
                            mm = t.matmul(psA[:, base + tc * 512:base + (tc + 1) * 512],
                                          lt, wdb[:, ki * 1024 + tc * 512:
                                                  ki * 1024 + (tc + 1) * 512],
                                          start=(ki == 0), stop=(ki == MI - 1))
                    mm.then_inc(c.s["MM"], 1)

            @c.block.vector
            def _(v, oh=oh, mm0=mm0, ev0=ev0, ss0=ss0):
                if oh == 0:
                    v.wait_ge(c.s["LG"], lg_sd)
                for i in range(NTT):
                    v.wait_ge(c.s["MM"], mm0 + i + 1)
                    if i >= 2:
                        v.wait_ge(c.s["SS"], ss0 + 16 * (i - 1))
                    base = (i % 2) * 1024
                    v.tensor_tensor(dstage[i % 2][:], psA[:, base:base + 1024],
                                    sdb[:, oh * 1024:(oh + 1) * 1024], ALU.mult
                                    ).then_inc(c.s["EV"], 1)

            @c.block.sync
            def _(sy, oh=oh, ev0=ev0):
                for i in range(NTT):
                    sy.wait_ge(c.s["EV"], ev0 + i + 1)
                    sy.dma_start(dout[i * 128:(i + 1) * 128,
                                      oh * 1024:(oh + 1) * 1024],
                                 dstage[i % 2][:]).then_inc(c.s["SS"], 16)
                    if oh == 1:
                        delta_done[i] = c.n["SS"] + 16 * (i + 1)

            c.n["LW"] = lw0 + 16 * MI
            c.n["LG"] = la0 + 16 * NTT
            c.n["MM"] = mm0 + NTT
            c.n["EV"] = ev0 + NTT
            c.n["SS"] = ss0 + 16 * NTT
        c.fence()
    return delta_done



def _phase_ssm(c, tb, xn_d, xn_done, L):
    nc = c.nc
    u_ = _uid()
    wz, wbc, wdt = L["wz"], L["wbc"], L["wdt"]
    zt, xst, bfm, cfm, btm = L["zt"], L["xst"], L["bfm"], L["cfm"], L["btm"]
    yt_d, y2_d, o1, dout = L["yt_d"], L["y2_d"], L["o1"], L["dout"]
    wout, wo = L["wout"], L["wo"]

    ops = {k: [] for k in ("tensor", "vector", "scalar", "sync", "gpsimd")}

    def emit(eng, fn):
        ops[eng].append(fn)

    def retag(eng, sem):
        last = ops[eng].pop()
        emit(eng, lambda e, last=last: last(e).then_inc(c.s[sem], 1))
        return c.inc(sem)

    def W(eng, sem, val=None):
        if val is None:
            sem, val = sem
        if val > 0:
            emit(eng, lambda e, s=sem, v=val: e.wait_ge(c.s[s], v))

    _CM = {"tensor": "CHT", "vector": "CHV", "scalar": "CHS", "gpsimd": "CHP"}

    def chain(eng):
        val = retag(eng, _CM[eng])
        W(eng, _CM[eng], val)

    def flush():
        for eng in ("tensor", "vector", "scalar", "sync", "gpsimd"):
            sec = ops[eng]
            if sec:
                @getattr(c.block, eng)
                def _(e, sec=sec):
                    for f in sec:
                        f(e)
            ops[eng] = []

    st = contextlib.ExitStack()
    st_ab = contextlib.ExitStack()
    with st:
        segL = st_ab.enter_context(nc.sbuf_tensor(f"sm{u_}_segL", [33, T], BF16))
        dtfm = st_ab.enter_context(nc.sbuf_tensor(f"sm{u_}_dtfm", [SH, T], BF16))
        dtdec = st_ab.enter_context(nc.sbuf_tensor(f"sm{u_}_dtdec", [SH, T], BF16))
        expac = st_ab.enter_context(nc.sbuf_tensor(f"sm{u_}_expac", [SH, T], BF16))
        acsb = st_ab.enter_context(nc.sbuf_tensor(f"sm{u_}_acsb", [SH, T], BF16))
        onesb = st_ab.enter_context(nc.sbuf_tensor(f"sm{u_}_ones", [1, 128], BF16))

        emit("vector", lambda v: v.memset(onesb[:], 1.0))
        retag("vector", "EV")
        emit("vector", lambda v: v.memset(segL[32:33, :], 1.0))
        retag("vector", "EV")

        st_a = contextlib.ExitStack()
        st_a1 = contextlib.ExitStack()
        trpp = [st_a.enter_context(nc.psum_tensor(f"sm{u_}_trq{j}", [128, 1024], BF16)) for j in (0, 1)]
        psA = st_a.enter_context(nc.psum_tensor(f"sm{u_}_psA", [128, 2048], F32))
        acc1 = st_a.enter_context(nc.sbuf_tensor(f"sm{u_}_acc1", [128, SEQ], BF16))
        acc2 = st_a.enter_context(nc.sbuf_tensor(f"sm{u_}_acc2", [128, SEQ], BF16))
        xnfm = st_a1.enter_context(nc.sbuf_tensor(f"sm{u_}_xnfm", [128, KC * T], BF16))
        flush()
        with (nc.sbuf_tensor(f"sm{u_}_stg0", [128, D], BF16) as stg0,
              nc.sbuf_tensor(f"sm{u_}_stg1", [128, D], BF16) as stg1,
              nc.psum_tensor(f"sm{u_}_trp0", [128, 1024], BF16) as trp0,
              nc.psum_tensor(f"sm{u_}_trp1", [128, 1024], BF16) as trp1):
            mm_fm, ev_fm = _emit_fm_rebuild(c, tb, xn_d, xn_done, xnfm, [stg0, stg1],
                                            [trp0, trp1])
        wt = [st_a1.enter_context(nc.sbuf_tensor(f"sm{u_}_wt{j}", [128, KC * 128], BF16)) for j in (0, 1)]
        ztp0 = st_a1.enter_context(nc.sbuf_tensor(f"sm{u_}_ztp0", [128, 16 * 128], BF16))
        ztp = [ztp0, ztp0]
        cvin0 = st_a1.enter_context(nc.sbuf_tensor(f"sm{u_}_cvin0", [128, 2051], BF16))
        cvin = [cvin0, cvin0]
        cvout = st_a1.enter_context(nc.sbuf_tensor(f"sm{u_}_cvo", [128, SEQ], BF16))

        emit("vector", lambda v: v.memset(cvin[0][:, 0:3], 0.0))
        retag("vector", "EV")
        W("tensor", "EV", c.n["EV"])

        NM = 16 + CCC + 1
        NM_RUN = NM if _SSM_NM is None else _SSM_NM
        pa_free = ("EV", 0)
        ztp_free = [("SS", 0)]
        cvin_free = [("EV", 0)]
        trpq_free = [("EV", 0), ("EV", 0)]
        tq_idx = [0]
        cvout_free = ("MM", 0)
        acc_free = ("AC", 0)
        mm_mat = {}

        def emit_transposes_and_scatter(srcbuf, buf, dst_tensor, dst_off,
                                        dst_pstride, deps):
            buf = 0
            sem2, val2 = ztp_free[buf]
            W("tensor", sem2, val2)
            for dep in deps:
                W("tensor", dep)
            mm_t = 0
            for ti in range(16):
                tq = tq_idx[0] % 2
                tq_idx[0] += 1
                W("tensor", trpq_free[tq])
                emit("tensor", lambda t, ti=ti, tq=tq, srcbuf=srcbuf:
                     t.transpose(trpp[tq][:, 0:128],
                                 srcbuf[:, ti * 128:(ti + 1) * 128],
                                 tb["eye128b"][:]))
                mm_t = retag("tensor", "MM")
                W("vector", "MM", mm_t)
                emit("vector", lambda v, ti=ti, tq=tq, buf=buf:
                     v.tensor_copy(ztp[buf][:, ti * 128:(ti + 1) * 128],
                                   trpp[tq][:, 0:128]))
                ev_t = retag("vector", "EV")
                trpq_free[tq] = ("EV", ev_t)
            W("sync", "EV", c.n["EV"])
            emit("sync", lambda sy, dst_tensor=dst_tensor, dst_off=dst_off,
                 dst_pstride=dst_pstride, buf=buf: sy.dma_start(
                     bass.AP(dst_tensor, dst_off,
                             [[dst_pstride, 128], [128 * dst_pstride, 16], [1, 128]]),
                     bass.AP(ztp[buf], 0, [[16 * 128, 128], [128, 16], [1, 128]])
                     ).then_inc(c.s["SS"], 16))
            ztp_free[buf] = ("SS", c.inc("SS", 16))
            return mm_t

        W("gpsimd", "EV", ev_fm)
        for m in range(NM_RUN):
            is_dt = (m == NM - 1)
            if m >= 2:
                W("gpsimd", "MM", mm_mat[(m - 2, 1)])
            if m < 16:
                emit("gpsimd", lambda g, m=m: g.dma_start(wt[m % 2][:], wz[m]
                                                          ).then_inc(c.s["LW"], 16))
            elif not is_dt:
                emit("gpsimd", lambda g, m=m: g.dma_start(wt[m % 2][:], wbc[m - 16]
                                                          ).then_inc(c.s["LW"], 16))
            else:
                emit("gpsimd", lambda g: g.dma_start(
                    wt[m % 2][:, 0:KC * SH],
                    wdt.rearrange("p k s -> p (k s)")).then_inc(c.s["LW"], 16))
            lw_m = c.inc("LW", 16)
            W("tensor", "LW", lw_m)

            for half in range(2):
                s_ = half
                W("tensor", pa_free)
                for k in range(KC):
                    for tc in range(4):
                        col = half * 2048 + tc * 512
                        if is_dt:
                            emit("tensor", lambda t, m=m, k=k, col=col, tc=tc:
                                 t.matmul(psA[0:SH, tc * 512:(tc + 1) * 512],
                                          wt[m % 2][:, k * SH:(k + 1) * SH],
                                          xnfm[:, k * T + col:k * T + col + 512],
                                          start=(k == 0), stop=(k == KC - 1)))
                        else:
                            emit("tensor", lambda t, m=m, k=k, col=col, tc=tc:
                                 t.matmul(psA[:, tc * 512:(tc + 1) * 512],
                                          wt[m % 2][:, k * 128:(k + 1) * 128],
                                          xnfm[:, k * T + col:k * T + col + 512],
                                          start=(k == 0), stop=(k == KC - 1)))
                mm_v = retag("tensor", "MM")
                mm_mat[(m, half)] = mm_v

                if m < 16:
                    W("vector", "MM", mm_v)
                    W("vector", cvout_free)
                    emit("vector", lambda v: v.tensor_copy(cvout[:], psA[:]))
                    ev_z = retag("vector", "EV")
                    pa_free = ("EV", ev_z)
                    mm_t = emit_transposes_and_scatter(
                        cvout, half, zt.tensor, s_ * SEQ * D + m * 128, D,
                        [("EV", ev_z)])
                    cvout_free = ("MM", mm_t)
                    mm_mat[(m, half)] = mm_t
                elif not is_dt:
                    cc = m - 16
                    jj = 0
                    W("vector", "MM", mm_v)
                    W("vector", cvin_free[jj])
                    emit("vector", lambda v, jj=jj: v.tensor_copy(
                        cvin[jj][:, 3:2051], psA[:]))
                    ev_c = retag("vector", "EV")
                    pa_free = ("EV", ev_c)
                    W("vector", "EV", ev_c)
                    W("vector", acc_free)
                    emit("vector", lambda v, jj=jj, cc=cc: v.tensor_scalar(
                        acc1[:], cvin[jj][:, 0:2048],
                        tb["convw"][:, cc * 4:cc * 4 + 1], None, ALU.mult))
                    chain("vector")
                    emit("vector", lambda v, jj=jj, cc=cc: v.scalar_tensor_tensor(
                        acc2[:], cvin[jj][:, 1:2049],
                        tb["convw"][:, cc * 4 + 1:cc * 4 + 2], acc1[:],
                        ALU.mult, ALU.add))
                    chain("vector")
                    emit("vector", lambda v, jj=jj, cc=cc: v.scalar_tensor_tensor(
                        acc1[:], cvin[jj][:, 2:2050],
                        tb["convw"][:, cc * 4 + 2:cc * 4 + 3], acc2[:],
                        ALU.mult, ALU.add))
                    chain("vector")
                    emit("vector", lambda v, jj=jj, cc=cc: v.scalar_tensor_tensor(
                        acc2[:], cvin[jj][:, 3:2051],
                        tb["convw"][:, cc * 4 + 3:cc * 4 + 4], acc1[:],
                        ALU.mult, ALU.add))
                    ev_tap = retag("vector", "EV")
                    cvin_free[jj] = ("EV", ev_tap)
                    W("scalar", "EV", ev_tap)
                    W("scalar", cvout_free)
                    emit("scalar", lambda s, cc=cc: s.activation(
                        cvout[:], acc2[:], AF.Silu,
                        bias=tb["convb"][:, cc:cc + 1], scale=1.0))
                    ac_cv = retag("scalar", "AC")
                    acc_free = ("AC", ac_cv)
                    if cc < 16:
                        mm_t = emit_transposes_and_scatter(
                            cvout, half, xst.tensor, s_ * SEQ * D + cc * 128, D,
                            [("AC", ac_cv)])
                        mm_mat[(m, half)] = mm_t
                        cvout_free = ("MM", mm_t)
                    else:
                        dstt = bfm if cc < 18 else cfm
                        nh = cc - (16 if cc < 18 else 18)
                        W("sync", "AC", ac_cv)
                        emit("sync", lambda sy, dstt=dstt, s_=s_, nh=nh: sy.dma_start(
                            dstt[s_][nh * 128:(nh + 1) * 128, :], cvout[:]
                            ).then_inc(c.s["SS"], 16))
                        ss_fm = c.inc("SS", 16)
                        cvout_free = ("SS", ss_fm)
                        if cc < 18:
                            mm_t = emit_transposes_and_scatter(
                                cvout, half, btm.tensor, s_ * SEQ * SN + nh * 128,
                                SN, [("AC", ac_cv)])
                            mm_mat[(m, half)] = mm_t
                            cvout_free = ("MM", mm_t)
                else:
                    W("vector", "MM", mm_v)
                    W("vector", acc_free)
                    dst_ = acc1 if half == 0 else acc2
                    emit("vector", lambda v, dst_=dst_: v.tensor_copy(
                        dst_[0:SH, :], psA[0:SH, :]))
                    ev_d = retag("vector", "EV")
                    pa_free = ("EV", ev_d)

        flush()
        c.fence()
        st_a1.close()
        if _SSM_NM is not None and _SSM_NM < NM:
            st_a.close()
            c.fence()
            st_ab.close()
            return None
        with (nc.sbuf_tensor(f"sm{u_}_t1", [SH, T], F32) as t1f,
              nc.sbuf_tensor(f"sm{u_}_u1", [SH, T], F32) as u1f,
              nc.sbuf_tensor(f"sm{u_}_u2", [SH, T], F32) as u2f,
              nc.sbuf_tensor(f"sm{u_}_acs", [SH, T], F32) as acsf,
              nc.sbuf_tensor(f"sm{u_}_dtm", [SH, T], F32) as dtmf):
            W("vector", "EV", c.n["EV"])
            emit("vector", lambda v: v.tensor_scalar(
                t1f[:, 0:2048], acc1[0:SH, :], tb["dtb"][:], None, ALU.add))
            chain("vector")
            emit("vector", lambda v: v.tensor_scalar(
                t1f[:, 2048:4096], acc2[0:SH, :], tb["dtb"][:], None, ALU.add))
            ev_t1 = retag("vector", "EV")
            W("scalar", "EV", ev_t1)
            emit("scalar", lambda s: s.activation(u1f[:], t1f[:], AF.Abs,
                                                  bias=0.0, scale=1.0))
            chain("scalar")
            emit("scalar", lambda s: s.activation(u2f[:], u1f[:], AF.Exp,
                                                  bias=0.0, scale=tb["misc"][0:SH, 3:4]))
            chain("scalar")
            emit("scalar", lambda s: s.activation(u1f[:], u2f[:], AF.Ln,
                                                  bias=1.0, scale=1.0))
            ac_sp = retag("scalar", "AC")
            emit("vector", lambda v: v.tensor_scalar(dtmf[:], t1f[:], 0.0, None,
                                                     ALU.max))
            chain("vector")
            W("vector", "AC", ac_sp)
            emit("vector", lambda v: v.tensor_tensor(t1f[:], dtmf[:], u1f[:],
                                                     ALU.add))
            chain("vector")
            emit("vector", lambda v: v.tensor_copy(dtfm[:], t1f[:]))
            retag("vector", "EV")
            emit("vector", lambda v: v.tensor_scalar(dtmf[:], t1f[:],
                                                     tb["anegcol"][:], None, ALU.mult))
            chain("vector")
            for s_ in range(SPC):
                for cc in range(NCHUNK):
                    off = s_ * SEQ + cc * CH
                    emit("vector", lambda v, off=off: v.tensor_tensor_scan(
                        acsf[:, off:off + CH], dtmf[:, off:off + CH],
                        dtmf[:, off:off + CH], 0.0, ALU.add, ALU.bypass))
            chain("vector")
            emit("vector", lambda v: v.tensor_scalar(segL[0:SH, :], acsf[:], -1.0,
                                                     None, ALU.mult))
            retag("vector", "EV")
            emit("vector", lambda v: v.tensor_copy(acsb[:], acsf[:]))
            ev_acsb = retag("vector", "EV")
            W("scalar", "EV", ev_acsb)
            emit("scalar", lambda s: s.activation(expac[:], acsf[:], AF.Exp,
                                                  bias=0.0, scale=1.0))
            retag("scalar", "AC")
            for s_ in range(SPC):
                for cc in range(NCHUNK):
                    off = s_ * SEQ + cc * CH
                    emit("vector", lambda v, off=off: v.tensor_scalar(
                        dtmf[:, off:off + CH], acsf[:, off:off + CH],
                        acsf[:, off + CH - 1:off + CH], None, ALU.subtract))
            chain("vector")
            W("scalar", "CHV", c.n["CHV"])
            emit("scalar", lambda s: s.activation(u1f[:], dtmf[:], AF.Exp,
                                                  bias=0.0, scale=tb["misc"][0:SH, 3:4]))
            ac_dec = retag("scalar", "AC")
            W("vector", "AC", ac_dec)
            emit("vector", lambda v: v.tensor_tensor(dtdec[:], dtfm[:], u1f[:],
                                                     ALU.mult))
            retag("vector", "EV")
            flush()
        st_a.close()
        c.fence()
        if _SSM_STOP == "A":
            st_ab.close()
            return None

        st_b = contextlib.ExitStack()
        xs_sb = [st_b.enter_context(nc.sbuf_tensor(f"sm{u_}_xs{j}", [64, D], BF16)) for j in (0, 1)]
        Bfm = [st_b.enter_context(nc.sbuf_tensor(f"sm{u_}_Bf{j}", [128, 128], BF16)) for j in (0, 1)]
        Cfm = [st_b.enter_context(nc.sbuf_tensor(f"sm{u_}_Cf{j}", [128, 128], BF16)) for j in (0, 1)]
        Btm = [st_b.enter_context(nc.sbuf_tensor(f"sm{u_}_Bt{j}", [64, SN], BF16)) for j in (0, 1)]
        segR = [st_b.enter_context(nc.sbuf_tensor(f"sm{u_}_sgR{j}", [33, SH * CH], BF16)) for j in (0, 1)]
        Sst = [st_b.enter_context(nc.sbuf_tensor(f"sm{u_}_S{j}", [128, 2 * 2048], F32)) for j in (0, 1)]
        Sb = [st_b.enter_context(nc.sbuf_tensor(f"sm{u_}_Sb{j}", [128, 2 * 2048], BF16)) for j in (0, 1)]
        Stmp = st_b.enter_context(nc.sbuf_tensor(f"sm{u_}_Stmp", [128, 2048], F32))
        decbc = [st_b.enter_context(nc.sbuf_tensor(f"sm{u_}_dbc{j}", [128, 2048], BF16)) for j in (0, 1)]
        CBm = st_b.enter_context(nc.sbuf_tensor(f"sm{u_}_CBm", [64, 64], BF16))
        segm = st_b.enter_context(nc.sbuf_tensor(f"sm{u_}_segm", [64, SH * CH], BF16))
        expsg = st_b.enter_context(nc.sbuf_tensor(f"sm{u_}_expsg", [64, SH * CH], BF16))
        e1 = st_b.enter_context(nc.sbuf_tensor(f"sm{u_}_e1", [64, SH * CH], BF16))
        e2 = st_b.enter_context(nc.sbuf_tensor(f"sm{u_}_e2", [64, SH * CH], BF16))
        e3 = st_b.enter_context(nc.sbuf_tensor(f"sm{u_}_e3", [64, SH * CH], BF16))
        xtd = st_b.enter_context(nc.sbuf_tensor(f"sm{u_}_xtd", [64, SH * CH], BF16))
        f3sb = st_b.enter_context(nc.sbuf_tensor(f"sm{u_}_f3", [64, SH * CH], BF16))
        yoffs = st_b.enter_context(nc.sbuf_tensor(f"sm{u_}_yo", [64, SH * CH], BF16))
        ysb = [st_b.enter_context(nc.sbuf_tensor(f"sm{u_}_y{j}", [64, SH * CH], BF16)) for j in (0, 1)]
        dcrow = st_b.enter_context(nc.sbuf_tensor(f"sm{u_}_dcr", [1, 2048], BF16))
        psB = st_b.enter_context(nc.psum_tensor(f"sm{u_}_psB", [128, 4096], F32))

        for jj in (0, 1):
            emit("sync", lambda sy, jj=jj: sy.dma_start(
                segR[jj][0:32, :], tb["eh32_d"][0:32, :]).then_inc(c.s["LA"], 16))
            c.inc("LA", 16)

        in_free = {"xs": [("MM", 0), ("MM", 0)], "Bfm": [("MM", 0), ("MM", 0)],
                   "Cfm": [("MM", 0), ("MM", 0)], "Btm": [("MM", 0), ("MM", 0)],
                   "segR": [("MM", 0), ("MM", 0)]}
        ysb_free = [("SS", 0), ("SS", 0)]
        pa_free = ("EV", 0)
        pb_free = ("EV", 0)
        s_upd = ("EV", 0)

        for s_ in range(SPC):
            W("vector", "PL", c.n["PL"])
            W("vector", "MM", c.n["MM"])
            emit("vector", lambda v: v.memset(Sst[0][:], 0.0))
            emit("vector", lambda v: v.memset(Sb[0][:], 0.0))
            emit("vector", lambda v: v.memset(decbc[0][:], 0.0))
            retag("vector", "EV")
            ev_init = c.n["EV"]
            s_upd = ("EV", ev_init)
            pa_free = ("EV", ev_init)
            W("gpsimd", "EV", ev_init)

            for cc in range(NCHUNK):
                jj = cc % 2
                tok0 = s_ * SEQ + cc * CH
                W("sync", in_free["xs"][jj])
                emit("sync", lambda sy, jj=jj, tok0=tok0: sy.dma_start(
                    xs_sb[jj][:], xst[tok0:tok0 + CH, :]).then_inc(c.s["LA"], 16))
                c.inc("LA", 16)
                W("sync", in_free["Bfm"][jj])
                for nh in range(2):
                    emit("sync", lambda sy, jj=jj, nh=nh, s_=s_, cc=cc: sy.dma_start(
                        Bfm[jj][:, nh * 64:(nh + 1) * 64],
                        bfm[s_][nh * 128:(nh + 1) * 128, cc * CH:(cc + 1) * CH]
                        ).then_inc(c.s["LA"], 16))
                    c.inc("LA", 16)
                W("sync", in_free["Cfm"][jj])
                for nh in range(2):
                    emit("sync", lambda sy, jj=jj, nh=nh, s_=s_, cc=cc: sy.dma_start(
                        Cfm[jj][:, nh * 64:(nh + 1) * 64],
                        cfm[s_][nh * 128:(nh + 1) * 128, cc * CH:(cc + 1) * CH]
                        ).then_inc(c.s["LA"], 16))
                    c.inc("LA", 16)
                W("sync", in_free["Btm"][jj])
                emit("sync", lambda sy, jj=jj, tok0=tok0: sy.dma_start(
                    Btm[jj][:], bass.AP(btm.tensor, tok0 * SN, [[SN, CH], [1, SN]])
                    ).then_inc(c.s["LA"], 16))
                la_in = c.inc("LA", 16)
                W("gpsimd", in_free["segR"][jj])
                emit("gpsimd", lambda g, jj=jj, tok0=tok0: g.dma_start(
                    segR[jj][32:33, :],
                    bass.AP(acsb, tok0, [[T, 32], [1, CH]])).then_inc(c.s["LG"], 16))
                lg_sr = c.inc("LG", 16)

                W("tensor", "LA", la_in)
                W("tensor", "LG", lg_sr)
                W("tensor", pa_free)
                for nh in range(2):
                    emit("tensor", lambda t, jj=jj, nh=nh: t.matmul(
                        psB[0:64, 0:64], Cfm[jj][:, nh * 64:(nh + 1) * 64],
                        Bfm[jj][:, nh * 64:(nh + 1) * 64],
                        start=(nh == 0), stop=(nh == 1)))
                mm_cb = retag("tensor", "MM")
                W("vector", "MM", mm_cb)
                emit("vector", lambda v: v.tensor_tensor(CBm[:], psB[0:64, 0:64],
                                                         tb["lt64"][:], ALU.mult))
                ev_cbm = retag("vector", "EV")
                W("tensor", "EV", ev_cbm)
                for tc in range(4):
                    emit("tensor", lambda t, jj=jj, tok0=tok0, tc=tc: t.matmul(
                        psB[0:64, tc * 512:(tc + 1) * 512],
                        segL[:, tok0:tok0 + CH],
                        segR[jj][:, tc * 512:(tc + 1) * 512],
                        start=True, stop=True))
                mm_seg = retag("tensor", "MM")
                in_free["segR"][jj] = ("MM", mm_seg)
                in_free["Bfm"][jj] = ("MM", mm_seg)
                W("vector", "MM", mm_seg)
                emit("vector", lambda v: v.tensor_scalar(segm[:], psB[0:64, 0:2048],
                                                         0.0, None, ALU.min))
                ev_sgm = retag("vector", "EV")
                W("scalar", "EV", ev_sgm)
                emit("scalar", lambda s: s.activation(expsg[:], segm[:], AF.Exp,
                                                      bias=0.0, scale=1.0))
                ac_es = retag("scalar", "AC")
                W("tensor", "EV", ev_sgm)
                for tc in range(4):
                    emit("tensor", lambda t, tc=tc: t.matmul(
                        psB[0:64, tc * 512:(tc + 1) * 512], CBm[:],
                        tb["eyetb"][:, tc * 512:(tc + 1) * 512],
                        start=True, stop=True))
                mm_cbt = retag("tensor", "MM")
                W("vector", "MM", mm_cbt)
                W("vector", "AC", ac_es)
                emit("vector", lambda v: v.tensor_tensor(e1[:], expsg[:],
                                                         psB[0:64, 0:2048], ALU.mult))
                ev_e1 = retag("vector", "EV")
                W("tensor", "EV", ev_e1)
                for tc in range(4):
                    emit("tensor", lambda t, jj=jj, tok0=tok0, tc=tc: t.matmul(
                        psB[0:64, tc * 512:(tc + 1) * 512],
                        dtfm[:, tok0:tok0 + CH],
                        segR[jj][0:32, tc * 512:(tc + 1) * 512],
                        start=True, stop=True))
                mm_f1 = retag("tensor", "MM")
                W("vector", "MM", mm_f1)
                emit("vector", lambda v: v.tensor_tensor(e2[:], e1[:],
                                                         psB[0:64, 0:2048], ALU.mult))
                ev_e2 = retag("vector", "EV")
                W("vector", "EV", ev_e2)
                emit("vector", lambda v: v.tensor_tensor(e3[:], e2[:],
                                                         tb["dskd"][:], ALU.add))
                ev_e3 = retag("vector", "EV")
                W("tensor", "EV", ev_e2)
                for tc in range(4):
                    emit("tensor", lambda t, jj=jj, tok0=tok0, tc=tc: t.matmul(
                        psB[0:64, tc * 512:(tc + 1) * 512],
                        dtdec[:, tok0:tok0 + CH],
                        segR[jj][0:32, tc * 512:(tc + 1) * 512],
                        start=True, stop=True))
                mm_f2 = retag("tensor", "MM")
                W("vector", "MM", mm_f2)
                W("vector", "LA", la_in)
                emit("vector", lambda v, jj=jj: v.tensor_tensor(
                    xtd[:], xs_sb[jj][:], psB[0:64, 0:2048], ALU.mult))
                ev_xtd = retag("vector", "EV")
                W("tensor", "EV", ev_xtd)
                for tc in range(4):
                    emit("tensor", lambda t, jj=jj, tok0=tok0, tc=tc: t.matmul(
                        psB[0:64, tc * 512:(tc + 1) * 512],
                        expac[:, tok0:tok0 + CH],
                        segR[jj][0:32, tc * 512:(tc + 1) * 512],
                        start=True, stop=True))
                mm_f3 = retag("tensor", "MM")
                W("scalar", "MM", mm_f3)
                emit("scalar", lambda s: s.activation(f3sb[:], psB[0:64, 0:2048],
                                                      AF.Copy, bias=0.0, scale=1.0))
                ac_f3 = retag("scalar", "AC")
                W("tensor", "EV", ev_e3)
                W("tensor", pb_free)
                for h in range(SH):
                    emit("tensor", lambda t, jj=jj, h=h: t.matmul(
                        psB[0:64, 2048 + h * 64:2048 + (h + 1) * 64],
                        e3[:, h * 64:(h + 1) * 64],
                        xs_sb[jj][:, h * 64:(h + 1) * 64],
                        start=True, stop=True))
                mm_yd = retag("tensor", "MM")
                in_free["xs"][jj] = ("MM", mm_yd)
                W("tensor", "AC", ac_f3)
                W("tensor", s_upd)
                for nh in range(2):
                    for tc in range(4):
                        emit("tensor", lambda t, jj=jj, nh=nh, tc=tc, cc=cc: t.matmul(
                            psB[0:64, tc * 512:(tc + 1) * 512],
                            Cfm[jj][:, nh * 64:(nh + 1) * 64],
                            Sb[cc % 2][:, nh * 2048 + tc * 512:
                                       nh * 2048 + (tc + 1) * 512],
                            start=(nh == 0), stop=(nh == 1)))
                mm_yo = retag("tensor", "MM")
                in_free["Cfm"][jj] = ("MM", mm_yo)
                W("vector", "MM", mm_yo)
                emit("vector", lambda v: v.tensor_tensor(yoffs[:], psB[0:64, 0:2048],
                                                         f3sb[:], ALU.mult))
                ev_yo = retag("vector", "EV")
                W("vector", "MM", mm_yd)
                W("vector", "EV", ev_yo)
                W("vector", ysb_free[jj])
                emit("vector", lambda v, jj=jj: v.tensor_tensor(
                    ysb[jj][:], psB[0:64, 2048:4096], yoffs[:], ALU.add))
                ev_y = retag("vector", "EV")
                pb_free = ("EV", ev_y)
                W("sync", "EV", ev_y)
                emit("sync", lambda sy, jj=jj, tok0=tok0: sy.dma_start(
                    yt_d[tok0:tok0 + CH, :], ysb[jj][:]).then_inc(c.s["SS"], 16))
                ysb_free[jj] = ("SS", c.inc("SS", 16))
                last_upd = 0
                for nh in range(2):
                    W("tensor", "EV", ev_yo if nh == 0 else last_upd)
                    for tc in range(4):
                        emit("tensor", lambda t, jj=jj, nh=nh, tc=tc: t.matmul(
                            psB[:, tc * 512:(tc + 1) * 512],
                            Btm[jj][:, nh * 128:(nh + 1) * 128],
                            xtd[:, tc * 512:(tc + 1) * 512],
                            start=True, stop=True))
                    mm_st = retag("tensor", "MM")
                    if nh == 1:
                        in_free["Btm"][jj] = ("MM", mm_st)
                    W("gpsimd", "EV", c.n["EV"])
                    emit("gpsimd", lambda g, cc=cc, nh=nh: g.tensor_tensor(
                        Stmp[:], Sst[cc % 2][:, nh * 2048:(nh + 1) * 2048],
                        decbc[cc % 2][:], ALU.mult))
                    pl_v = retag("gpsimd", "PL")
                    W("vector", "MM", mm_st)
                    W("vector", "PL", pl_v)
                    emit("vector", lambda v, cc=cc, nh=nh: v.tensor_tensor(
                        Sst[(cc + 1) % 2][:, nh * 2048:(nh + 1) * 2048],
                        Stmp[:], psB[:, 0:2048], ALU.add))
                    chain("vector")
                    emit("vector", lambda v, cc=cc, nh=nh: v.tensor_copy(
                        Sb[(cc + 1) % 2][:, nh * 2048:(nh + 1) * 2048],
                        Sst[(cc + 1) % 2][:, nh * 2048:(nh + 1) * 2048]))
                    last_upd = retag("vector", "EV")
                s_upd = ("EV", last_upd)
                pa_free = ("EV", last_upd)
                if cc < NCHUNK - 1:
                    W("tensor", pa_free)
                    col = s_ * SEQ + (cc + 1) * CH + CH - 1
                    for tc in range(4):
                        emit("tensor", lambda t, col=col, jj=jj, tc=tc: t.matmul(
                            psB[0:1, tc * 512:(tc + 1) * 512],
                            expac[:, col:col + 1],
                            segR[jj][0:32, tc * 512:(tc + 1) * 512],
                            start=True, stop=True))
                    mm_dr = retag("tensor", "MM")
                    W("scalar", "MM", mm_dr)
                    emit("scalar", lambda s: s.activation(dcrow[:], psB[0:1, 0:2048],
                                                          AF.Copy, bias=0.0, scale=1.0))
                    ac_dr = retag("scalar", "AC")
                    W("tensor", "AC", ac_dr)
                    for tc in range(4):
                        emit("tensor", lambda t, tc=tc: t.matmul(
                            psB[:, tc * 512:(tc + 1) * 512], onesb[:],
                            dcrow[:, tc * 512:(tc + 1) * 512],
                            start=True, stop=True))
                    mm_db = retag("tensor", "MM")
                    W("vector", "MM", mm_db)
                    W("vector", "PL", c.n["PL"])
                    emit("vector", lambda v, cc=cc: v.tensor_copy(
                        decbc[(cc + 1) % 2][:], psB[:, 0:2048]))
                    ev_db = retag("vector", "EV")
                    pa_free = ("EV", ev_db)
        flush()
        st_b.close()
        st_ab.close()
        c.fence()
        if _SSM_STOP == "B":
            return None

        st_c = contextlib.ExitStack()
        ya = [st_c.enter_context(nc.sbuf_tensor(f"sm{u_}_ya{j}", [128, D], BF16)) for j in (0, 1)]
        za = [st_c.enter_context(nc.sbuf_tensor(f"sm{u_}_za{j}", [128, D], BF16)) for j in (0, 1)]
        zs = st_c.enter_context(nc.sbuf_tensor(f"sm{u_}_zs", [128, D], BF16))
        yg = [st_c.enter_context(nc.sbuf_tensor(f"sm{u_}_yg{j}", [128, D], BF16)) for j in (0, 1)]
        y2s = [st_c.enter_context(nc.sbuf_tensor(f"sm{u_}_y2{j}", [128, D], BF16)) for j in (0, 1)]
        gtrash = st_c.enter_context(nc.sbuf_tensor(f"sm{u_}_gt", [128, D], BF16))
        gcols = [st_c.enter_context(nc.sbuf_tensor(f"sm{u_}_gc{j}", [128, 4], F32)) for j in (0, 1)]
        gnw = st_c.enter_context(nc.sbuf_tensor(f"sm{u_}_gnw", [128, D], F32))

        emit("gpsimd", lambda g: g.dma_start(gnw[:], tb["gnormwb_d"][:]
                                             ).then_inc(c.s["LG"], 16))
        lg_gn = c.inc("LG", 16)
        W("vector", "LG", lg_gn)
        ev0g = c.n["EV"]
        ac0g = c.n["AC"]
        ss0g = c.n["SS"]

        def ev_yg(i):
            return ev0g + 2 * i + 1

        def ev_y2(i):
            return ev0g + 2 * i + 2

        def ac_z(i):
            return ac0g + 2 * i + 1

        def ac_r(i):
            return ac0g + 2 * i + 2

        y2_done = []
        for i in range(NTT):
            jj = i % 2
            if i >= 2:
                W("sync", "EV", ev_y2(i - 2))
            emit("sync", lambda sy, jj=jj, i=i: sy.dma_start(
                ya[jj][:], yt_d[i * 128:(i + 1) * 128, :]).then_inc(c.s["LA"], 16))
            c.inc("LA", 16)
            emit("sync", lambda sy, jj=jj, i=i: sy.dma_start(
                za[jj][:], zt[i * 128:(i + 1) * 128, :]).then_inc(c.s["LA"], 16))
            la_i = c.inc("LA", 16)
            W("scalar", "LA", la_i)
            if i >= 1:
                W("scalar", "EV", ev_yg(i - 1))
            emit("scalar", lambda s, jj=jj: s.activation(zs[:], za[jj][:], AF.Silu,
                                                         bias=0.0, scale=1.0))
            assert retag("scalar", "AC") == ac_z(i)
            W("vector", "AC", ac_z(i))
            emit("vector", lambda v, jj=jj: v.tensor_tensor(yg[jj][:], ya[jj][:],
                                                            zs[:], ALU.mult))
            assert retag("vector", "EV") == ev_yg(i)
            W("scalar", "EV", ev_yg(i))
            emit("scalar", lambda s, jj=jj: s.activation(
                gtrash[:], yg[jj][:], AF.Square, bias=0.0, scale=1.0,
                accum_out=gcols[jj][:, 0:1]))
            chain("scalar")
            emit("scalar", lambda s, jj=jj: s.activation(
                gcols[jj][:, 1:2], gcols[jj][:, 0:1], AF.Ln,
                bias=tb["misc"][:, 0:1], scale=tb["misc"][:, 1:2]))
            chain("scalar")
            emit("scalar", lambda s, jj=jj: s.activation(
                gcols[jj][:, 2:3], gcols[jj][:, 1:2], AF.Exp,
                bias=0.0, scale=tb["misc"][:, 2:3]))
            assert retag("scalar", "AC") == ac_r(i)
            W("vector", "AC", ac_r(i))
            if i >= 2:
                W("vector", "SS", ss0g + 16 * (i - 1))
            emit("vector", lambda v, jj=jj: v.tensor_scalar(
                gtrash[:], yg[jj][:], gcols[jj][:, 2:3], None, ALU.mult))
            chain("vector")
            emit("vector", lambda v, jj=jj: v.tensor_tensor(
                y2s[jj][:], gtrash[:], gnw[:], ALU.mult))
            assert retag("vector", "EV") == ev_y2(i)
            W("sync", "EV", ev_y2(i))
            emit("sync", lambda sy, jj=jj, i=i: sy.dma_start(
                y2_d[i * 128:(i + 1) * 128, :], y2s[jj][:]).then_inc(c.s["SS"], 16))
            y2_done.append(c.inc("SS", 16))
        flush()
        st_c.close()
        c.fence()
        if _SSM_STOP == "C":
            return None

        st_d = contextlib.ExitStack()
        y2fm = st_d.enter_context(nc.sbuf_tensor(f"sm{u_}_y2fm", [128, KC * T], BF16))
        stg2 = [st_d.enter_context(nc.sbuf_tensor(f"sm{u_}_st2{j}", [128, D], BF16)) for j in (0, 1)]
        with (nc.psum_tensor(f"sm{u_}_tr20", [128, 1024], BF16) as tr20,
              nc.psum_tensor(f"sm{u_}_tr21", [128, 1024], BF16) as tr21):
            mm_fm2, ev_fm2 = _emit_fm_rebuild(c, tb, y2_d, y2_done, y2fm, stg2,
                                              [tr20, tr21])
        psD = st_d.enter_context(nc.psum_tensor(f"sm{u_}_psD", [128, 2048], F32))
        wt2 = [st_d.enter_context(nc.sbuf_tensor(f"sm{u_}_w2{j}", [128, KC * 128], BF16)) for j in (0, 1)]
        ostg = [st_d.enter_context(nc.sbuf_tensor(f"sm{u_}_os{j}", [128, T], BF16)) for j in (0, 1)]
        lw0 = c.n["LW"]
        mm0 = c.n["MM"]
        ev0 = c.n["EV"]
        ss0 = c.n["SS"]

        @c.block.gpsimd
        def _(g):
            for m in range(KC):
                if m >= 2:
                    g.wait_ge(c.s["MM"], mm0 + 2 * (m - 1))
                g.dma_start(wt2[m % 2][:], wout[m]).then_inc(c.s["LW"], 16)

        @c.block.tensor
        def _(t):
            t.wait_ge(c.s["EV"], ev_fm2)
            for m in range(KC):
                t.wait_ge(c.s["LW"], lw0 + 16 * (m + 1))
                for half in range(2):
                    idx = 2 * m + half
                    if idx > 0:
                        t.wait_ge(c.s["EV"], ev0 + idx)
                    for k in range(KC):
                        for tc in range(4):
                            col = half * 2048 + tc * 512
                            mm = t.matmul(psD[:, tc * 512:(tc + 1) * 512],
                                          wt2[m % 2][:, k * 128:(k + 1) * 128],
                                          y2fm[:, k * T + col:k * T + col + 512],
                                          start=(k == 0), stop=(k == KC - 1))
                    mm.then_inc(c.s["MM"], 1)

        @c.block.vector
        def _(v):
            for m in range(KC):
                for half in range(2):
                    v.wait_ge(c.s["MM"], mm0 + 2 * m + half + 1)
                    if m >= 2 and half == 0:
                        v.wait_ge(c.s["SS"], ss0 + 16 * 2 * (m - 1))
                    sl = slice(half * 2048, (half + 1) * 2048)
                    v.tensor_copy(ostg[m % 2][:, sl], psD[:]).then_inc(c.s["EV"], 1)

        @c.block.sync
        def _(sy):
            for m in range(KC):
                sy.wait_ge(c.s["EV"], ev0 + 2 * m + 2)
                for part in range(2):
                    dst = bass.AP(o1.tensor,
                                  part * 16 * (128 * KC * 128) + m * 128,
                                  [[KC * 128, 128], [128 * KC * 128, 16], [1, 128]])
                    srcap = bass.AP(ostg[m % 2], part * 2048,
                                    [[T, 128], [128, 16], [1, 128]])
                    sy.dma_start(dst, srcap).then_inc(c.s["SS"], 16)

        c.n["LW"] = lw0 + 16 * KC
        c.n["MM"] = mm0 + 2 * KC
        c.n["EV"] = ev0 + 2 * KC
        c.n["SS"] = ss0 + 16 * 2 * KC
        c.fence()
        st_d.close()

        st_e = contextlib.ExitStack()
        wob = st_e.enter_context(nc.sbuf_tensor(f"sm{u_}_wob", [128, KC * 2048], BF16))
        olhs = [st_e.enter_context(nc.sbuf_tensor(f"sm{u_}_ol{j}", [128, KC * 128], BF16)) for j in (0, 1)]
        dstg = [st_e.enter_context(nc.sbuf_tensor(f"sm{u_}_dg{j}", [128, D], F32)) for j in (0, 1)]
        psE = st_e.enter_context(nc.psum_tensor(f"sm{u_}_psE", [128, 4096], F32))
        lw0 = c.n["LW"]
        lg0 = c.n["LG"]
        mm0 = c.n["MM"]
        ev0 = c.n["EV"]
        ss0 = c.n["SS"]
        delta_done = [ss0 + 16 * (i + 1) for i in range(NTT)]

        @c.block.gpsimd
        def _(g):
            for k in range(KC):
                g.dma_start(wob[:, k * 2048:(k + 1) * 2048],
                            wo[k * 128:(k + 1) * 128, :]).then_inc(c.s["LW"], 16)
            for i in range(NTT):
                if i >= 2:
                    g.wait_ge(c.s["MM"], mm0 + i - 1)
                g.dma_start(olhs[i % 2][:], o1[i].rearrange("p k c -> p (k c)")
                            ).then_inc(c.s["LG"], 16)

        @c.block.tensor
        def _(t):
            t.wait_ge(c.s["LW"], lw0 + 16 * KC)
            for i in range(NTT):
                t.wait_ge(c.s["LG"], lg0 + 16 * (i + 1))
                base = (i % 2) * 2048
                if i >= 2:
                    t.wait_ge(c.s["EV"], ev0 + i - 1)
                for k in range(KC):
                    for tc in range(4):
                        mm = t.matmul(psE[:, base + tc * 512:base + (tc + 1) * 512],
                                      olhs[i % 2][:, k * 128:(k + 1) * 128],
                                      wob[:, k * 2048 + tc * 512:k * 2048 + (tc + 1) * 512],
                                      start=(k == 0), stop=(k == KC - 1))
                mm.then_inc(c.s["MM"], 1)

        @c.block.vector
        def _(v):
            for i in range(NTT):
                v.wait_ge(c.s["MM"], mm0 + i + 1)
                if i >= 2:
                    v.wait_ge(c.s["SS"], ss0 + 16 * (i - 1))
                base = (i % 2) * 2048
                v.tensor_copy(dstg[i % 2][:], psE[:, base:base + 2048]
                              ).then_inc(c.s["EV"], 1)

        @c.block.sync
        def _(sy):
            for i in range(NTT):
                sy.wait_ge(c.s["EV"], ev0 + i + 1)
                sy.dma_start(dout[i * 128:(i + 1) * 128, :], dstg[i % 2][:]
                             ).then_inc(c.s["SS"], 16)

        c.n["LW"] = lw0 + 16 * KC
        c.n["LG"] = lg0 + 16 * NTT
        c.n["MM"] = mm0 + NTT
        c.n["EV"] = ev0 + NTT
        c.n["SS"] = ss0 + 16 * NTT
        c.fence()
        st_e.close()
    return delta_done



def build_nc(stop_after=None):
    nc = bass.Bass("TRN2", target_bir_lowering=False)
    dt_ = nc.dram_tensor

    x_in = dt_("x", [T, D], F32, kind="ExternalInput").ap()
    wqk = dt_("wqk", [32, 128, KC, 128], F32, kind="ExternalInput").ap()
    wv = dt_("wv", [D, D], F32, kind="ExternalInput").ap()
    battn = dt_("battn", [3 * D], F32, kind="ExternalInput").ap()
    wproj = dt_("wproj", [D, D], F32, kind="ExternalInput").ap()
    bprojb_d = dt_("bprojb", [128, D], F32, kind="ExternalInput").ap()
    wg = dt_("wg", [MI, 128, KC, 128], I8, kind="ExternalInput").ap()
    wu = dt_("wu", [MI, 128, KC, 128], I8, kind="ExternalInput").ap()
    wd = dt_("wd", [INTER, D], I8, kind="ExternalInput").ap()
    sg_in = dt_("sg", [INTER], F32, kind="ExternalInput").ap()
    su_in = dt_("su", [INTER], F32, kind="ExternalInput").ap()
    sdownb_d = dt_("sdownb", [128, D], F32, kind="ExternalInput").ap()
    normwb_d = dt_("normwb", [128, D], F32, kind="ExternalInput").ap()
    gnormwb_d = dt_("gnormwb", [128, D], F32, kind="ExternalInput").ap()
    wz = dt_("wz", [16, 128, KC, 128], F32, kind="ExternalInput").ap()
    wbc = dt_("wbc", [CCC, 128, KC, 128], F32, kind="ExternalInput").ap()
    wdt = dt_("wdt", [128, KC, SH], F32, kind="ExternalInput").ap()
    convw_d = dt_("convw", [CONVD, KCONV], F32, kind="ExternalInput").ap()
    convb_d = dt_("convb", [CONVD], F32, kind="ExternalInput").ap()
    dtb_d = dt_("dtb", [SH, 1], F32, kind="ExternalInput").ap()
    alog_d = dt_("alog", [SH, 1], F32, kind="ExternalInput").ap()
    dsk_d = dt_("dsk", [SH, 1], F32, kind="ExternalInput").ap()
    wout = dt_("wout", [16, 128, KC, 128], F32, kind="ExternalInput").ap()
    wo = dt_("wo", [D, D], F32, kind="ExternalInput").ap()
    eye128b_d = dt_("eye128b", [128, 128], BF16, kind="ExternalInput").ap()
    eye128f_d = dt_("eye128f", [128, 128], F32, kind="ExternalInput").ap()
    eyetb_d = dt_("eyetb", [64, SH * CH], BF16, kind="ExternalInput").ap()
    eh32_d = dt_("eh32", [33, SH * CH], BF16, kind="ExternalInput").ap()
    miota_hi_d = dt_("miota_hi", [1, SEQ], BF16, kind="ExternalInput").ap()
    miota_lo_d = dt_("miota_lo", [1, SEQ], BF16, kind="ExternalInput").ap()
    negmiota_hi_d = dt_("negmiota_hi", [1, SEQ], BF16, kind="ExternalInput").ap()
    negmiota_lo_d = dt_("negmiota_lo", [1, SEQ], BF16, kind="ExternalInput").ap()
    negmiotacols_d = dt_("negmiotacols", [128, 16], F32, kind="ExternalInput").ap()
    onesrowb_d = dt_("onesrowb", [1, SEQ], BF16, kind="ExternalInput").ap()
    onestile_d = dt_("onestile", [128, 16], BF16, kind="ExternalInput").ap()
    diagmask_d = dt_("diagmask", [128, 128], F32, kind="ExternalInput").ap()
    tri128_d = dt_("tri128", [128, 128], BF16, kind="ExternalInput").ap()
    lt64_d = dt_("lt64", [64, 64], BF16, kind="ExternalInput").ap()
    ones3264_d = dt_("ones3264", [SH, 64], BF16, kind="ExternalInput").ap()

    y_out = dt_("y", [T, D], F32, kind="ExternalOutput").ap()

    xres = dt_("xres", [T, D], F32).ap()
    xn_d = dt_("xn", [T, D], BF16).ap()
    qf = dt_("qf", [D, T], BF16).ap()
    kf = dt_("kf", [D, T], BF16).ap()
    vt = dt_("vt", [T, D], BF16).ap()
    fu = dt_("fu", [NTT, 128, MI, 128], BF16).ap()
    dout = dt_("dout", [T, D], F32).ap()
    zt = dt_("zt", [T, D], BF16).ap()
    xst = dt_("xst", [T, D], BF16).ap()
    bfm = dt_("bfm", [SPC, SN, SEQ], BF16).ap()
    cfm = dt_("cfm", [SPC, SN, SEQ], BF16).ap()
    btm = dt_("btm", [SPC, SEQ, SN], BF16).ap()
    yt_d = dt_("yt", [T, D], BF16).ap()
    y2_d = dt_("y2", [T, D], BF16).ap()
    o1 = dt_("o1", [NTT, 128, KC, 128], BF16).ap()

    stack = contextlib.ExitStack()
    with stack:
        sems = {name: stack.enter_context(nc.semaphore(name)) for name in SEMS}
        block = stack.enter_context(nc.Block())
        c = C(nc, block, sems)

        tb = {}
        for name, shape, dty in [
            ("bq", [128, 16], F32), ("bk", [128, 16], F32),
            ("sg", [128, MI], F32), ("su", [128, MI], F32),
            ("eye128b", [128, 128], BF16), ("eye128f", [128, 128], F32),
            ("eyetb", [64, SH * CH], BF16),
            ("negmiotacols", [128, 16], F32), ("diagmask", [128, 128], F32),
            ("tri128", [128, 128], BF16), ("lt64", [64, 64], BF16),
            ("convw", [128, CCC * KCONV], F32), ("convb", [128, CCC], F32),
            ("dtb", [SH, 1], F32), ("alogcol", [SH, 1], F32),
            ("anegcol", [SH, 1], F32), ("dskcol", [SH, 1], F32),
            ("dskd", [64, SH * CH], BF16), ("misc", [128, 8], F32),
        ]:
            tb[name] = stack.enter_context(nc.sbuf_tensor("t_" + name, shape, dty))
        tb["battn_d"] = battn
        tb["normwb_d"] = normwb_d
        tb["gnormwb_d"] = gnormwb_d
        tb["bprojb_d"] = bprojb_d
        tb["sdownb_d"] = sdownb_d
        tb["onesrowb_d"] = onesrowb_d
        tb["eh32_d"] = eh32_d
        tb["miota_hi_d"] = miota_hi_d
        tb["miota_lo_d"] = miota_lo_d
        tb["negmiota_hi_d"] = negmiota_hi_d
        tb["negmiota_lo_d"] = negmiota_lo_d
        tb["onestile_d"] = onestile_d

        L = {"battn": battn, "sg_in": sg_in, "su_in": su_in, "convw": convw_d,
             "convb": convb_d, "eye128b": eye128b_d, "eye128f": eye128f_d,
             "eyetb": eyetb_d, "eh32": eh32_d, "negmiotacols": negmiotacols_d,
             "diagmask": diagmask_d, "tri128": tri128_d, "lt64": lt64_d,
             "dtb": dtb_d, "alog": alog_d, "dsk": dsk_d,
             "ones3264_in": ones3264_d}
        ini = _phase_init_tables(c, tb, L)

        for eng in ("tensor", "vector", "scalar", "sync", "gpsimd"):
            @getattr(c.block, eng)
            def _(e, la=ini["la"], lg=ini["lg"], ev=ini["ev"], ac=ini["ac"]):
                e.wait_ge(c.s["LA"], la)
                e.wait_ge(c.s["LG"], lg)
                e.wait_ge(c.s["EV"], ev)
                e.wait_ge(c.s["AC"], ac)

        _, xn_done = _phase_resln(c, tb, src_x=x_in, src_delta=None,
                                  delta_done=None, src_done=None,
                                  dst_x=None, xn_dst=xn_d, extra_rms=False)
        c.fence()
        q_done, qk_done, v_done = _phase_qkv(c, tb, xn_d, xn_done, wqk, wv, qf, kf, vt)
        c.fence()
        pd = _phase_attn(c, tb, qf, kf, vt, v_done, wproj, dout)
        c.fence()
        if stop_after == "attn":
            _phase_resln(c, tb, src_x=x_in, src_delta=dout,
                         delta_done=pd, src_done=None,
                         dst_x=y_out, xn_dst=None, extra_rms=False, final=True,
                         bias_d=bprojb_d)
            _emit_taps(c, {"xn": xn_d, "dout": dout, "qf": qf, "kf": kf, "vt": vt})
            return nc
        _, xn_done = _phase_resln(c, tb, src_x=x_in, src_delta=dout,
                                  delta_done=pd, src_done=None,
                                  dst_x=xres, xn_dst=xn_d, extra_rms=False,
                                  bias_d=bprojb_d)
        c.fence()
        fd = _phase_ffn(c, tb, xn_d, xn_done, wg, wu, wd, fu, dout)
        if stop_after == "ffn1":
            _phase_resln(c, tb, src_x=xres, src_delta=dout, delta_done=fd,
                         src_done=None, dst_x=y_out, xn_dst=None,
                         extra_rms=False, final=True)
            _emit_taps(c, {"xn": xn_d, "dout": dout, "xres": xres})
            return nc
        xd2, xn_done = _phase_resln(c, tb, src_x=xres, src_delta=dout,
                                    delta_done=fd, src_done=None,
                                    dst_x=xres, xn_dst=xn_d, extra_rms=True)
        c.fence()
        Lssm = {"wz": wz, "wbc": wbc, "wdt": wdt, "zt": zt, "xst": xst,
                "bfm": bfm, "cfm": cfm, "btm": btm, "yt_d": yt_d, "y2_d": y2_d,
                "o1": o1, "dout": dout, "wout": wout, "wo": wo}
        sd_ = _phase_ssm(c, tb, xn_d, xn_done, Lssm)
        if sd_ is None:
            _phase_resln(c, tb, src_x=xres, src_delta=None, delta_done=None,
                         src_done=xd2, dst_x=y_out, xn_dst=None,
                         extra_rms=False, final=True)
            _emit_taps(c, {"xn": xn_d, "zt": zt, "xst": xst, "yt": yt_d,
                           "y2": y2_d, "bfm": bfm, "cfm": cfm, "btm": btm})
            return nc
        if stop_after == "ssm":
            _phase_resln(c, tb, src_x=xres, src_delta=dout, delta_done=sd_,
                         src_done=xd2, dst_x=y_out, xn_dst=None,
                         extra_rms=False, final=True)
            _emit_taps(c, {"xn": xn_d, "dout": dout, "zt": zt, "xst": xst,
                           "yt": yt_d, "y2": y2_d, "bfm": bfm, "cfm": cfm,
                           "btm": btm})
            return nc
        xd3, xn_done = _phase_resln(c, tb, src_x=xres, src_delta=dout,
                                    delta_done=sd_, src_done=xd2,
                                    dst_x=xres, xn_dst=xn_d, extra_rms=False)
        c.fence()
        fd = _phase_ffn(c, tb, xn_d, xn_done, wg, wu, wd, fu, dout)
        _phase_resln(c, tb, src_x=xres, src_delta=dout, delta_done=fd,
                     src_done=xd3, dst_x=y_out, xn_dst=None,
                     extra_rms=False, final=True)
        return nc
    return nc


_TAPS = []
_DBG_SKIP_DIAG = False
_SSM_STOP = None
_SSM_NM = None
_RMS_TEST = False


def _emit_taps(c, avail):
    if not _TAPS:
        return
    nc = c.nc
    out_final = c.n["OUT"]
    dbgs = []
    for nm in _TAPS:
        if nm not in avail:
            continue
        srcap = avail[nm]
        th = srcap.tensor
        dbg = nc.dram_tensor("dbg_" + nm, list(th.shape), th.dtype,
                             kind="ExternalOutput").ap()
        dbgs.append((dbg, srcap))

    @c.block.gpsimd
    def _(g):
        g.wait_ge(c.s["OUT"], out_final)
        g.wait_ge(c.s["SS"], c.n["SS"])
        for dbg, srcap in dbgs:
            g.dma_start(dbg, srcap).then_inc(c.s["SG"], 16)
            c.inc("SG", 16)
        g.wait_ge(c.s["SG"], c.n["SG"])


_NC_CACHE = {}
_LAST_RES = [None]


def kernel(_stop_after=None, **inputs):
    shared = _prep_shared(inputs)
    x = np.asarray(inputs["L0_attn_a_hidden_states"], dtype=np.float32)
    key = _stop_after
    if key not in _NC_CACHE:
        _NC_CACHE[key] = build_nc(stop_after=_stop_after)
    nc = _NC_CACHE[key]
    in_maps = []
    for cid in range(NCORES):
        m = dict(shared)
        m["x"] = np.ascontiguousarray(
            x[SPC * cid:SPC * (cid + 1)].reshape(T, D))
        in_maps.append(m)
    res = run_bass_kernel_spmd(nc, in_maps, core_ids=list(range(NCORES)))
    _LAST_RES[0] = res
    out = np.stack([np.asarray(res.results[cid]["y"]).reshape(SPC, SEQ, D)
                    for cid in range(NCORES)])
    return out.reshape(BATCH, SEQ, D)
